# version 19
# baseline (speedup 1.0000x reference)
import numpy as np
import ml_dtypes
from contextlib import ExitStack

import concourse.bass as bass
import concourse.mybir as mybir
from concourse.bass_core_rust import IndirectOffsetOnAxis
from concourse.bass_utils import run_bass_kernel_spmd

BF = ml_dtypes.bfloat16
F32 = mybir.dt.float32
BF16 = mybir.dt.bfloat16
I32 = mybir.dt.int32
U32 = mybir.dt.uint32
AF = mybir.ActivationFunctionType
ALU = mybir.AluOpType
AX = mybir.AxisListType

D = 2048
NH = 16
HD = 128
NSLOT = 8
NT = 16
NE = 16
IFF = 1408
SI = 2816
CAP = 256
NIC = IFF // 128
NSC = SI // 128
SCALE = float(HD) ** -0.5
EPS = 1e-5

ENGS = ("tensor", "vector", "scalar", "gpsimd", "sync")
SELF = {"tensor": "PE", "vector": "DVE", "scalar": "ACT", "gpsimd": "GP", "sync": "SQ"}


class Sched:

    def __init__(self, nc, es):
        self.nc = nc
        self.es = es
        self.q = {e: [] for e in ENGS}
        self.sems = {}
        self.C = {}
        self._pe_pending = None

    def _pe_flush(self):
        if self._pe_pending is not None:
            fnp = self._pe_pending
            self._pe_pending = None
            self.q["tensor"].append(lambda E, fnp=fnp: fnp(E))

    def sem(self, name):
        if name not in self.sems:
            self.sems[name] = self.es.enter_context(self.nc.semaphore(name))
            self.C[name] = 0
        return self.sems[name]

    def wait(self, eng, name, val=None):
        s = self.sem(name)
        v = self.C[name] if val is None else val
        if v > 0:
            self.q[eng].append(lambda E, s=s, v=v: E.wait_ge(s, v))

    def wt(self, eng, tok):
        if tok is not None:
            self.wait(eng, tok[0], tok[1])

    def op(self, eng, fn):
        if eng == "tensor":
            self._pe_flush()
        sn = SELF[eng]
        s = self.sem(sn)
        if eng != "tensor" and self.C[sn] > 0:
            self.q[eng].append(lambda E, s=s, v=self.C[sn]: E.wait_ge(s, v))
        self.C[sn] += 1
        v = self.C[sn]
        self.q[eng].append(lambda E, fn=fn, s=s: fn(E).then_inc(s, 1))
        return (sn, v)

    def pe(self, fn):
        self._pe_flush()
        self._pe_pending = fn

    def pe_mark(self):
        if self._pe_pending is None:
            return self.op("tensor", lambda E: E.nop())
        fnp = self._pe_pending
        self._pe_pending = None
        return self.op("tensor", fnp)

    def dma(self, eng, name, fn):
        s = self.sem(name)
        self.C[name] += 16
        v = self.C[name]
        self.q[eng].append(lambda E, fn=fn, s=s: fn(E).then_inc(s, 16))
        return (name, v)

    def barrier(self):
        snap = {n: v for n, v in self.C.items() if v > 0}
        for eng in ENGS:
            sn = SELF[eng]
            for name, v in snap.items():
                if name != sn:
                    self.wait(eng, name, v)

    def build(self, block):
        self._pe_flush()
        for eng in ENGS:
            lst = self.q[eng]
            if lst:
                getattr(block, eng)(lambda E, lst=lst: [f(E) for f in lst])


def _tcopy(E, out, in_):
    fn = getattr(E, "tensor_copy", None)
    if fn is None:
        fn = E.copy
    return fn(out, in_)


def ceil_div(a, b):
    return (a + b - 1) // b


def build_program(dbg=False):
    nc = bass.Bass("TRN2", target_bir_lowering=False)

    def din(name, shape, dt):
        return nc.dram_tensor(name, shape, dt, kind="ExternalInput").ap()

    xT = din("xT", [D, 2048], F32)
    xT_own = din("xT_own", [D, 1024], F32)
    x_own = din("x_own", [1024, D], F32)
    wqh = din("wqh", [NH, 128, NT, HD], F32)
    wkh = din("wkh", [NH, 128, NT, HD], F32)
    wvh = din("wvh", [NH, 128, NT, HD], F32)
    wo_in = din("wo", [D, D], F32)
    wg_in = din("wg", [D, 48], F32)
    gateT = din("gateT", [D, NE], F32)
    sw1c = din("sw1c", [NSC, 128, D], BF16)
    sw3c = din("sw3c", [NSC, 128, D], BF16)
    sw2r = din("sw2r", [NSC, 128, D], BF16)
    w1c = din("w1c", [NE, NT, 128, IFF], BF16)
    w3c = din("w3c", [NE, NT, 128, IFF], BF16)
    w2r = din("w2r", [NE, NIC, 128, D], BF16)
    c_ident = din("c_ident", [128, 128], BF16)
    c_cc = din("c_cc", [128, 2048], BF16)
    c_ss = din("c_ss", [128, 2048], BF16)
    c_ccq = din("c_ccq", [128, 1024], BF16)
    c_ssq = din("c_ssq", [128, 1024], BF16)
    c_pswap = din("c_pswap", [128, 128], BF16)
    c_causal = din("c_causal", [128, 256], BF16)
    c_band = din("c_band", [2, 128, 320], BF16)
    c_cmpm = din("c_cmpm", [NSLOT, 128, 32], F32)
    c_curadd = din("c_curadd", [NSLOT, 128, 32], F32)
    c_ones = din("c_ones", [128, 128], BF16)
    c_triu = din("c_triu", [128, 128], BF16)
    c_iota16 = din("c_iota16", [128, NE], F32)
    c_iotatok = din("c_iotatok", [128, NSLOT], F32)

    out_d = nc.dram_tensor("out", [1024, D], F32, kind="ExternalOutput").ap()

    x1_dram = nc.dram_tensor("x1_dram", [1024, D], BF16, kind="Internal").ap()
    y0_dram = nc.dram_tensor("y0_dram", [1024, D], F32, kind="Internal").ap()
    ys_dram = nc.dram_tensor("ys_dram", [NE * CAP, D], F32, kind="Internal").ap()
    idx_dram = nc.dram_tensor("idx_dram", [128 * 32, 1], I32, kind="Internal").ap()

    dbg_outs = {}
    if dbg:
        for name, shape, dt in [
            ("dbg_krot", [128, 2048], BF16), ("dbg_e", [128, 2048], BF16),
            ("dbg_pcomb", [128, 2048], BF16), ("dbg_oT", [128, 1024], BF16),
            ("dbg_x1", [128, D], BF16), ("dbg_scores", [128, NSLOT * NE], F32),
            ("dbg_rank", [128, NSLOT * NE], F32), ("dbg_s12", [128, 2 * NSLOT], F32),
            ("dbg_g", [128, 48], F32), ("dbg_idx", [128, 32], I32),
            ("dbg_y0", [128, D], F32), ("dbg_ysort", [128, D], F32),
        ]:
            dbg_outs[name] = nc.dram_tensor(name, shape, dt, kind="ExternalOutput").ap()

    es_root = ExitStack()
    with es_root:
        S = Sched(nc, es_root)
        sb = lambda name, shape, dt: es_root.enter_context(nc.sbuf_tensor(name, shape, dt))

        ident = sb("ident", [128, 128], BF16)
        pswap = sb("pswap", [128, 128], BF16)
        onesb = sb("onesb", [128, 128], BF16)
        triu = sb("triu", [128, 128], BF16)
        iota16 = sb("iota16", [128, NE], F32)
        iotatok = sb("iotatok", [128, NSLOT], F32)
        sm = sb("sm", [128, 64], F32)
        gA = sb("gA", [128, NSLOT * 8], F32)
        s1i = sb("s1i", [128, NSLOT], I32)
        s2i = sb("s2i", [128, NSLOT], I32)
        toki = sb("toki", [128, NSLOT], I32)
        iwrk = sb("iwrk", [128, 8], I32)
        posb = sb("posb", [128, 16], I32)
        idxsb = sb("idxsb", [128, 32], I32)

        psA = es_root.enter_context(nc.psum_tensor("psA", [128, 2048], F32))
        psS = es_root.enter_context(nc.psum_tensor("psS", [128, 1024], F32))
        psT = es_root.enter_context(nc.psum_tensor("psT", [128, 1024], BF16))
        psU = es_root.enter_context(nc.psum_tensor("psU", [128, 1024], BF16))

        trs = {"bank": 0, "free": [None, None], "last": [None, None]}

        def transpose_run(n, src_chunk_fn, dst_view_fn):
            j0 = 0
            while j0 < n:
                cnt = min(8, n - j0)
                bank = trs["bank"]
                trs["bank"] ^= 1
                ps_t = psT if bank == 0 else psU
                eng = "scalar"
                S.wt("tensor", trs["free"][bank])
                for g in range(cnt):
                    S.pe(lambda E, ps_t=ps_t, g=g, j=j0 + g: E.transpose(
                        ps_t[:, g * 128:(g + 1) * 128], src_chunk_fn(j), ident[:]))
                tkm = S.pe_mark()
                S.wt(eng, tkm)
                tev = S.op(eng, lambda E, ps_t=ps_t, cnt=cnt, j0=j0: _tcopy(
                    E, dst_view_fn(j0, cnt),
                    ps_t[:, 0:cnt * 128].rearrange("p (j n) -> p j n", n=128)))
                trs["free"][bank] = tev
                trs["last"][0] = tev
                trs["last"][1] = tev
                j0 += cnt
            return trs["last"]

        def transpose_to(src_ap_fn, dst_ap_fn, rows=128):
            bank = trs["bank"]
            trs["bank"] ^= 1
            ps_t = psT if bank == 0 else psU
            eng = "scalar"
            S.wt("tensor", trs["free"][bank])
            S.pe(lambda E, ps_t=ps_t, rows=rows: E.transpose(
                ps_t[0:rows, 0:128], src_ap_fn(), ident[:]))
            tkm = S.pe_mark()
            S.wt(eng, tkm)
            tev = S.op(eng, lambda E, ps_t=ps_t, rows=rows, dst_ap_fn=dst_ap_fn: _tcopy(
                E, dst_ap_fn(), ps_t[0:rows, 0:128]))
            trs["free"][bank] = tev
            trs["last"][0] = tev
            trs["last"][1] = tev
            return tev

        def tr_both():
            return (trs["last"][0], trs["last"][1])

        for nm, dst, src in [("ident", ident, c_ident), ("pswap", pswap, c_pswap),
                             ("ones", onesb, c_ones), ("triu", triu, c_triu),
                             ("iota16", iota16, c_iota16), ("iotatok", iotatok, c_iotatok)]:
            S.dma("sync", "cst", lambda E, dst=dst, src=src: E.dma_start(out=dst[:], in_=src))

        esA = ExitStack()
        sbA = lambda name, shape, dt: esA.enter_context(nc.sbuf_tensor(name, shape, dt))
        cc = sbA("cc", [128, 2048], BF16)
        ss = sbA("ss", [128, 2048], BF16)
        ccq = sbA("ccq", [128, 1024], BF16)
        ssq = sbA("ssq", [128, 1024], BF16)
        causal = sbA("causal", [128, 256], BF16)
        bandm = sbA("bandm", [128, 2 * 320], BF16)
        cmpm = sbA("cmpm", [128, NSLOT * 32], F32)
        curadd = sbA("curadd", [128, NSLOT * 32], F32)
        g_all = sbA("g_all", [128, NSLOT * 48], F32)
        xfm = sbA("xfm", [128, NT * 2048], BF16)
        xfo = sbA("xfo", [128, NT * 1024], BF16)
        wkb = sbA("wkb", [128, NT * 128], BF16)
        wqb = sbA("wqb", [128, NT * 128], BF16)
        wvb = sbA("wvb", [128, NT * 128], BF16)
        wgb = sbA("wgb", [128, NT * 48], BF16)
        kpre = sbA("kpre", [128, 2048], BF16)
        krot = sbA("krot", [128, 2048], BF16)
        vfm_sb = sbA("vfm_sb", [128, 2048], BF16)
        vtok = sbA("vtok", [128, 2048], BF16)
        qpre = sbA("qpre", [128, 1024], BF16)
        qrot = sbA("qrot", [128, 1024], BF16)
        kcmp = sbA("kcmp", [128, 32], BF16)
        cmpf = sbA("cmpf", [128, 32], F32)
        e_sb = sbA("e_sb", [128, 3840], BF16)
        ecmp = sbA("ecmp", [128, 32], F32)
        tpcmp = sbA("tpcmp", [128, 2 * 32], BF16)
        pcmpT = sbA("pcmpT", [128, 2 * 128], BF16)
        vcmpb = sbA("vcmpb", [128, 32], BF16)
        vcmpt = sbA("vcmpt", [128, 128], BF16)
        m8a = sbA("m8a", [128, 8], F32)
        selb = sbA("selb", [128, 32], F32)
        eswa = sbA("eswa", [128, 320], F32)
        pcomb = sbA("pcomb", [128, 4096], BF16)
        rtmp = sbA("rtmp", [128, 512], BF16)
        pT = sbA("pT", [128, NT * 256], BF16)
        esO = ExitStack()
        oT_all = esO.enter_context(nc.sbuf_tensor("oT_all", [128, NH * 1024], BF16, side="right"))

        for nm, dst, src in [("cc", cc, c_cc), ("ss", ss, c_ss), ("ccq", ccq, c_ccq),
                             ("ssq", ssq, c_ssq), ("causal", causal, c_causal)]:
            S.dma("sync", "cst", lambda E, dst=dst, src=src: E.dma_start(out=dst[:], in_=src))
        for v in range(2):
            S.dma("sync", "cst", lambda E, v=v: E.dma_start(
                out=bandm[:, v * 320:(v + 1) * 320], in_=c_band[v]))
        for kk in range(NSLOT):
            S.dma("sync", "cst", lambda E, kk=kk: E.dma_start(
                out=cmpm[:, kk * 32:(kk + 1) * 32], in_=c_cmpm[kk]))
            S.dma("sync", "cst", lambda E, kk=kk: E.dma_start(
                out=curadd[:, kk * 32:(kk + 1) * 32], in_=c_curadd[kk]))

        gx_tok = {}
        for c in range(NT):
            gx_tok[c] = S.dma("gpsimd", f"gx{c // 4}", lambda E, c=c: E.dma_start(
                out=xfm[:, c * 2048:(c + 1) * 2048], in_=xT[c * 128:(c + 1) * 128, :]))
        for c in range(NT):
            S.dma("gpsimd", "xfo", lambda E, c=c: E.dma_start(
                out=xfo[:, c * 1024:(c + 1) * 1024], in_=xT_own[c * 128:(c + 1) * 128, :]))
        for c in range(NT):
            S.dma("gpsimd", "gw", lambda E, c=c: E.dma_start(
                out=wgb[:, c * 48:(c + 1) * 48], in_=wg_in[c * 128:(c + 1) * 128, :]))

        for eng in ("tensor", "vector", "scalar", "gpsimd"):
            S.wait(eng, "cst")
        t_e_init = S.op("vector", lambda E: E.memset(e_sb[:], 0.0))
        t_p_init = S.op("vector", lambda E: E.memset(pcomb[:], 0.0))

        def load_w(nm, h, gate_tok):
            src, buf = {"wk": (wkh, wkb), "wq": (wqh, wqb), "wv": (wvh, wvb)}[nm]
            if gate_tok is not None:
                S.wt("gpsimd", gate_tok)
            return S.dma("gpsimd", nm, lambda E, src=src, buf=buf, h=h: E.dma_start(
                out=buf[:], in_=src[h].rearrange("p c m -> p (c m)")))

        hw_tok = {0: {nm: load_w(nm, 0, None) for nm in ("wk", "wq", "wv")}}

        for i in range(4):
            S.wait("tensor", f"gx{i}")
        S.wait("tensor", "xfo")
        S.wait("tensor", "gw")
        t_g_evac = None
        for kk in range(NSLOT):
            if t_g_evac is not None:
                S.wt("tensor", t_g_evac)
            for c in range(NT):
                S.pe(lambda E, kk=kk, c=c: E.matmul(
                    psS[:, 0:48],
                    xfo[:, c * 1024 + kk * 128: c * 1024 + (kk + 1) * 128],
                    wgb[:, c * 48:(c + 1) * 48],
                    start=(c == 0), stop=(c == NT - 1)))
            tpe = S.pe_mark()
            S.wt("scalar", tpe)
            t_g_evac = S.op("scalar", lambda E, kk=kk: E.activation(
                g_all[:, kk * 48:(kk + 1) * 48], psS[:, 0:48], AF.Sigmoid))
        if dbg:
            S.wt("sync", t_g_evac)
            S.dma("sync", "dbg", lambda E: E.dma_start(out=dbg_outs["dbg_g"], in_=g_all[:, 0:48]))

        t_oT_evac = [None] * NSLOT
        t_trA = [None] * 8
        t_trB = [None] * 8
        t_projA = t_g_evac
        t_sev = [None, None, None, None]
        t_scmp_ev = None
        t_kpre_used = None
        t_krot_used = None
        t_vfm_used2 = None
        t_qrot_used = None
        t_vtok_used = None
        t_e_used = [None, None]
        t_pcomb_used = [None, None]
        t_ecmp_used = [None, None]
        t_eswa_used = None
        t_bs_used = None
        t_sel_used = None
        t_imp_used = None
        t_w_cons = {}

        PS = {}

        def emit_kproj(h):
            st = PS.setdefault(h, {})
            S.wait("tensor", "wk", hw_tok[h]["wk"][1])
            S.wt("tensor", t_qrot_done_g[0])
            S.wt("tensor", t_kpre_used_g[0])
            S.wt("tensor", t_sev[2])
            S.wt("tensor", t_sev[3])
            kev = None
            for pair in range(2):
                for c in range(NT):
                    if h == 0 and c % 4 == 0:
                        S.wait("tensor", f"gx{c // 4}", gx_tok[c + 3][1])
                    for i in range(2):
                        tc = pair * 2 + i
                        S.pe(lambda E, c=c, tc=tc, i=i: E.matmul(
                            psA[:, i * 512:(i + 1) * 512],
                            wkb[:, c * 128:(c + 1) * 128],
                            xfm[:, c * 2048 + tc * 512: c * 2048 + (tc + 1) * 512],
                            start=(c == 0), stop=(c == NT - 1)))
                tpe = S.pe_mark()
                S.wt("scalar", tpe)
                for i in range(2):
                    tc = pair * 2 + i
                    kev = S.op("scalar", lambda E, tc=tc, i=i: E.copy(
                        kpre[:, tc * 512:(tc + 1) * 512], psA[:, i * 512:(i + 1) * 512]))
                S.wt("tensor", kev)
            if h + 1 < NH:
                hw_tok.setdefault(h + 1, {})["wk"] = load_w("wk", h + 1, tpe)
            st["kpre_ev"] = kev

        def emit_vproj(h):
            st = PS.setdefault(h, {})
            S.wait("tensor", "wv", hw_tok[h]["wv"][1])
            vev = None
            for pair in range(2):
                for c in range(NT):
                    for i in range(2):
                        tc = pair * 2 + i
                        S.pe(lambda E, c=c, tc=tc, i=i: E.matmul(
                            psA[:, i * 512:(i + 1) * 512],
                            wvb[:, c * 128:(c + 1) * 128],
                            xfm[:, c * 2048 + tc * 512: c * 2048 + (tc + 1) * 512],
                            start=(c == 0), stop=(c == NT - 1)))
                tpe = S.pe_mark()
                S.wt("scalar", tpe)
                for i in range(2):
                    tc = pair * 2 + i
                    vev = S.op("scalar", lambda E, tc=tc, i=i: E.copy(
                        vfm_sb[:, tc * 512:(tc + 1) * 512], psA[:, i * 512:(i + 1) * 512]))
                S.wt("tensor", vev)
            if h + 1 < NH:
                hw_tok[h + 1]["wv"] = load_w("wv", h + 1, tpe)
            st["vfm_ev"] = vev

        def emit_qproj(h):
            st = PS.setdefault(h, {})
            S.wait("tensor", "wq", hw_tok[h]["wq"][1])
            qev = None
            for c in range(NT):
                for i in range(2):
                    S.pe(lambda E, c=c, i=i: E.matmul(
                        psA[:, i * 512:(i + 1) * 512],
                        wqb[:, c * 128:(c + 1) * 128],
                        xfo[:, c * 1024 + i * 512: c * 1024 + (i + 1) * 512],
                        start=(c == 0), stop=(c == NT - 1)))
            tpe = S.pe_mark()
            if h + 1 < NH:
                hw_tok[h + 1]["wq"] = load_w("wq", h + 1, tpe)
            S.wt("scalar", tpe)
            for i in range(2):
                qev = S.op("scalar", lambda E, i=i: E.copy(
                    qpre[:, i * 512:(i + 1) * 512], psA[:, i * 512:(i + 1) * 512]))
            S.wt("tensor", qev)
            st["qev"] = qev

        t_qrot_done_g = [None]
        t_kpre_used_g = [None]

        for h in range(NH):
            if h == 0:
                emit_kproj(0)
                emit_vproj(0)
                emit_qproj(0)
            t_kpre_ev = PS[h]["kpre_ev"]
            t_vfm_ev = PS[h]["vfm_ev"]
            qev = PS[h]["qev"]

            S.wt("tensor", t_vfm_ev)
            S.wt("tensor", t_vtok_used)
            transpose_run(
                NT,
                lambda j: vfm_sb[:, j * 128:(j + 1) * 128],
                lambda j0, cnt: vtok[:].rearrange("p (j n) -> p j n", n=128)[:, j0:j0 + cnt, :])
            t_v_last2 = {"vector": trs["last"][0], "scalar": trs["last"][1]}
            S.wt("vector", t_vfm_ev)
            S.op("vector", lambda E: E.reduce_sum(
                cmpf[:], vfm_sb[:].rearrange("p (nb k) -> p nb k", nb=32), axis=AX.X))
            t_vcb = S.op("vector", lambda E: _tcopy(E, vcmpb[:], cmpf[:]))
            S.wt("tensor", t_vcb)
            t_vct = transpose_to(lambda: vcmpb[:], lambda: vcmpt[0:32, :], rows=32)

            for kq in range(NSLOT):
                S.wt("tensor", t_oT_evac[kq])
            S.wt("vector", t_kpre_ev)
            S.wt("vector", t_krot_used)
            S.op("vector", lambda E: E.tensor_mul(krot[:], kpre[:], cc[:]))
            t_rope_last = None
            t_r1k = [None] * 4
            swp_toks = []
            for tc in range(4):
                S.pe(lambda E, tc=tc: E.matmul(
                    psA[:, tc * 512:(tc + 1) * 512], pswap[:], kpre[:, tc * 512:(tc + 1) * 512],
                    start=True, stop=True))
                swp_toks.append(S.pe_mark())
            for tc in range(4):
                S.wt("vector", swp_toks[tc])
                t_r1k[tc] = S.op("vector", lambda E, tc=tc: E.tensor_mul(
                    rtmp[:], psA[:, tc * 512:(tc + 1) * 512], ss[:, tc * 512:(tc + 1) * 512]))
                t_rope_last = S.op("vector", lambda E, tc=tc: E.tensor_add(
                    krot[:, tc * 512:(tc + 1) * 512], krot[:, tc * 512:(tc + 1) * 512], rtmp[:]))
            t_krot_done = t_rope_last
            S.wt("vector", qev)
            S.op("vector", lambda E: E.tensor_mul(qrot[:], qpre[:], ccq[:]))
            swq_toks = []
            for i in range(2):
                S.wt("tensor", t_r1k[i])
                S.pe(lambda E, i=i: E.matmul(
                    psA[:, i * 512:(i + 1) * 512], pswap[:], qpre[:, i * 512:(i + 1) * 512],
                    start=True, stop=True))
                swq_toks.append(S.pe_mark())
            for i in range(2):
                S.wt("vector", swq_toks[i])
                t_r1 = S.op("vector", lambda E, i=i: E.tensor_mul(
                    rtmp[:], psA[:, i * 512:(i + 1) * 512], ssq[:, i * 512:(i + 1) * 512]))
                t_rope_last = S.op("vector", lambda E, i=i: E.tensor_add(
                    qrot[:, i * 512:(i + 1) * 512], qrot[:, i * 512:(i + 1) * 512], rtmp[:]))
                S.wt("tensor", t_r1)
            t_qrot_done = t_rope_last
            t_qrot_done_g[0] = t_rope_last
            t_projA = t_rope_last
            t_kpre_used = t_krot_done
            t_kpre_used_g[0] = t_krot_done

            S.wt("vector", t_krot_done)
            S.op("vector", lambda E: E.reduce_sum(
                cmpf[:], krot[:].rearrange("p (nb k) -> p nb k", nb=32), axis=AX.X))
            t_kcmp = S.op("vector", lambda E: _tcopy(E, kcmp[:], cmpf[:]))

            if dbg and h == 0:
                S.wt("sync", t_krot_done)
                S.dma("sync", "dbg", lambda E: E.dma_start(out=dbg_outs["dbg_krot"], in_=krot[:]))

            pending_tp = None
            pending_pv = None
            t_pv = None
            t_pv_pair_last = None
            for kk in range(NSLOT):
                par = kk % 2
                L = 256 * (kk + 1)
                nb = L // 64
                nchunks = ceil_div(L, 512)
                eS = e_sb[:, par * 1792: par * 1792 + L]
                pC = pcomb[:, par * 2048: par * 2048 + L + (256 if par == 0 else 0)]

                S.wt("tensor", t_kcmp)
                S.wt("tensor", t_qrot_done)
                S.wt("tensor", t_scmp_ev)
                for kq in range(4, NSLOT):
                    S.wt("tensor", t_oT_evac[kq])
                S.pe(lambda E, kk=kk: E.matmul(
                    psA[:, 2016:2048], qrot[:, kk * 128:(kk + 1) * 128], kcmp[:],
                    start=True, stop=True))
                t_scmp = S.pe_mark()

                t_ex_last = None
                for sc in range(nchunks):
                    N = min(512, L - sc * 512)
                    bank = (sc % 4) if kk < 4 else (sc % 2)
                    ps_b = psS if bank < 2 else psA
                    boff = (bank % 2) * 512
                    S.wt("tensor", t_sev[bank])
                    S.pe(lambda E, kk=kk, sc=sc, N=N, ps_b=ps_b, boff=boff: E.matmul(
                        ps_b[:, boff: boff + N],
                        qrot[:, kk * 128:(kk + 1) * 128],
                        krot[:, sc * 512: sc * 512 + N],
                        start=True, stop=True))
                    tpe = S.pe_mark()
                    S.wt("scalar", tpe)
                    S.wt("scalar", t_e_used[par])
                    tex = S.op("scalar", lambda E, eS=eS, sc=sc, N=N, ps_b=ps_b, boff=boff: E.activation(
                        eS[:, sc * 512: sc * 512 + N], ps_b[:, boff: boff + N],
                        AF.Exp, scale=SCALE))
                    t_sev[bank] = tex
                    t_ex_last = tex

                if pending_tp is not None:
                    trtoks = pending_tp()
                    pending_tp = None
                    if pending_pv is not None:
                        t_pv = pending_pv(trtoks)
                        t_pv_pair_last = t_pv
                        pending_pv = None
                if h + 1 < NH:
                    if kk == 4:
                        emit_kproj(h + 1)
                    elif kk == 5:
                        emit_vproj(h + 1)
                    elif kk == 6:
                        emit_qproj(h + 1)

                S.wt("scalar", t_scmp)
                S.wt("scalar", t_ecmp_used[par])
                t_ec0 = S.op("scalar", lambda E: E.activation(
                    sm[:, 32:64], psA[:, 2016:2048], AF.Exp, scale=SCALE / 64.0))
                t_scmp_ev = t_ec0
                for bank in range(min(nchunks, 4)):
                    S.wt("vector", t_sev[bank])
                S.wt("vector", t_e_used[par])
                t_diag = S.op("vector", lambda E, eS=eS, L=L: E.tensor_mul(
                    eS[:, L - 256:L], eS[:, L - 256:L], causal[:]))
                S.wt("vector", t_ec0)
                t_ecm = S.op("vector", lambda E, kk=kk: E.scalar_tensor_tensor(
                    ecmp[:], sm[:, 32:64], 1.0, cmpm[:, kk * 32:(kk + 1) * 32],
                    ALU.mult, ALU.mult, accum_out=sm[:, 0:1]))
                t_imp = S.op("vector", lambda E, kk=kk: E.tensor_add(
                    eswa[:, 0:32], ecmp[:], curadd[:, kk * 32:(kk + 1) * 32]))
                S.op("vector", lambda E: E.max(m8a[:], eswa[:, 0:32]))
                t_sel = S.op("vector", lambda E: E.tensor_single_scalar(
                    selb[:], eswa[:, 0:32], m8a[:, 7:8], ALU.is_ge))
                t_imp_used = t_sel
                S.wt("tensor", t_imp)
                S.pe(lambda E: E.matmul(
                    psA[:, 2016:2032], ident[:], kcmp[:, 0:16], start=True, stop=True))
                t_warm = S.pe_mark()

                if t_pcomb_used[par] is not None:
                    S.wt("vector", t_pcomb_used[par][0])
                    S.wt("vector", t_pcomb_used[par][1])
                S.wt("vector", t_diag)
                tA = S.op("vector", lambda E, eS=eS, pC=pC, L=L, nb=nb: E.scalar_tensor_tensor(
                    pC[:, 0:L], eS[:, 0:L], 1.0,
                    selb[:, 0:nb].rearrange("p n -> p n ()").broadcast_to([128, nb, 64]),
                    ALU.mult, ALU.mult, accum_out=sm[:, 1:2]))
                t_e_used[par] = tA
                bs0 = 0 if kk == 0 else L - 320
                bv = 0 if kk == 0 else 1
                S.wt("vector", t_eswa_used)
                t_esw = S.op("vector", lambda E, par=par, bs0=bs0, bv=bv: E.scalar_tensor_tensor(
                    eswa[:], e_sb[:, par * 1792 + bs0: par * 1792 + bs0 + 320],
                    1.0, bandm[:, bv * 320:(bv + 1) * 320],
                    ALU.mult, ALU.mult, accum_out=sm[:, 2:3]))
                S.op("vector", lambda E: E.tensor_scalar_add(sm[:, 4:7], sm[:, 0:3], 1e-30))
                S.op("vector", lambda E: E.reciprocal(sm[:, 12:15], sm[:, 4:7]))
                t_rr = S.op("vector", lambda E, kk=kk, h=h: E.tensor_tensor(
                    sm[:, 8:11],
                    g_all[:, kk * 48: kk * 48 + 48].rearrange("p (a b) -> p a b", b=16)[:, :, h:h + 1],
                    sm[:, 12:15], ALU.mult))
                tB = S.op("vector", lambda E, pC=pC, L=L: E.tensor_scalar_mul(
                    pC[:, 0:L], pC[:, 0:L], sm[:, 9:10]))
                S.wt("tensor", tA)
                S.pe(lambda E: E.matmul(
                    psA[:, 2016:2032], ident[:], kcmp[:, 0:16], start=True, stop=True))
                S.pe_mark()
                t4 = S.op("vector", lambda E, par=par, bs0=bs0: E.scalar_tensor_tensor(
                    pcomb[:, par * 2048 + bs0: par * 2048 + bs0 + 320], eswa[:], sm[:, 10:11],
                    pcomb[:, par * 2048 + bs0: par * 2048 + bs0 + 320],
                    ALU.mult, ALU.add))
                if par == 0:
                    S.wt("gpsimd", t4)
                    if t_pcomb_used[par] is not None:
                        S.wt("gpsimd", t_pcomb_used[par][0])
                    t4 = S.op("gpsimd", lambda E, par=par, L=L: E.memset(
                        pcomb[:, par * 2048 + L: par * 2048 + L + 256], 0.0))
                t_eswa_used = t4
                t_pre = S.op("vector", lambda E, kk=kk, par=par: E.tensor_scalar(
                    tpcmp[:, par * 32:(par + 1) * 32], ecmp[:], sm[:, 8:9], 1.0 / 64.0,
                    ALU.mult, ALU.mult))
                t_ecmp_used[par] = t_pre
                t_pcomb_ready = t4
                t_tpcmp_ready = t_pre

                if dbg and h == 0 and kk == 2:
                    S.wt("sync", t4)
                    S.dma("sync", "dbg", lambda E, eS=eS: E.dma_start(out=dbg_outs["dbg_e"], in_=eS))
                    S.dma("sync", "dbg", lambda E, pC=pC: E.dma_start(out=dbg_outs["dbg_pcomb"], in_=pC))

                def make_tp(kk=kk, par=par, L=L, pC=pC, t_pcomb_ready=t_pcomb_ready,
                            t_tpcmp_ready=t_tpcmp_ready, t_vct=t_vct, h=h):
                    def emit():
                        S.wt("tensor", t_pcomb_ready)
                        S.wt("tensor", t_pv_pair_last)
                        nch = L // 128 + (2 if par == 0 else 0)
                        transpose_run(
                            nch,
                            lambda j: pC[:, j * 128:(j + 1) * 128],
                            lambda j0, cnt, par=par: pT[:].rearrange(
                                "p (j two n) -> p j two n", two=2, n=128)[:, j0:j0 + cnt, par, :])
                        S.wt("tensor", t_tpcmp_ready)
                        t_pct = transpose_to(
                            lambda: tpcmp[:, par * 32:(par + 1) * 32],
                            lambda: pcmpT[0:32, par * 128:(par + 1) * 128], rows=32)
                        t_tr_last2 = {"vector": trs["last"][0], "scalar": trs["last"][1]}
                        t_pcomb_used[par] = (t_tr_last2["vector"], t_tr_last2["scalar"])
                        return (t_tr_last2, t_pct)
                    return emit

                def make_pv(kk=kk, L=L, h=h):
                    def emit(trtoks):
                        t_tr_last2, t_pct = trtoks
                        pair = kk // 2
                        S.wt("tensor", t_tr_last2["vector"])
                        S.wt("tensor", t_tr_last2["scalar"])
                        S.wt("tensor", t_pct)
                        S.wt("tensor", t_vct)
                        for kq in range(4 * (kk // 4), 4 * (kk // 4) + 4):
                            S.wt("tensor", t_oT_evac[kq])
                        for jc in range(L // 128):
                            S.pe(lambda E, jc=jc, pair=pair: E.matmul(
                                psA[:, 1024 + pair * 256: 1024 + (pair + 1) * 256],
                                vtok[:, jc * 128:(jc + 1) * 128],
                                pT[:, jc * 256:(jc + 1) * 256],
                                start=(jc == 0), stop=False))
                        S.pe(lambda E, pair=pair: E.matmul(
                            psA[:, 1024 + pair * 256: 1024 + pair * 256 + 128],
                            vcmpt[0:32, :], pcmpT[0:32, 0:128], start=False, stop=False))
                        S.pe(lambda E, pair=pair: E.matmul(
                            psA[:, 1024 + pair * 256 + 128: 1024 + (pair + 1) * 256],
                            vcmpt[0:32, :], pcmpT[0:32, 128:256], start=False, stop=True))
                        t_pv_ = S.pe_mark()
                        eng = "vector" if kk < 4 else "scalar"
                        S.wt(eng, t_pv_)
                        tev = S.op(eng, lambda E, kk=kk, h=h, pair=pair: _tcopy(E, 
                            oT_all[:, h * 1024 + pair * 256: h * 1024 + (pair + 1) * 256],
                            psA[:, 1024 + pair * 256: 1024 + (pair + 1) * 256]))
                        t_oT_evac[kk - 1] = tev
                        t_oT_evac[kk] = tev
                        return t_pv_
                    return emit

                pending_tp = make_tp()
                if par == 1:
                    pending_pv = make_pv()
            trtoks = pending_tp()
            t_pv = pending_pv(trtoks)
            pending_tp = None
            pending_pv = None
            t_vtok_used = t_pv
            t_krot_used = t_pv
            t_qrot_used = t_pv

            if dbg and h == 0:
                S.wt("sync", t_oT_evac[NSLOT - 1])
                S.dma("sync", "dbg", lambda E: E.dma_start(
                    out=dbg_outs["dbg_oT"], in_=oT_all[:, 0:1024]))

        S.barrier()
        esA.close()

        esB1 = ExitStack()
        esB2 = ExitStack()
        sbB = lambda name, shape, dt: esB1.enter_context(nc.sbuf_tensor(name, shape, dt, side="right"))
        wo_sb = sbB("wo_sb", [128, NT * 2048], BF16)
        xo_sb = sbB("xo_sb", [128, 2 * 2048], F32)
        x1pre = sbB("x1pre", [128, 2048], F32)
        x1fl = sbB("x1fl", [128, 2048], F32)
        sqscr = sbB("sqscr", [128, 2048], BF16)
        x1bf = esB2.enter_context(nc.sbuf_tensor("x1bf", [128, NSLOT * 2048], BF16))
        x1lo = esB2.enter_context(nc.sbuf_tensor("x1lo", [128, NSLOT * 2048], BF16))

        for c in range(NT):
            S.dma("gpsimd", "gwo", lambda E, c=c: E.dma_start(
                out=wo_sb[:, c * 2048:(c + 1) * 2048], in_=wo_in[c * 128:(c + 1) * 128, :]))
        xo_tok = [None, None]
        xo_cons = [None, None]
        for kk in range(2):
            xo_tok[kk] = S.dma("sync", f"xo{kk}", lambda E, kk=kk: E.dma_start(
                out=xo_sb[:, kk * 2048:(kk + 1) * 2048], in_=x_own[kk * 128:(kk + 1) * 128, :]))

        S.wait("tensor", "gwo")
        t_op_evac = None
        t_x1_stores = []
        for kk in range(NSLOT):
            par = kk % 2
            S.wt("tensor", t_op_evac)
            for hh in range(NH):
                for n in range(4):
                    S.pe(lambda E, kk=kk, hh=hh, n=n: E.matmul(
                        psA[:, n * 512:(n + 1) * 512],
                        oT_all[:, hh * 1024 + kk * 128: hh * 1024 + (kk + 1) * 128],
                        wo_sb[:, hh * 2048 + n * 512: hh * 2048 + (n + 1) * 512],
                        start=(hh == 0), stop=(hh == NH - 1)))
            t_mm = S.pe_mark()
            S.wt("vector", t_mm)
            S.wait("vector", f"xo{par}", xo_tok[par][1])
            t_res = S.op("vector", lambda E, par=par: E.tensor_add(
                x1pre[:], psA[:, 0:2048], xo_sb[:, par * 2048:(par + 1) * 2048]))
            t_op_evac = t_res
            xo_cons[par] = t_res
            if kk + 2 < NSLOT:
                S.wt("sync", t_res)
                xo_tok[par] = S.dma("sync", f"xo{par}", lambda E, kk=kk: E.dma_start(
                    out=xo_sb[:, (kk % 2) * 2048:(kk % 2 + 1) * 2048],
                    in_=x_own[(kk + 2) * 128:(kk + 3) * 128, :]))
            S.wt("scalar", t_res)
            t_sq = S.op("scalar", lambda E: E.activation(
                sqscr[:], x1pre[:], AF.Square, accum_out=sm[:, 16:17]))
            t_mean = S.op("vector", lambda E: E.reduce_sum(sm[:, 17:18], x1pre[:], axis=AX.X))
            S.op("vector", lambda E: E.tensor_scalar_mul(sm[:, 18:19], sm[:, 17:18], 1.0 / D))
            S.wt("vector", t_sq)
            S.op("vector", lambda E: E.tensor_scalar_mul(sm[:, 19:20], sm[:, 16:17], 1.0 / D))
            S.op("vector", lambda E: E.tensor_mul(sm[:, 20:21], sm[:, 18:19], sm[:, 18:19]))
            S.op("vector", lambda E: E.tensor_sub(sm[:, 21:22], sm[:, 19:20], sm[:, 20:21]))
            t_var = S.op("vector", lambda E: E.tensor_scalar_add(sm[:, 22:23], sm[:, 21:22], EPS))
            S.wt("scalar", t_var)
            t_std = S.op("scalar", lambda E: E.activation(sm[:, 23:24], sm[:, 22:23], AF.Sqrt))
            S.wt("vector", t_std)
            t_rstd = S.op("vector", lambda E: E.reciprocal(sm[:, 24:25], sm[:, 23:24]))
            t_x1 = S.op("vector", lambda E: E.tensor_scalar(
                x1fl[:], x1pre[:], sm[:, 18:19], sm[:, 24:25], ALU.subtract, ALU.mult))
            t_hi = S.op("vector", lambda E, kk=kk: _tcopy(E, 
                x1bf[:, kk * 2048:(kk + 1) * 2048], x1fl[:]))
            t_lo = S.op("vector", lambda E, kk=kk: E.tensor_sub(
                x1lo[:, kk * 2048:(kk + 1) * 2048], x1fl[:], x1bf[:, kk * 2048:(kk + 1) * 2048]))
            S.wt("gpsimd", t_hi)
            t_x1_stores.append(S.dma("gpsimd", "x1s", lambda E, kk=kk: E.dma_start(
                out=x1_dram[kk * 128:(kk + 1) * 128, :], in_=x1bf[:, kk * 2048:(kk + 1) * 2048])))
            if dbg and kk == 0:
                S.wt("sync", t_hi)
                S.dma("sync", "dbg", lambda E: E.dma_start(
                    out=dbg_outs["dbg_x1"], in_=x1bf[:, 0:2048]))

        S.barrier()
        esB1.close()
        esO.close()

        esF = ExitStack()
        esF2 = ExitStack()
        sbF = lambda name, shape, dt: esF2.enter_context(nc.sbuf_tensor(name, shape, dt, side="right"))
        x1fm = esF.enter_context(nc.sbuf_tensor("x1fm", [128, NT * 1024], BF16))
        x1lofm = sbF("x1lofm", [128, NT * 1024], BF16)
        ghi = sbF("ghi", [128, NT * NE], BF16)
        glo = sbF("glo", [128, NT * NE], BF16)
        gfl = sbF("gfl", [128, NT * NE], F32)
        scores = sbF("scores", [128, NSLOT * NE], F32)
        m8g = sbF("m8g", [128, NSLOT * 8], F32)
        mig = sbF("mig", [128, NSLOT * 8], U32)
        sel2 = sbF("sel2", [128, NSLOT * NE], BF16)
        rank_all = sbF("rank_all", [128, NSLOT * NE], F32)

        for src_sb, dst_sb in ((x1bf, x1fm), (x1lo, x1lofm)):
            for kk in range(NSLOT):
                transpose_run(
                    NT,
                    lambda j, src_sb=src_sb, kk=kk: src_sb[:, kk * 2048 + j * 128: kk * 2048 + (j + 1) * 128],
                    lambda j0, cnt, dst_sb=dst_sb, kk=kk: dst_sb[:].rearrange(
                        "p (j n) -> p j n", n=1024)[:, j0:j0 + cnt, kk * 128:(kk + 1) * 128])
        t_fm_done2 = {"vector": trs["last"][0], "scalar": trs["last"][1]}

        for c in range(NT):
            S.dma("sync", "cst", lambda E, c=c: E.dma_start(
                out=gfl[:, c * NE:(c + 1) * NE], in_=gateT[c * 128:(c + 1) * 128, :]))
        S.wait("vector", "cst")
        t_ghi = S.op("vector", lambda E: _tcopy(E, ghi[:], gfl[:]))
        t_glo = S.op("vector", lambda E: E.tensor_sub(glo[:], gfl[:], ghi[:]))

        S.wt("tensor", t_fm_done2["vector"])
        S.wt("tensor", t_fm_done2["scalar"])
        S.wt("tensor", t_glo)
        t_sc_ev = None
        gate_steps = [(0, c) for c in range(NT)] + [(1, c) for c in range(NT)] + [(2, c) for c in range(NT)]
        for kk in range(NSLOT):
            S.wt("tensor", t_sc_ev)
            for idx, (p_, c) in enumerate(gate_steps):
                xa = x1fm if p_ in (0, 2) else x1lofm
                ga = ghi if p_ in (0, 1) else glo
                S.pe(lambda E, xa=xa, ga=ga, kk=kk, c=c, idx=idx: E.matmul(
                    psS[:, 0:NE],
                    xa[:, c * 1024 + kk * 128: c * 1024 + (kk + 1) * 128],
                    ga[:, c * NE:(c + 1) * NE],
                    start=(idx == 0), stop=(idx == len(gate_steps) - 1)))
            tpe = S.pe_mark()
            S.wt("scalar", tpe)
            t_sc_ev = S.op("scalar", lambda E, kk=kk: E.activation(
                scores[:, kk * NE:(kk + 1) * NE], psS[:, 0:NE], AF.Sigmoid))

        t_sel2_done = None
        S.wt("vector", t_sc_ev)
        for kk in range(NSLOT):
            S.op("vector", lambda E, kk=kk: E.max(
                m8g[:, kk * 8:(kk + 1) * 8], scores[:, kk * NE:(kk + 1) * NE]))
            S.op("vector", lambda E, kk=kk: E.max_index(
                mig[:, kk * 8:(kk + 1) * 8], m8g[:, kk * 8:(kk + 1) * 8],
                scores[:, kk * NE:(kk + 1) * NE]))
            t_sel2_done = S.op("vector", lambda E, kk=kk: E.tensor_single_scalar(
                sel2[:, kk * NE:(kk + 1) * NE], scores[:, kk * NE:(kk + 1) * NE],
                m8g[:, kk * 8 + 1:kk * 8 + 2], ALU.is_ge))
            S.op("vector", lambda E, kk=kk: _tcopy(E, 
                gA[:, kk * 8 + 0:kk * 8 + 1], mig[:, kk * 8 + 0:kk * 8 + 1]))
            S.op("vector", lambda E, kk=kk: _tcopy(E, 
                gA[:, kk * 8 + 1:kk * 8 + 2], mig[:, kk * 8 + 1:kk * 8 + 2]))
            S.op("vector", lambda E, kk=kk: E.tensor_add(
                sm[:, 30:31], m8g[:, kk * 8 + 0:kk * 8 + 1], m8g[:, kk * 8 + 1:kk * 8 + 2]))
            S.op("vector", lambda E: E.reciprocal(sm[:, 31:32], sm[:, 30:31]))
            S.op("vector", lambda E, kk=kk: E.tensor_mul(
                gA[:, kk * 8 + 4:kk * 8 + 5], m8g[:, kk * 8 + 0:kk * 8 + 1], sm[:, 31:32]))
            S.op("vector", lambda E, kk=kk: E.tensor_mul(
                gA[:, kk * 8 + 5:kk * 8 + 6], m8g[:, kk * 8 + 1:kk * 8 + 2], sm[:, 31:32]))
        if dbg:
            S.wt("sync", t_sel2_done)
            S.dma("sync", "dbg", lambda E: E.dma_start(out=dbg_outs["dbg_scores"], in_=scores[:]))

        S.wt("tensor", t_sel2_done)
        t_rank_ev = None
        for kk in range(NSLOT):
            S.wt("tensor", t_rank_ev)
            for j in range(kk + 1):
                S.pe(lambda E, kk=kk, j=j: E.matmul(
                    psS[:, 0:NE],
                    (triu if j == kk else onesb)[:],
                    sel2[:, j * NE:(j + 1) * NE],
                    start=(j == 0), stop=(j == kk)))
            tpe = S.pe_mark()
            S.wt("vector", tpe)
            t_rank_ev = S.op("vector", lambda E, kk=kk: _tcopy(E, 
                rank_all[:, kk * NE:(kk + 1) * NE], psS[:, 0:NE]))
        if dbg:
            S.wt("sync", t_rank_ev)
            S.dma("sync", "dbg", lambda E: E.dma_start(out=dbg_outs["dbg_rank"], in_=rank_all[:]))

        t_toki = S.op("vector", lambda E: _tcopy(E, toki[:], iotatok[:]))
        zbuf = sbF("zbuf", [128, 32], I32)
        t_z = S.op("vector", lambda E: E.memset(zbuf[:], 0))
        S.wt("sync", t_z)
        t_idz = S.dma("sync", "idz", lambda E: E.dma_start(
            out=idx_dram.rearrange("(p k) o -> p (k o)", p=128), in_=zbuf[:]))
        ohb = sbF("ohb", [128, NE], F32)
        t_scat_last = None
        for kk in range(NSLOT):
            for cpy in range(2):
                col = kk * 8 + cpy
                rcol = kk * 8 + 2 + cpy
                S.op("vector", lambda E, col=col: E.tensor_single_scalar(
                    ohb[:], iota16[:], gA[:, col:col + 1], ALU.is_equal))
                S.op("vector", lambda E, kk=kk, rcol=rcol: E.scalar_tensor_tensor(
                    ohb[:], rank_all[:, kk * NE:(kk + 1) * NE], 1.0, ohb[:],
                    ALU.mult, ALU.mult, accum_out=gA[:, rcol:rcol + 1]))
                S.op("vector", lambda E, rcol=rcol: E.tensor_scalar_min(
                    gA[:, rcol:rcol + 1], gA[:, rcol:rcol + 1], float(CAP - 1)))
                S.op("vector", lambda E, col=col, rcol=rcol: E.tensor_scalar(
                    sm[:, 33:34], gA[:, col:col + 1], float(CAP), 1.0, ALU.mult, ALU.mult))
                t_s = S.op("vector", lambda E, rcol=rcol: E.tensor_add(
                    sm[:, 34:35], sm[:, 33:34], gA[:, rcol:rcol + 1]))
                dst_i = s1i if cpy == 0 else s2i
                t_si = S.op("vector", lambda E, dst_i=dst_i, kk=kk: _tcopy(E, 
                    dst_i[:, kk:kk + 1], sm[:, 34:35]))
                S.op("vector", lambda E, dst_i=dst_i, kk=kk: E.tensor_single_scalar(
                    iwrk[:, 0:1], dst_i[:, kk:kk + 1], 127, ALU.bitwise_and))
                S.op("vector", lambda E: E.tensor_single_scalar(
                    iwrk[:, 1:2], iwrk[:, 0:1], 5, ALU.logical_shift_left))
                S.op("vector", lambda E, dst_i=dst_i, kk=kk: E.tensor_single_scalar(
                    iwrk[:, 2:3], dst_i[:, kk:kk + 1], 7, ALU.arith_shift_right))
                pcol = kk * 2 + cpy
                t_pos = S.op("vector", lambda E, pcol=pcol: E.tensor_tensor(
                    posb[:, pcol:pcol + 1], iwrk[:, 1:2], iwrk[:, 2:3], ALU.add))
                S.wt("gpsimd", t_pos)
                S.wt("gpsimd", t_toki)
                S.wait("gpsimd", "idz", t_idz[1])
                t_scat_last = S.dma("gpsimd", "isc", lambda E, kk=kk, pcol=pcol: E.indirect_dma_start(
                    idx_dram[:], IndirectOffsetOnAxis(posb[:, pcol:pcol + 1], 0),
                    toki[:, kk:kk + 1], None))
        S.wait("gpsimd", "isc")
        t_ilb = S.dma("gpsimd", "ilb", lambda E: E.dma_start(
            out=idxsb[:], in_=idx_dram.rearrange("(p k) o -> p (k o)", p=128)))
        if dbg:
            S.wait("sync", "ilb")
            S.dma("sync", "dbg", lambda E: E.dma_start(out=dbg_outs["dbg_idx"], in_=idxsb[:]))

        S.barrier()
        esF2.close()
        esS1 = ExitStack()
        rbuf = esF.enter_context(nc.sbuf_tensor("rbuf", [128, 8 * 2048], BF16))
        ring_cons = [None] * 8
        ring_n = [0]

        def ring_load(src_ap):
            g = ring_n[0]
            ring_n[0] += 1
            slot = g % 8
            if ring_cons[slot] is not None:
                S.wt("sync", ring_cons[slot])
            tok = S.dma("sync", f"ring{slot}", lambda E, src_ap=src_ap, slot=slot: E.dma_start(
                out=rbuf[:, slot * 2048:(slot + 1) * 2048], in_=src_ap))
            return slot, tok

        h_sh = esS1.enter_context(nc.sbuf_tensor("h_sh", [128, NSC * 1024], BF16, side="right"))
        sa_sh = esS1.enter_context(nc.sbuf_tensor("sa_sh", [128, 1024], BF16, side="right"))

        sh_chunks = []
        for j in range(NSC):
            sh_chunks.append(sw1c[j])
            sh_chunks.append(sw3c[j])
        sh_slots = [None] * len(sh_chunks)
        loaded_n = [0]

        def ensure_loaded(upto):
            while loaded_n[0] <= upto:
                g = loaded_n[0]
                sh_slots[g] = ring_load(sh_chunks[g])
                loaded_n[0] += 1

        ensure_loaded(5)
        S.wt("tensor", t_fm_done2["vector"])
        S.wt("tensor", t_fm_done2["scalar"])
        t_psS_u = None
        t_psA_u = None
        t_sa_u = None
        for j in range(NSC):
            ensure_loaded(min(2 * j + 7, len(sh_chunks) - 1))
            sl1, tk1 = sh_slots[2 * j]
            sl3, tk3 = sh_slots[2 * j + 1]
            S.wait("tensor", tk1[0], tk1[1])
            S.wt("tensor", t_psS_u)
            for c in range(NT):
                for i in range(2):
                    S.pe(lambda E, sl1=sl1, c=c, i=i: E.matmul(
                        psS[:, i * 512:(i + 1) * 512],
                        rbuf[:, sl1 * 2048 + c * 128: sl1 * 2048 + (c + 1) * 128],
                        x1fm[:, c * 1024 + i * 512: c * 1024 + (i + 1) * 512],
                        start=(c == 0), stop=(c == NT - 1)))
            t_a = S.pe_mark()
            ring_cons[sl1] = t_a
            S.wait("tensor", tk3[0], tk3[1])
            S.wt("tensor", t_psA_u)
            for c in range(NT):
                for i in range(2):
                    S.pe(lambda E, sl3=sl3, c=c, i=i: E.matmul(
                        psA[:, i * 512:(i + 1) * 512],
                        rbuf[:, sl3 * 2048 + c * 128: sl3 * 2048 + (c + 1) * 128],
                        x1fm[:, c * 1024 + i * 512: c * 1024 + (i + 1) * 512],
                        start=(c == 0), stop=(c == NT - 1)))
            t_b = S.pe_mark()
            ring_cons[sl3] = t_b
            S.wt("scalar", t_a)
            S.wt("scalar", t_sa_u)
            t_sa = S.op("scalar", lambda E: E.activation(sa_sh[:], psS[:, 0:1024], AF.Silu))
            t_psS_u = t_sa
            S.wt("vector", t_b)
            S.wt("vector", t_sa)
            t_h = S.op("vector", lambda E, j=j: E.tensor_mul(
                h_sh[:, j * 1024:(j + 1) * 1024], psA[:, 0:1024], sa_sh[:]))
            t_psA_u = t_h
            t_sa_u = t_h

        S.barrier()
        esF.close()
        esB2.close()

        esS2 = ExitStack()
        s2w = esS2.enter_context(nc.sbuf_tensor("s2w", [128, NSC * 2048], BF16))
        y0b = esS2.enter_context(nc.sbuf_tensor("y0b", [128, 2 * 2048], F32))
        for j in range(NSC):
            S.dma("sync", "s2w", lambda E, j=j: E.dma_start(
                out=s2w[:, j * 2048:(j + 1) * 2048], in_=sw2r[j]))
        S.wait("tensor", "s2w")
        t_y0_ev = [None, None]
        t_y0_st = [None, None]
        for tt in range(NSLOT):
            pb = tt % 2
            S.wt("tensor", t_psA_u)
            S.wt("tensor", t_y0_ev[pb])
            for j in range(NSC):
                for n in range(4):
                    S.pe(lambda E, j=j, tt=tt, n=n: E.matmul(
                        psA[:, n * 512:(n + 1) * 512],
                        h_sh[:, j * 1024 + tt * 128: j * 1024 + (tt + 1) * 128],
                        s2w[:, j * 2048 + n * 512: j * 2048 + (n + 1) * 512],
                        start=(j == 0), stop=(j == NSC - 1)))
            t_mm = S.pe_mark()
            t_psA_u = t_mm
            S.wt("vector", t_mm)
            if t_y0_st[pb] is not None:
                S.wait("vector", f"y0s{pb}", t_y0_st[pb][1])
            t_ev = S.op("vector", lambda E, pb=pb: _tcopy(E, 
                y0b[:, pb * 2048:(pb + 1) * 2048], psA[:, 0:2048]))
            t_y0_ev[pb] = t_ev
            t_psA_u = t_ev
            S.wt("gpsimd", t_ev)
            t_y0_st[pb] = S.dma("gpsimd", f"y0s{pb}", lambda E, tt=tt, pb=pb: E.dma_start(
                out=y0_dram[tt * 128:(tt + 1) * 128, :], in_=y0b[:, pb * 2048:(pb + 1) * 2048]))
        if dbg:
            S.wait("sync", "y0s0")
            S.wait("sync", "y0s1")
            S.dma("sync", "dbg", lambda E: E.dma_start(
                out=dbg_outs["dbg_y0"], in_=y0b[:, (NSLOT - 1) % 2 * 2048:((NSLOT - 1) % 2 + 1) * 2048]))

        S.barrier()
        esS1.close()
        esS2.close()

        esE = ExitStack()
        sbE = lambda name, shape, dt: esE.enter_context(nc.sbuf_tensor(name, shape, dt))
        rbuf2 = sbE("rbuf2", [128, 32 * 2048], BF16)
        xg = sbE("xg", [128, 2 * 2048], BF16)
        xgfm = sbE("xgfm", [128, NT * 256], BF16)
        sa_e = sbE("sa_e", [128, 2 * IFF], BF16)
        h_tok = sbE("h_tok", [128, 2 * IFF], BF16)
        h_ex = sbE("h_ex", [128, NIC * 256], BF16)
        ysb = sbE("ysb", [128, 2 * 2048], F32)

        ring2_cons = [None] * 32
        ring2_n = [0]

        def ring2_load(src_ap):
            g = ring2_n[0]
            ring2_n[0] += 1
            slot = g % 32
            w = src_ap.shape[-1]
            if ring2_cons[slot] is not None:
                S.wt("sync", ring2_cons[slot])
            tok = S.dma("sync", f"rg{slot}", lambda E, src_ap=src_ap, slot=slot, w=w: E.dma_start(
                out=rbuf2[:, slot * 2048:slot * 2048 + w], in_=src_ap))
            return slot, tok

        EXC = 2 * NT + NIC
        ex_chunks = []
        for e in range(NE):
            for c in range(NT):
                ex_chunks.append(w1c[e, c])
            for c in range(NT):
                ex_chunks.append(w3c[e, c])
            for j in range(NIC):
                ex_chunks.append(w2r[e, j])
        ex_slots = [None] * len(ex_chunks)
        ex_loaded = [0]

        def ex_ensure(upto):
            while ex_loaded[0] <= upto:
                g = ex_loaded[0]
                ex_slots[g] = ring2_load(ex_chunks[g])
                ex_loaded[0] += 1

        S.wait("gpsimd", "x1s")
        S.wait("gpsimd", "ilb", t_ilb[1])
        t_xg_cons = [None, None]
        t_xgfm_cons = None
        t_s1_free = None
        t_sa_u = None
        t_htok_u = None
        t_hex_w = None
        t_s2_banks = None
        t_ys_st = [None, None]
        t_ys_ev = [None, None]
        NCH = [512, 512, IFF - 1024]

        def s1_region(n):
            if n < 2:
                return psS[:, n * 512: n * 512 + NCH[n]]
            return psA[:, 0:NCH[2]]

        pending_s2 = []

        for e in range(NE):
            ex_ensure(min(EXC * e + 15, len(ex_chunks) - 1))
            for te in range(2):
                kt = 2 * e + te
                if t_xg_cons[te] is not None:
                    S.wt("gpsimd", t_xg_cons[te])
                gtok = S.dma("gpsimd", f"xg{te}", lambda E, kt=kt, te=te: E.indirect_dma_start(
                    xg[:, te * 2048:(te + 1) * 2048], None,
                    x1_dram[:], IndirectOffsetOnAxis(idxsb[:, kt:kt + 1], 0)))
                S.wait("tensor", f"xg{te}", gtok[1])
                if t_xgfm_cons is not None:
                    S.wt("tensor", t_xgfm_cons)
                for c in range(NT):
                    transpose_to(
                        lambda te=te, c=c: xg[:, te * 2048 + c * 128: te * 2048 + (c + 1) * 128],
                        lambda te=te, c=c: xgfm[:, c * 256 + te * 128: c * 256 + (te + 1) * 128])
                t_xg_cons[te] = None
            t_xg_tr = tr_both()

            def s1_pass(mat, t, first):
                S.wt("tensor", t_s1_free)
                if first:
                    S.wt("tensor", t_xg_tr[0])
                    S.wt("tensor", t_xg_tr[1])
                tkm = None
                for c in range(NT):
                    ex_ensure(EXC * e + mat * NT + c)
                    slw, tkw = ex_slots[EXC * e + mat * NT + c]
                    S.wait("tensor", tkw[0], tkw[1])
                    for n in range(3):
                        S.pe(lambda E, slw=slw, c=c, t=t, n=n: E.matmul(
                            s1_region(n),
                            xgfm[:, c * 256 + t * 128: c * 256 + (t + 1) * 128],
                            rbuf2[:, slw * 2048 + n * 512: slw * 2048 + n * 512 + NCH[n]],
                            start=(c == 0), stop=(c == NT - 1)))
                    tkm = S.pe_mark()
                    if mat == 1 and t == 1:
                        ring2_cons[slw] = tkm
                    if mat == 0 and t == 1:
                        ring2_cons[slw] = tkm
                return tkm

            def silu_evac(t):
                nonlocal_toks = []
                last = None
                for n in range(3):
                    S.wt("scalar", t_s1_pass_tok)
                    if t == 0 and t_sa_u is not None:
                        S.wt("scalar", t_sa_u)
                    last = S.op("scalar", lambda E, t=t, n=n: E.activation(
                        sa_e[:, t * IFF + 512 * n: t * IFF + 512 * n + NCH[n]],
                        s1_region(n), AF.Silu))
                return last

            def h_evac(t):
                last = None
                for n in range(3):
                    S.wt("vector", t_s1_pass_tok)
                    S.wt("vector", t_sa_tok[t])
                    if t_htok_u is not None:
                        S.wt("vector", t_htok_u[0])
                        S.wt("vector", t_htok_u[1])
                    last = S.op("vector", lambda E, t=t, n=n: E.tensor_mul(
                        h_tok[:, t * IFF + 512 * n: t * IFF + 512 * n + NCH[n]],
                        s1_region(n),
                        sa_e[:, t * IFF + 512 * n: t * IFF + 512 * n + NCH[n]]))
                return last

            t_sa_tok = [None, None]
            t_s1_pass_tok = s1_pass(0, 0, True)
            if len(pending_s2) > 0:
                pending_s2[0]()
            t_sa_tok[0] = silu_evac(0)
            t_s1_free = t_sa_tok[0]
            if len(pending_s2) > 1:
                pending_s2[1]()
            t_s1_pass_tok = s1_pass(0, 1, False)
            if len(pending_s2) > 2:
                pending_s2[2]()
            t_sa_tok[1] = silu_evac(1)
            t_s1_free = t_sa_tok[1]
            if len(pending_s2) > 3:
                pending_s2[3]()
            t_s1_pass_tok = s1_pass(1, 0, False)
            t_h0 = h_evac(0)
            t_s1_free = t_h0
            t_s1_pass_tok = s1_pass(1, 1, False)
            t_xgfm_cons = t_s1_pass_tok
            t_xg_cons = [t_s1_pass_tok, t_s1_pass_tok]
            t_h1 = h_evac(1)
            t_s1_free = t_h1

            S.wt("tensor", t_h1)
            for t in range(2):
                transpose_run(
                    NIC,
                    lambda j, t=t: h_tok[:, t * IFF + j * 128: t * IFF + (j + 1) * 128],
                    lambda j0, cnt, t=t: h_ex[:].rearrange(
                        "p (j n) -> p j n", n=256)[:, j0:j0 + cnt, t * 128:(t + 1) * 128])
            t_hex_w = tr_both()
            t_htok_u = t_hex_w

            def make_piece(e, tt, dh, hexw):
                def emit():
                    nonlocal t_s2_banks
                    S.wt("tensor", t_s2_banks)
                    S.wt("tensor", hexw[0])
                    S.wt("tensor", hexw[1])
                    tkm = None
                    for j in range(NIC):
                        ex_ensure(EXC * e + 2 * NT + j)
                        sl2, tk2 = ex_slots[EXC * e + 2 * NT + j]
                        S.wait("tensor", tk2[0], tk2[1])
                        for n in range(2):
                            S.pe(lambda E, sl2=sl2, j=j, tt=tt, dh=dh, n=n: E.matmul(
                                psA[:, 1024 + n * 512: 1024 + (n + 1) * 512],
                                h_ex[:, j * 256 + tt * 128: j * 256 + (tt + 1) * 128],
                                rbuf2[:, sl2 * 2048 + dh * 1024 + n * 512: sl2 * 2048 + dh * 1024 + (n + 1) * 512],
                                start=(j == 0), stop=(j == NIC - 1)))
                    tkm = S.pe_mark()
                    if tt == 1 and dh == 1:
                        for j in range(NIC):
                            ring2_cons[ex_slots[EXC * e + 2 * NT + j][0]] = tkm
                    pb = tt
                    S.wt("vector", tkm)
                    if dh == 0 and t_ys_st[pb] is not None:
                        S.wait("vector", f"yss{pb}", t_ys_st[pb][1])
                    t_ev = S.op("vector", lambda E, pb=pb, dh=dh: _tcopy(E, 
                        ysb[:, pb * 2048 + dh * 1024: pb * 2048 + (dh + 1) * 1024],
                        psA[:, 1024:2048]))
                    t_s2_banks = t_ev
                    if dh == 1:
                        S.wt("gpsimd", t_ev)
                        t_ys_st[pb] = S.dma("gpsimd", f"yss{pb}", lambda E, e=e, tt=tt, pb=pb: E.dma_start(
                            out=ys_dram[e * CAP + tt * 128: e * CAP + (tt + 1) * 128, :],
                            in_=ysb[:, pb * 2048:(pb + 1) * 2048]))
                    return tkm
                return emit
            pending_s2 = [make_piece(e, 0, 0, t_hex_w), make_piece(e, 0, 1, t_hex_w),
                          make_piece(e, 1, 0, t_hex_w), make_piece(e, 1, 1, t_hex_w)]
            ex_ensure(min(EXC * (e + 1) + 15, len(ex_chunks) - 1))

        for p in pending_s2:
            p()

        if dbg:
            S.wait("sync", "yss0")
            S.wait("sync", "yss1")
            S.dma("sync", "dbg", lambda E: E.dma_start(
                out=dbg_outs["dbg_ysort"], in_=ysb[:, 0:2048]))

        S.barrier()
        esE.close()

        esC = ExitStack()
        sbC = lambda name, shape, dt: esC.enter_context(nc.sbuf_tensor(name, shape, dt))
        y0l = sbC("y0l", [128, 2 * 2048], F32)
        g1b = sbC("g1b", [128, 2 * 2048], F32)
        g2b = sbC("g2b", [128, 2 * 2048], F32)
        x1rl = sbC("x1rl", [128, 2 * 2048], BF16)
        ycmb = sbC("ycmb", [128, 2048], F32)
        sqscr2 = sbC("sqscr2", [128, 2048], BF16)
        outsb = sbC("outsb", [128, 2 * 2048], F32)

        S.wait("gpsimd", "yss0")
        S.wait("gpsimd", "yss1")
        S.wait("sync", "y0s0")
        S.wait("sync", "y0s1")
        t_y0l = [None, None]
        t_g1 = [None, None]
        t_g2 = [None, None]
        t_x1l = [None, None]
        t_cons = [None, None]
        t_out_ev = [None, None]

        def phc_load(kk):
            pb = kk % 2
            if t_cons[pb] is not None:
                S.wt("sync", t_cons[pb])
                S.wt("gpsimd", t_cons[pb])
            t_y0l[pb] = S.dma("sync", f"y0l{pb}", lambda E, kk=kk, pb=pb: E.dma_start(
                out=y0l[:, pb * 2048:(pb + 1) * 2048], in_=y0_dram[kk * 128:(kk + 1) * 128, :]))
            t_x1l[pb] = S.dma("sync", f"x1l{pb}", lambda E, kk=kk, pb=pb: E.dma_start(
                out=x1rl[:, pb * 2048:(pb + 1) * 2048], in_=x1_dram[kk * 128:(kk + 1) * 128, :]))
            t_g1[pb] = S.dma("gpsimd", f"gg1{pb}", lambda E, kk=kk, pb=pb: E.indirect_dma_start(
                g1b[:, pb * 2048:(pb + 1) * 2048], None,
                ys_dram[:], IndirectOffsetOnAxis(s1i[:, kk:kk + 1], 0)))
            t_g2[pb] = S.dma("gpsimd", f"gg2{pb}", lambda E, kk=kk, pb=pb: E.indirect_dma_start(
                g2b[:, pb * 2048:(pb + 1) * 2048], None,
                ys_dram[:], IndirectOffsetOnAxis(s2i[:, kk:kk + 1], 0)))

        phc_load(0)
        phc_load(1)
        for kk in range(NSLOT):
            pb = kk % 2
            for tname, tl in (("y0l", t_y0l), ("x1l", t_x1l)):
                S.wait("vector", f"{tname}{pb}", tl[pb][1])
            S.wait("vector", f"gg1{pb}", t_g1[pb][1])
            S.wait("vector", f"gg2{pb}", t_g2[pb][1])
            t1 = S.op("vector", lambda E, kk=kk, pb=pb: E.scalar_tensor_tensor(
                ycmb[:], g1b[:, pb * 2048:(pb + 1) * 2048], gA[:, kk * 8 + 4:kk * 8 + 5],
                y0l[:, pb * 2048:(pb + 1) * 2048], ALU.mult, ALU.add))
            t2 = S.op("vector", lambda E, kk=kk, pb=pb: E.scalar_tensor_tensor(
                ycmb[:], g2b[:, pb * 2048:(pb + 1) * 2048], gA[:, kk * 8 + 5:kk * 8 + 6],
                ycmb[:], ALU.mult, ALU.add))
            t3 = S.op("vector", lambda E, pb=pb: E.tensor_add(
                ycmb[:], ycmb[:], x1rl[:, pb * 2048:(pb + 1) * 2048]))
            t_cons[pb] = t3
            S.wt("scalar", t3)
            t_sq = S.op("scalar", lambda E: E.activation(
                sqscr2[:], ycmb[:], AF.Square, accum_out=sm[:, 40:41]))
            S.op("vector", lambda E: E.reduce_sum(sm[:, 41:42], ycmb[:], axis=AX.X))
            S.op("vector", lambda E: E.tensor_scalar_mul(sm[:, 42:43], sm[:, 41:42], 1.0 / D))
            S.wt("vector", t_sq)
            S.op("vector", lambda E: E.tensor_scalar_mul(sm[:, 43:44], sm[:, 40:41], 1.0 / D))
            S.op("vector", lambda E: E.tensor_mul(sm[:, 44:45], sm[:, 42:43], sm[:, 42:43]))
            S.op("vector", lambda E: E.tensor_sub(sm[:, 45:46], sm[:, 43:44], sm[:, 44:45]))
            t_var = S.op("vector", lambda E: E.tensor_scalar_add(sm[:, 46:47], sm[:, 45:46], EPS))
            S.wt("scalar", t_var)
            t_std = S.op("scalar", lambda E: E.activation(sm[:, 47:48], sm[:, 46:47], AF.Sqrt))
            S.wt("vector", t_std)
            S.op("vector", lambda E: E.reciprocal(sm[:, 48:49], sm[:, 47:48]))
            if t_out_ev[pb] is not None:
                S.wait("vector", f"sto{pb}", t_out_ev[pb][1])
            t_o = S.op("vector", lambda E, pb=pb: E.tensor_scalar(
                outsb[:, pb * 2048:(pb + 1) * 2048], ycmb[:], sm[:, 42:43], sm[:, 48:49],
                ALU.subtract, ALU.mult))
            S.wt("scalar", t_o)
            t_out_ev[pb] = S.dma("scalar", f"sto{pb}", lambda E, kk=kk, pb=pb: E.dma_start(
                out=out_d[kk * 128:(kk + 1) * 128, :], in_=outsb[:, pb * 2048:(pb + 1) * 2048]))
            if kk + 2 < NSLOT:
                phc_load(kk + 2)

        if dbg:
            S.wt("sync", t_sel2_done)
            s12f = sbC("s12f", [128, 2 * NSLOT], F32)
            S.op("vector", lambda E: _tcopy(E, s12f[:, 0:NSLOT], s1i[:]))
            tlast = S.op("vector", lambda E: _tcopy(E, s12f[:, NSLOT:2 * NSLOT], s2i[:]))
            S.wt("sync", tlast)
            S.dma("sync", "dbg", lambda E: E.dma_start(out=dbg_outs["dbg_s12"], in_=s12f[:]))

        S.wait("sync", "sto0")
        S.wait("sync", "sto1")
        esC.close()

        with nc.Block() as block:
            S.build(block)

    return nc



def _host_prep(inputs):
    x = np.asarray(inputs["L0_attn_x"], dtype=np.float32)
    wq = np.asarray(inputs["L0_attn_Wq"], dtype=np.float32)
    wk = np.asarray(inputs["L0_attn_Wk"], dtype=np.float32)
    wv = np.asarray(inputs["L0_attn_Wv"], dtype=np.float32)
    wg = np.asarray(inputs["L0_attn_Wg"], dtype=np.float32)
    wo = np.asarray(inputs["L0_attn_Wo"], dtype=np.float32)
    gate_w = np.asarray(inputs["L0_ffn_gate_w"], dtype=np.float32)
    w1 = np.asarray(inputs["L0_ffn_w1"])
    w2 = np.asarray(inputs["L0_ffn_w2"])
    w3 = np.asarray(inputs["L0_ffn_w3"])
    sw1 = np.asarray(inputs["L0_ffn_sw1"])
    sw2 = np.asarray(inputs["L0_ffn_sw2"])
    sw3 = np.asarray(inputs["L0_ffn_sw3"])

    def headchunk(w):
        return np.ascontiguousarray(w.reshape(NT, 128, NH, HD).transpose(2, 1, 0, 3))

    wqh, wkh, wvh = headchunk(wq), headchunk(wk), headchunk(wv)

    def ichunk(wT, n):
        out = np.empty((n, 128, D), dtype=wT.dtype)
        for j in range(n):
            out[j] = wT[:, j * 128:(j + 1) * 128].reshape(NT, 128, 128).transpose(1, 0, 2).reshape(128, D)
        return out

    sw1c = ichunk(np.ascontiguousarray(sw1.T), NSC)
    sw3c = ichunk(np.ascontiguousarray(sw3.T), NSC)
    sw2r = np.ascontiguousarray(sw2.T.reshape(NSC, 128, D))
    w1c = np.ascontiguousarray(w1.transpose(0, 2, 1).reshape(NE, NT, 128, IFF))
    w3c = np.ascontiguousarray(w3.transpose(0, 2, 1).reshape(NE, NT, 128, IFF))
    w2r = np.stack([np.ascontiguousarray(w2[e].T.reshape(NIC, 128, D)) for e in range(NE)])
    gateT = np.ascontiguousarray(gate_w.T)

    ident = np.eye(128, dtype=BF)
    half = HD // 2
    inv = 1.0 / (10000.0 ** (np.arange(half, dtype=np.float64) / half))
    pos = np.arange(2048, dtype=np.float64)
    th = pos[:, None] * inv[None, :]
    cosv = np.cos(th).astype(np.float32)
    sinv = np.sin(th).astype(np.float32)
    cc = np.concatenate([cosv.T, cosv.T], axis=0).astype(BF)
    ssf = np.concatenate([-sinv.T, sinv.T], axis=0).astype(BF)
    pswap = np.zeros((128, 128), dtype=BF)
    pswap[(np.arange(128) + 64) % 128, np.arange(128)] = 1.0
    ones = np.ones((128, 128), dtype=BF)
    triu = np.triu(np.ones((128, 128), dtype=np.float32), 1).astype(BF)
    iota16 = np.broadcast_to(np.arange(NE, dtype=np.float32), (128, NE)).copy()
    iotatok = (np.arange(NSLOT)[None, :] * 128 + np.arange(128)[:, None]).astype(np.float32)

    consts = dict(c_ident=ident, c_cc=cc, c_ss=ssf, c_pswap=pswap, c_ones=ones,
                  c_triu=triu, c_iota16=iota16, c_iotatok=iotatok)
    shared = dict(wqh=wqh, wkh=wkh, wvh=wvh, wo=wo, wg=wg, gateT=gateT,
                  sw1c=sw1c, sw3c=sw3c, sw2r=sw2r, w1c=w1c, w3c=w3c, w2r=w2r, **consts)

    pvec = np.arange(128)
    in_maps = []
    for core in range(8):
        b, par = core // 2, core % 2
        own = np.concatenate([np.arange(128 * (2 * kk + par), 128 * (2 * kk + par) + 128)
                              for kk in range(NSLOT)])
        xb = x[b]
        m = dict(shared)
        m["xT"] = np.ascontiguousarray(xb.T)
        m["xT_own"] = np.ascontiguousarray(xb.T[:, own])
        m["x_own"] = np.ascontiguousarray(xb[own])
        m["c_ccq"] = np.ascontiguousarray(cc[:, own])
        m["c_ssq"] = np.ascontiguousarray(ssf[:, own])
        causal = np.zeros((128, 256), dtype=np.float32)
        for p in range(128):
            lim = p if par == 0 else 128 + p
            causal[p, :lim + 1] = 1.0
        m["c_causal"] = causal.astype(BF)
        bandm = np.zeros((2, 128, 320), dtype=np.float32)
        for v in range(2):
            for p in range(128):
                for bj in range(320):
                    if v == 0:
                        L = 256
                        j = bj
                    else:
                        L = 512
                        j = L - 320 + bj
                    i = (L - 256 + p) if par == 0 else (L - 128 + p)
                    if 0 <= j < L and 0 <= i - j <= 64:
                        bandm[v, p, bj] = 1.0
        m["c_band"] = bandm.astype(BF)
        cmpm = np.zeros((NSLOT, 128, 32), dtype=np.float32)
        curadd = np.zeros((NSLOT, 128, 32), dtype=np.float32)
        for kk in range(NSLOT):
            t = 2 * kk + par
            i = 128 * t + pvec
            for mm_ in range(32):
                vis = (i >= 64 * (mm_ + 1) - 1)
                cmpm[kk, vis, mm_] = 1.0
            cur = i // 64
            curadd[kk, pvec, cur] = 1e30
        m["c_cmpm"] = cmpm
        m["c_curadd"] = curadd
        in_maps.append(m)
    return in_maps


def kernel(**inputs):
    in_maps = _host_prep(inputs)
    nc = build_program()
    res = run_bass_kernel_spmd(nc, in_maps, core_ids=list(range(8)))
    out = np.empty((4, 2048, D), dtype=np.float32)
    for core in range(8):
        b, par = core // 2, core % 2
        o = res.results[core]["out"]
        for kk in range(NSLOT):
            t = 2 * kk + par
            out[b, 128 * t:128 * (t + 1)] = o[128 * kk:128 * (kk + 1)]
    return out


# version 22
# speedup vs baseline: 1.0350x; 1.0350x over previous
import numpy as np
import ml_dtypes
from contextlib import ExitStack

import concourse.bass as bass
import concourse.mybir as mybir
from concourse.bass_core_rust import IndirectOffsetOnAxis
from concourse.bass_utils import run_bass_kernel_spmd

BF = ml_dtypes.bfloat16
F32 = mybir.dt.float32
BF16 = mybir.dt.bfloat16
I32 = mybir.dt.int32
U32 = mybir.dt.uint32
AF = mybir.ActivationFunctionType
ALU = mybir.AluOpType
AX = mybir.AxisListType

D = 2048
NH = 16
HD = 128
NSLOT = 8
NT = 16
NE = 16
IFF = 1408
SI = 2816
CAP = 256
NIC = IFF // 128
NSC = SI // 128
SCALE = float(HD) ** -0.5
EPS = 1e-5

ENGS = ("tensor", "vector", "scalar", "gpsimd", "sync")
SELF = {"tensor": "PE", "vector": "DVE", "scalar": "ACT", "gpsimd": "GP", "sync": "SQ"}


class Sched:

    def __init__(self, nc, es):
        self.nc = nc
        self.es = es
        self.q = {e: [] for e in ENGS}
        self.sems = {}
        self.C = {}
        self._pe_pending = None

    def _pe_flush(self):
        if self._pe_pending is not None:
            fnp = self._pe_pending
            self._pe_pending = None
            self.q["tensor"].append(lambda E, fnp=fnp: fnp(E))

    def sem(self, name):
        if name not in self.sems:
            self.sems[name] = self.es.enter_context(self.nc.semaphore(name))
            self.C[name] = 0
        return self.sems[name]

    def wait(self, eng, name, val=None):
        s = self.sem(name)
        v = self.C[name] if val is None else val
        if v > 0:
            self.q[eng].append(lambda E, s=s, v=v: E.wait_ge(s, v))

    def wt(self, eng, tok):
        if tok is not None:
            self.wait(eng, tok[0], tok[1])

    def op(self, eng, fn):
        if eng == "tensor":
            self._pe_flush()
        sn = SELF[eng]
        s = self.sem(sn)
        if eng != "tensor" and self.C[sn] > 0:
            self.q[eng].append(lambda E, s=s, v=self.C[sn]: E.wait_ge(s, v))
        self.C[sn] += 1
        v = self.C[sn]
        self.q[eng].append(lambda E, fn=fn, s=s: fn(E).then_inc(s, 1))
        return (sn, v)

    def pe(self, fn):
        self._pe_flush()
        self._pe_pending = fn

    def pe_mark(self):
        if self._pe_pending is None:
            return self.op("tensor", lambda E: E.nop())
        fnp = self._pe_pending
        self._pe_pending = None
        return self.op("tensor", fnp)

    def dma(self, eng, name, fn):
        s = self.sem(name)
        self.C[name] += 16
        v = self.C[name]
        self.q[eng].append(lambda E, fn=fn, s=s: fn(E).then_inc(s, 16))
        return (name, v)

    def barrier(self):
        snap = {n: v for n, v in self.C.items() if v > 0}
        for eng in ENGS:
            sn = SELF[eng]
            for name, v in snap.items():
                if name != sn:
                    self.wait(eng, name, v)

    def build(self, block):
        self._pe_flush()
        for eng in ENGS:
            lst = self.q[eng]
            if lst:
                getattr(block, eng)(lambda E, lst=lst: [f(E) for f in lst])


def _tcopy(E, out, in_):
    fn = getattr(E, "tensor_copy", None)
    if fn is None:
        fn = E.copy
    return fn(out, in_)


def ceil_div(a, b):
    return (a + b - 1) // b


def build_program(dbg=False):
    nc = bass.Bass("TRN2", target_bir_lowering=False)

    def din(name, shape, dt):
        return nc.dram_tensor(name, shape, dt, kind="ExternalInput").ap()

    xT = din("xT", [D, 2048], F32)
    xT_own = din("xT_own", [D, 1024], F32)
    x_own = din("x_own", [1024, D], F32)
    wqh = din("wqh", [NH, 128, NT, HD], F32)
    wkh = din("wkh", [NH, 128, NT, HD], F32)
    wvh = din("wvh", [NH, 128, NT, HD], F32)
    wo_in = din("wo", [D, D], F32)
    wg_in = din("wg", [D, 48], F32)
    gateT = din("gateT", [D, NE], F32)
    sw1c = din("sw1c", [NSC, 128, D], BF16)
    sw3c = din("sw3c", [NSC, 128, D], BF16)
    sw2r = din("sw2r", [NSC, 128, D], BF16)
    w1c = din("w1c", [NE, NT, 128, IFF], BF16)
    w3c = din("w3c", [NE, NT, 128, IFF], BF16)
    w2r = din("w2r", [NE, NIC, 128, D], BF16)
    c_ident = din("c_ident", [128, 128], BF16)
    c_cc = din("c_cc", [128, 2048], BF16)
    c_ss = din("c_ss", [128, 2048], BF16)
    c_ccq = din("c_ccq", [128, 1024], BF16)
    c_ssq = din("c_ssq", [128, 1024], BF16)
    c_pswap = din("c_pswap", [128, 128], BF16)
    c_causal = din("c_causal", [128, 256], BF16)
    c_band = din("c_band", [2, 128, 320], BF16)
    c_cmpm = din("c_cmpm", [NSLOT, 128, 32], F32)
    c_curadd = din("c_curadd", [NSLOT, 128, 32], F32)
    c_ones = din("c_ones", [128, 128], BF16)
    c_triu = din("c_triu", [128, 128], BF16)
    c_iota16 = din("c_iota16", [128, NE], F32)
    c_iotatok = din("c_iotatok", [128, NSLOT], F32)

    out_d = nc.dram_tensor("out", [1024, D], F32, kind="ExternalOutput").ap()

    x1_dram = nc.dram_tensor("x1_dram", [1024, D], BF16, kind="Internal").ap()
    y0_dram = nc.dram_tensor("y0_dram", [1024, D], F32, kind="Internal").ap()
    ys_dram = nc.dram_tensor("ys_dram", [NE * CAP, D], F32, kind="Internal").ap()
    idx_dram = nc.dram_tensor("idx_dram", [128 * 32, 1], I32, kind="Internal").ap()

    dbg_outs = {}
    if dbg:
        for name, shape, dt in [
            ("dbg_krot", [128, 2048], BF16), ("dbg_e", [128, 2048], BF16),
            ("dbg_pcomb", [128, 2048], BF16), ("dbg_oT", [128, 1024], BF16),
            ("dbg_x1", [128, D], BF16), ("dbg_scores", [128, NSLOT * NE], F32),
            ("dbg_rank", [128, NSLOT * NE], F32), ("dbg_s12", [128, 2 * NSLOT], F32),
            ("dbg_g", [128, 48], F32), ("dbg_idx", [128, 32], I32),
            ("dbg_y0", [128, D], F32), ("dbg_ysort", [128, D], F32),
        ]:
            dbg_outs[name] = nc.dram_tensor(name, shape, dt, kind="ExternalOutput").ap()

    es_root = ExitStack()
    with es_root:
        S = Sched(nc, es_root)
        sb = lambda name, shape, dt: es_root.enter_context(nc.sbuf_tensor(name, shape, dt))

        ident = sb("ident", [128, 128], BF16)
        pswap = sb("pswap", [128, 128], BF16)
        onesb = sb("onesb", [128, 128], BF16)
        triu = sb("triu", [128, 128], BF16)
        iota16 = sb("iota16", [128, NE], F32)
        iotatok = sb("iotatok", [128, NSLOT], F32)
        sm = sb("sm", [128, 64], F32)
        gA = sb("gA", [128, NSLOT * 8], F32)
        s1i = sb("s1i", [128, NSLOT], I32)
        s2i = sb("s2i", [128, NSLOT], I32)
        toki = sb("toki", [128, NSLOT], I32)
        iwrk = sb("iwrk", [128, 8], I32)
        posb = sb("posb", [128, 16], I32)
        idxsb = sb("idxsb", [128, 32], I32)

        psA = es_root.enter_context(nc.psum_tensor("psA", [128, 2048], F32))
        psS = es_root.enter_context(nc.psum_tensor("psS", [128, 1024], F32))
        psT = es_root.enter_context(nc.psum_tensor("psT", [128, 1024], BF16))
        psU = es_root.enter_context(nc.psum_tensor("psU", [128, 1024], BF16))

        trs = {"bank": 0, "free": [None, None], "last": [None, None]}

        def transpose_run(n, src_chunk_fn, dst_view_fn):
            j0 = 0
            while j0 < n:
                cnt = min(8, n - j0)
                bank = trs["bank"]
                trs["bank"] ^= 1
                ps_t = psT if bank == 0 else psU
                eng = "vector" if bank == 0 else "scalar"
                S.wt("tensor", trs["free"][bank])
                for g in range(cnt):
                    S.pe(lambda E, ps_t=ps_t, g=g, j=j0 + g: E.transpose(
                        ps_t[:, g * 128:(g + 1) * 128], src_chunk_fn(j), ident[:]))
                tkm = S.pe_mark()
                S.wt(eng, tkm)
                tev = S.op(eng, lambda E, ps_t=ps_t, cnt=cnt, j0=j0: _tcopy(
                    E, dst_view_fn(j0, cnt),
                    ps_t[:, 0:cnt * 128].rearrange("p (j n) -> p j n", n=128)))
                trs["free"][bank] = tev
                trs["last"][0 if eng == "vector" else 1] = tev
                j0 += cnt
            return trs["last"]

        def transpose_to(src_ap_fn, dst_ap_fn, rows=128):
            bank = trs["bank"]
            trs["bank"] ^= 1
            ps_t = psT if bank == 0 else psU
            eng = "vector" if bank == 0 else "scalar"
            S.wt("tensor", trs["free"][bank])
            S.pe(lambda E, ps_t=ps_t, rows=rows: E.transpose(
                ps_t[0:rows, 0:128], src_ap_fn(), ident[:]))
            tkm = S.pe_mark()
            S.wt(eng, tkm)
            tev = S.op(eng, lambda E, ps_t=ps_t, rows=rows, dst_ap_fn=dst_ap_fn: _tcopy(
                E, dst_ap_fn(), ps_t[0:rows, 0:128]))
            trs["free"][bank] = tev
            trs["last"][0 if eng == "vector" else 1] = tev
            return tev

        def tr_both():
            return (trs["last"][0], trs["last"][1])

        for nm, dst, src in [("ident", ident, c_ident), ("pswap", pswap, c_pswap),
                             ("ones", onesb, c_ones), ("triu", triu, c_triu),
                             ("iota16", iota16, c_iota16), ("iotatok", iotatok, c_iotatok)]:
            S.dma("sync", "cst", lambda E, dst=dst, src=src: E.dma_start(out=dst[:], in_=src))

        esA = ExitStack()
        sbA = lambda name, shape, dt: esA.enter_context(nc.sbuf_tensor(name, shape, dt))
        cc = sbA("cc", [128, 2048], BF16)
        ss = sbA("ss", [128, 2048], BF16)
        ccq = sbA("ccq", [128, 1024], BF16)
        ssq = sbA("ssq", [128, 1024], BF16)
        causal = sbA("causal", [128, 256], BF16)
        bandm = sbA("bandm", [128, 2 * 320], BF16)
        cmpm = sbA("cmpm", [128, NSLOT * 32], F32)
        curadd = sbA("curadd", [128, NSLOT * 32], F32)
        g_all = sbA("g_all", [128, NSLOT * 48], F32)
        xfm = sbA("xfm", [128, NT * 2048], BF16)
        xfo = sbA("xfo", [128, NT * 1024], BF16)
        wkb = sbA("wkb", [128, NT * 128], BF16)
        wqb = sbA("wqb", [128, NT * 128], BF16)
        wvb = sbA("wvb", [128, NT * 128], BF16)
        wgb = sbA("wgb", [128, NT * 48], BF16)
        kpre = sbA("kpre", [128, 2048], BF16)
        krot = sbA("krot", [128, 2048], BF16)
        vfm_sb = sbA("vfm_sb", [128, 2048], BF16)
        vtok = sbA("vtok", [128, 2048], BF16)
        qpre = sbA("qpre", [128, 1024], BF16)
        qrot = sbA("qrot", [128, 1024], BF16)
        kcmp = sbA("kcmp", [128, 32], BF16)
        cmpf = sbA("cmpf", [128, 32], F32)
        e_sb = sbA("e_sb", [128, 3840], BF16)
        ecmp = sbA("ecmp", [128, 32], F32)
        tpcmp = sbA("tpcmp", [128, 2 * 32], BF16)
        pcmpT = sbA("pcmpT", [128, 2 * 128], BF16)
        vcmpb = sbA("vcmpb", [128, 32], BF16)
        vcmpt = sbA("vcmpt", [128, 128], BF16)
        m8a = sbA("m8a", [128, 8], F32)
        selb = sbA("selb", [128, 32], F32)
        eswa = sbA("eswa", [128, 320], F32)
        pcomb = sbA("pcomb", [128, 4096], BF16)
        rtmp = sbA("rtmp", [128, 512], BF16)
        pT = sbA("pT", [128, NT * 256], BF16)
        esO = ExitStack()
        oT_all = esO.enter_context(nc.sbuf_tensor("oT_all", [128, NH * 1024], BF16, side="right"))

        for nm, dst, src in [("cc", cc, c_cc), ("ss", ss, c_ss), ("ccq", ccq, c_ccq),
                             ("ssq", ssq, c_ssq), ("causal", causal, c_causal)]:
            S.dma("sync", "cst", lambda E, dst=dst, src=src: E.dma_start(out=dst[:], in_=src))
        for v in range(2):
            S.dma("sync", "cst", lambda E, v=v: E.dma_start(
                out=bandm[:, v * 320:(v + 1) * 320], in_=c_band[v]))
        for kk in range(NSLOT):
            S.dma("sync", "cst", lambda E, kk=kk: E.dma_start(
                out=cmpm[:, kk * 32:(kk + 1) * 32], in_=c_cmpm[kk]))
            S.dma("sync", "cst", lambda E, kk=kk: E.dma_start(
                out=curadd[:, kk * 32:(kk + 1) * 32], in_=c_curadd[kk]))

        gx_tok = {}
        for c in range(NT):
            gx_tok[c] = S.dma("gpsimd", f"gx{c // 4}", lambda E, c=c: E.dma_start(
                out=xfm[:, c * 2048:(c + 1) * 2048], in_=xT[c * 128:(c + 1) * 128, :]))
        for c in range(NT):
            S.dma("gpsimd", "xfo", lambda E, c=c: E.dma_start(
                out=xfo[:, c * 1024:(c + 1) * 1024], in_=xT_own[c * 128:(c + 1) * 128, :]))
        for c in range(NT):
            S.dma("gpsimd", "gw", lambda E, c=c: E.dma_start(
                out=wgb[:, c * 48:(c + 1) * 48], in_=wg_in[c * 128:(c + 1) * 128, :]))

        for eng in ("tensor", "vector", "scalar", "gpsimd"):
            S.wait(eng, "cst")
        t_e_init = S.op("vector", lambda E: E.memset(e_sb[:], 0.0))
        t_p_init = S.op("vector", lambda E: E.memset(pcomb[:], 0.0))

        def load_w(nm, h, gate_tok):
            src, buf = {"wk": (wkh, wkb), "wq": (wqh, wqb), "wv": (wvh, wvb)}[nm]
            if gate_tok is not None:
                S.wt("gpsimd", gate_tok)
            return S.dma("gpsimd", nm, lambda E, src=src, buf=buf, h=h: E.dma_start(
                out=buf[:], in_=src[h].rearrange("p c m -> p (c m)")))

        hw_tok = {0: {nm: load_w(nm, 0, None) for nm in ("wk", "wq", "wv")}}

        for i in range(4):
            S.wait("tensor", f"gx{i}")
        S.wait("tensor", "xfo")
        S.wait("tensor", "gw")
        t_g_evac = None
        for kk in range(NSLOT):
            if t_g_evac is not None:
                S.wt("tensor", t_g_evac)
            for c in range(NT):
                S.pe(lambda E, kk=kk, c=c: E.matmul(
                    psS[:, 0:48],
                    xfo[:, c * 1024 + kk * 128: c * 1024 + (kk + 1) * 128],
                    wgb[:, c * 48:(c + 1) * 48],
                    start=(c == 0), stop=(c == NT - 1)))
            tpe = S.pe_mark()
            S.wt("scalar", tpe)
            t_g_evac = S.op("scalar", lambda E, kk=kk: E.activation(
                g_all[:, kk * 48:(kk + 1) * 48], psS[:, 0:48], AF.Sigmoid))
        if dbg:
            S.wt("sync", t_g_evac)
            S.dma("sync", "dbg", lambda E: E.dma_start(out=dbg_outs["dbg_g"], in_=g_all[:, 0:48]))

        t_oT_evac = [None] * NSLOT
        t_trA = [None] * 8
        t_trB = [None] * 8
        t_projA = t_g_evac
        t_sev = [None, None, None, None]
        t_scmp_ev = None
        t_kpre_used = None
        t_krot_used = None
        t_vfm_used2 = None
        t_qrot_used = None
        t_vtok_used = None
        t_e_used = [None, None]
        t_pcomb_used = [None, None]
        t_ecmp_used = [None, None]
        t_eswa_used = None
        t_bs_used = None
        t_sel_used = None
        t_imp_used = None
        t_w_cons = {}

        PS = {}

        def emit_kproj(h):
            st = PS.setdefault(h, {})
            S.wait("tensor", "wk", hw_tok[h]["wk"][1])
            S.wt("tensor", t_qrot_done_g[0])
            S.wt("tensor", t_kpre_used_g[0])
            S.wt("tensor", t_sev[2])
            S.wt("tensor", t_sev[3])
            kev = None
            for pair in range(2):
                for c in range(NT):
                    if h == 0 and c % 4 == 0:
                        S.wait("tensor", f"gx{c // 4}", gx_tok[c + 3][1])
                    for i in range(2):
                        tc = pair * 2 + i
                        S.pe(lambda E, c=c, tc=tc, i=i: E.matmul(
                            psA[:, i * 512:(i + 1) * 512],
                            wkb[:, c * 128:(c + 1) * 128],
                            xfm[:, c * 2048 + tc * 512: c * 2048 + (tc + 1) * 512],
                            start=(c == 0), stop=(c == NT - 1)))
                tpe = S.pe_mark()
                S.wt("scalar", tpe)
                for i in range(2):
                    tc = pair * 2 + i
                    kev = S.op("scalar", lambda E, tc=tc, i=i: E.copy(
                        kpre[:, tc * 512:(tc + 1) * 512], psA[:, i * 512:(i + 1) * 512]))
                S.wt("tensor", kev)
            if h + 1 < NH:
                hw_tok.setdefault(h + 1, {})["wk"] = load_w("wk", h + 1, tpe)
            st["kpre_ev"] = kev

        def emit_vproj(h):
            st = PS.setdefault(h, {})
            S.wait("tensor", "wv", hw_tok[h]["wv"][1])
            vev = None
            for pair in range(2):
                for c in range(NT):
                    for i in range(2):
                        tc = pair * 2 + i
                        S.pe(lambda E, c=c, tc=tc, i=i: E.matmul(
                            psA[:, i * 512:(i + 1) * 512],
                            wvb[:, c * 128:(c + 1) * 128],
                            xfm[:, c * 2048 + tc * 512: c * 2048 + (tc + 1) * 512],
                            start=(c == 0), stop=(c == NT - 1)))
                tpe = S.pe_mark()
                S.wt("scalar", tpe)
                for i in range(2):
                    tc = pair * 2 + i
                    vev = S.op("scalar", lambda E, tc=tc, i=i: E.copy(
                        vfm_sb[:, tc * 512:(tc + 1) * 512], psA[:, i * 512:(i + 1) * 512]))
                S.wt("tensor", vev)
            if h + 1 < NH:
                hw_tok[h + 1]["wv"] = load_w("wv", h + 1, tpe)
            st["vfm_ev"] = vev

        def emit_qproj(h):
            st = PS.setdefault(h, {})
            S.wait("tensor", "wq", hw_tok[h]["wq"][1])
            qev = None
            for c in range(NT):
                for i in range(2):
                    S.pe(lambda E, c=c, i=i: E.matmul(
                        psA[:, i * 512:(i + 1) * 512],
                        wqb[:, c * 128:(c + 1) * 128],
                        xfo[:, c * 1024 + i * 512: c * 1024 + (i + 1) * 512],
                        start=(c == 0), stop=(c == NT - 1)))
            tpe = S.pe_mark()
            if h + 1 < NH:
                hw_tok[h + 1]["wq"] = load_w("wq", h + 1, tpe)
            S.wt("scalar", tpe)
            for i in range(2):
                qev = S.op("scalar", lambda E, i=i: E.copy(
                    qpre[:, i * 512:(i + 1) * 512], psA[:, i * 512:(i + 1) * 512]))
            S.wt("tensor", qev)
            st["qev"] = qev

        t_qrot_done_g = [None]
        t_kpre_used_g = [None]

        for h in range(NH):
            if h == 0:
                emit_kproj(0)
                emit_vproj(0)
                emit_qproj(0)
            t_kpre_ev = PS[h]["kpre_ev"]
            t_vfm_ev = PS[h]["vfm_ev"]
            qev = PS[h]["qev"]

            S.wt("tensor", t_vfm_ev)
            S.wt("tensor", t_vtok_used)
            transpose_run(
                NT,
                lambda j: vfm_sb[:, j * 128:(j + 1) * 128],
                lambda j0, cnt: vtok[:].rearrange("p (j n) -> p j n", n=128)[:, j0:j0 + cnt, :])
            t_v_last2 = {"vector": trs["last"][0], "scalar": trs["last"][1]}
            S.wt("vector", t_vfm_ev)
            S.op("vector", lambda E: E.reduce_sum(
                cmpf[:], vfm_sb[:].rearrange("p (nb k) -> p nb k", nb=32), axis=AX.X))
            t_vcb = S.op("vector", lambda E: _tcopy(E, vcmpb[:], cmpf[:]))
            S.wt("tensor", t_vcb)
            t_vct = transpose_to(lambda: vcmpb[:], lambda: vcmpt[0:32, :], rows=32)

            for kq in range(NSLOT):
                S.wt("tensor", t_oT_evac[kq])
            S.wt("vector", t_kpre_ev)
            S.wt("vector", t_krot_used)
            S.op("vector", lambda E: E.tensor_mul(krot[:], kpre[:], cc[:]))
            t_rope_last = None
            t_r1k = [None] * 4
            swp_toks = []
            for tc in range(4):
                S.pe(lambda E, tc=tc: E.matmul(
                    psA[:, tc * 512:(tc + 1) * 512], pswap[:], kpre[:, tc * 512:(tc + 1) * 512],
                    start=True, stop=True))
                swp_toks.append(S.pe_mark())
            for tc in range(4):
                S.wt("vector", swp_toks[tc])
                t_r1k[tc] = S.op("vector", lambda E, tc=tc: E.tensor_mul(
                    rtmp[:], psA[:, tc * 512:(tc + 1) * 512], ss[:, tc * 512:(tc + 1) * 512]))
                t_rope_last = S.op("vector", lambda E, tc=tc: E.tensor_add(
                    krot[:, tc * 512:(tc + 1) * 512], krot[:, tc * 512:(tc + 1) * 512], rtmp[:]))
            t_krot_done = t_rope_last
            S.wt("vector", qev)
            S.op("vector", lambda E: E.tensor_mul(qrot[:], qpre[:], ccq[:]))
            swq_toks = []
            for i in range(2):
                S.wt("tensor", t_r1k[i])
                S.pe(lambda E, i=i: E.matmul(
                    psA[:, i * 512:(i + 1) * 512], pswap[:], qpre[:, i * 512:(i + 1) * 512],
                    start=True, stop=True))
                swq_toks.append(S.pe_mark())
            for i in range(2):
                S.wt("vector", swq_toks[i])
                t_r1 = S.op("vector", lambda E, i=i: E.tensor_mul(
                    rtmp[:], psA[:, i * 512:(i + 1) * 512], ssq[:, i * 512:(i + 1) * 512]))
                t_rope_last = S.op("vector", lambda E, i=i: E.tensor_add(
                    qrot[:, i * 512:(i + 1) * 512], qrot[:, i * 512:(i + 1) * 512], rtmp[:]))
                S.wt("tensor", t_r1)
            t_qrot_done = t_rope_last
            t_qrot_done_g[0] = t_rope_last
            t_projA = t_rope_last
            t_kpre_used = t_krot_done
            t_kpre_used_g[0] = t_krot_done

            S.wt("vector", t_krot_done)
            S.op("vector", lambda E: E.reduce_sum(
                cmpf[:], krot[:].rearrange("p (nb k) -> p nb k", nb=32), axis=AX.X))
            t_kcmp = S.op("vector", lambda E: _tcopy(E, kcmp[:], cmpf[:]))

            if dbg and h == 0:
                S.wt("sync", t_krot_done)
                S.dma("sync", "dbg", lambda E: E.dma_start(out=dbg_outs["dbg_krot"], in_=krot[:]))

            pending_tp = None
            pending_pv = None
            t_pv = None
            t_pv_pair_last = None
            for kk in range(NSLOT):
                par = kk % 2
                L = 256 * (kk + 1)
                nb = L // 64
                nchunks = ceil_div(L, 512)
                eS = e_sb[:, par * 1792: par * 1792 + L]
                pC = pcomb[:, par * 2048: par * 2048 + L + (256 if par == 0 else 0)]

                S.wt("tensor", t_kcmp)
                S.wt("tensor", t_qrot_done)
                S.wt("tensor", t_scmp_ev)
                for kq in range(4, NSLOT):
                    S.wt("tensor", t_oT_evac[kq])
                S.pe(lambda E, kk=kk: E.matmul(
                    psA[:, 2016:2048], qrot[:, kk * 128:(kk + 1) * 128], kcmp[:],
                    start=True, stop=True))
                t_scmp = S.pe_mark()

                t_ex_last = None
                for sc in range(nchunks):
                    N = min(512, L - sc * 512)
                    bank = (sc % 4) if kk < 4 else (sc % 2)
                    ps_b = psS if bank < 2 else psA
                    boff = (bank % 2) * 512
                    S.wt("tensor", t_sev[bank])
                    S.pe(lambda E, kk=kk, sc=sc, N=N, ps_b=ps_b, boff=boff: E.matmul(
                        ps_b[:, boff: boff + N],
                        qrot[:, kk * 128:(kk + 1) * 128],
                        krot[:, sc * 512: sc * 512 + N],
                        start=True, stop=True))
                    tpe = S.pe_mark()
                    S.wt("scalar", tpe)
                    S.wt("scalar", t_e_used[par])
                    tex = S.op("scalar", lambda E, eS=eS, sc=sc, N=N, ps_b=ps_b, boff=boff: E.activation(
                        eS[:, sc * 512: sc * 512 + N], ps_b[:, boff: boff + N],
                        AF.Exp, scale=SCALE))
                    t_sev[bank] = tex
                    t_ex_last = tex

                if pending_tp is not None:
                    trtoks = pending_tp()
                    pending_tp = None
                    if pending_pv is not None:
                        t_pv = pending_pv(trtoks)
                        t_pv_pair_last = t_pv
                        pending_pv = None
                if h + 1 < NH:
                    if kk == 4:
                        emit_kproj(h + 1)
                    elif kk == 5:
                        emit_vproj(h + 1)
                    elif kk == 6:
                        emit_qproj(h + 1)

                S.wt("scalar", t_scmp)
                S.wt("scalar", t_ecmp_used[par])
                t_ec0 = S.op("scalar", lambda E: E.activation(
                    sm[:, 32:64], psA[:, 2016:2048], AF.Exp, scale=SCALE / 64.0))
                t_scmp_ev = t_ec0
                for bank in range(min(nchunks, 4)):
                    S.wt("vector", t_sev[bank])
                S.wt("vector", t_e_used[par])
                t_diag = S.op("vector", lambda E, eS=eS, L=L: E.tensor_mul(
                    eS[:, L - 256:L], eS[:, L - 256:L], causal[:]))
                S.wt("vector", t_ec0)
                t_ecm = S.op("vector", lambda E, kk=kk: E.scalar_tensor_tensor(
                    ecmp[:], sm[:, 32:64], 1.0, cmpm[:, kk * 32:(kk + 1) * 32],
                    ALU.mult, ALU.mult, accum_out=sm[:, 0:1]))
                t_imp = S.op("vector", lambda E, kk=kk: E.tensor_add(
                    eswa[:, 0:32], ecmp[:], curadd[:, kk * 32:(kk + 1) * 32]))
                S.op("vector", lambda E: E.max(m8a[:], eswa[:, 0:32]))
                t_sel = S.op("vector", lambda E: E.tensor_single_scalar(
                    selb[:], eswa[:, 0:32], m8a[:, 7:8], ALU.is_ge))
                t_imp_used = t_sel
                S.wt("tensor", t_imp)
                S.pe(lambda E: E.matmul(
                    psA[:, 2016:2032], ident[:], kcmp[:, 0:16], start=True, stop=True))
                t_warm = S.pe_mark()

                if t_pcomb_used[par] is not None:
                    S.wt("vector", t_pcomb_used[par][0])
                    S.wt("vector", t_pcomb_used[par][1])
                S.wt("vector", t_diag)
                tA = S.op("vector", lambda E, eS=eS, pC=pC, L=L, nb=nb: E.scalar_tensor_tensor(
                    pC[:, 0:L], eS[:, 0:L], 1.0,
                    selb[:, 0:nb].rearrange("p n -> p n ()").broadcast_to([128, nb, 64]),
                    ALU.mult, ALU.mult, accum_out=sm[:, 1:2]))
                t_e_used[par] = tA
                bs0 = 0 if kk == 0 else L - 320
                bv = 0 if kk == 0 else 1
                S.wt("vector", t_eswa_used)
                t_esw = S.op("vector", lambda E, par=par, bs0=bs0, bv=bv: E.scalar_tensor_tensor(
                    eswa[:], e_sb[:, par * 1792 + bs0: par * 1792 + bs0 + 320],
                    1.0, bandm[:, bv * 320:(bv + 1) * 320],
                    ALU.mult, ALU.mult, accum_out=sm[:, 2:3]))
                S.op("vector", lambda E: E.tensor_scalar_add(sm[:, 4:7], sm[:, 0:3], 1e-30))
                S.op("vector", lambda E: E.reciprocal(sm[:, 12:15], sm[:, 4:7]))
                t_rr = S.op("vector", lambda E, kk=kk, h=h: E.tensor_tensor(
                    sm[:, 8:11],
                    g_all[:, kk * 48: kk * 48 + 48].rearrange("p (a b) -> p a b", b=16)[:, :, h:h + 1],
                    sm[:, 12:15], ALU.mult))
                tB = S.op("vector", lambda E, pC=pC, L=L: E.tensor_scalar_mul(
                    pC[:, 0:L], pC[:, 0:L], sm[:, 9:10]))
                S.wt("tensor", tA)
                S.pe(lambda E: E.matmul(
                    psA[:, 2016:2032], ident[:], kcmp[:, 0:16], start=True, stop=True))
                S.pe_mark()
                t4 = S.op("vector", lambda E, par=par, bs0=bs0: E.scalar_tensor_tensor(
                    pcomb[:, par * 2048 + bs0: par * 2048 + bs0 + 320], eswa[:], sm[:, 10:11],
                    pcomb[:, par * 2048 + bs0: par * 2048 + bs0 + 320],
                    ALU.mult, ALU.add))
                if par == 0:
                    S.wt("gpsimd", t4)
                    if t_pcomb_used[par] is not None:
                        S.wt("gpsimd", t_pcomb_used[par][0])
                    t4 = S.op("gpsimd", lambda E, par=par, L=L: E.memset(
                        pcomb[:, par * 2048 + L: par * 2048 + L + 256], 0.0))
                t_eswa_used = t4
                t_pre = S.op("vector", lambda E, kk=kk, par=par: E.tensor_scalar(
                    tpcmp[:, par * 32:(par + 1) * 32], ecmp[:], sm[:, 8:9], 1.0 / 64.0,
                    ALU.mult, ALU.mult))
                t_ecmp_used[par] = t_pre
                t_pcomb_ready = t4
                t_tpcmp_ready = t_pre

                if dbg and h == 0 and kk == 2:
                    S.wt("sync", t4)
                    S.dma("sync", "dbg", lambda E, eS=eS: E.dma_start(out=dbg_outs["dbg_e"], in_=eS))
                    S.dma("sync", "dbg", lambda E, pC=pC: E.dma_start(out=dbg_outs["dbg_pcomb"], in_=pC))

                def make_tp(kk=kk, par=par, L=L, pC=pC, t_pcomb_ready=t_pcomb_ready,
                            t_tpcmp_ready=t_tpcmp_ready, t_vct=t_vct, h=h):
                    def emit():
                        S.wt("tensor", t_pcomb_ready)
                        S.wt("tensor", t_pv_pair_last)
                        nch = L // 128 + (2 if par == 0 else 0)
                        transpose_run(
                            nch,
                            lambda j: pC[:, j * 128:(j + 1) * 128],
                            lambda j0, cnt, par=par: pT[:].rearrange(
                                "p (j two n) -> p j two n", two=2, n=128)[:, j0:j0 + cnt, par, :])
                        S.wt("tensor", t_tpcmp_ready)
                        t_pct = transpose_to(
                            lambda: tpcmp[:, par * 32:(par + 1) * 32],
                            lambda: pcmpT[0:32, par * 128:(par + 1) * 128], rows=32)
                        t_tr_last2 = {"vector": trs["last"][0], "scalar": trs["last"][1]}
                        t_pcomb_used[par] = (t_tr_last2["vector"], t_tr_last2["scalar"])
                        return (t_tr_last2, t_pct)
                    return emit

                def make_pv(kk=kk, L=L, h=h):
                    def emit(trtoks):
                        t_tr_last2, t_pct = trtoks
                        pair = kk // 2
                        S.wt("tensor", t_tr_last2["vector"])
                        S.wt("tensor", t_tr_last2["scalar"])
                        S.wt("tensor", t_pct)
                        S.wt("tensor", t_vct)
                        for kq in range(4 * (kk // 4), 4 * (kk // 4) + 4):
                            S.wt("tensor", t_oT_evac[kq])
                        for jc in range(L // 128):
                            S.pe(lambda E, jc=jc, pair=pair: E.matmul(
                                psA[:, 1024 + pair * 256: 1024 + (pair + 1) * 256],
                                vtok[:, jc * 128:(jc + 1) * 128],
                                pT[:, jc * 256:(jc + 1) * 256],
                                start=(jc == 0), stop=False))
                        S.pe(lambda E, pair=pair: E.matmul(
                            psA[:, 1024 + pair * 256: 1024 + pair * 256 + 128],
                            vcmpt[0:32, :], pcmpT[0:32, 0:128], start=False, stop=False))
                        S.pe(lambda E, pair=pair: E.matmul(
                            psA[:, 1024 + pair * 256 + 128: 1024 + (pair + 1) * 256],
                            vcmpt[0:32, :], pcmpT[0:32, 128:256], start=False, stop=True))
                        t_pv_ = S.pe_mark()
                        eng = "vector" if kk < 4 else "scalar"
                        S.wt(eng, t_pv_)
                        tev = S.op(eng, lambda E, kk=kk, h=h, pair=pair: _tcopy(E, 
                            oT_all[:, h * 1024 + pair * 256: h * 1024 + (pair + 1) * 256],
                            psA[:, 1024 + pair * 256: 1024 + (pair + 1) * 256]))
                        t_oT_evac[kk - 1] = tev
                        t_oT_evac[kk] = tev
                        return t_pv_
                    return emit

                pending_tp = make_tp()
                if par == 1:
                    pending_pv = make_pv()
            trtoks = pending_tp()
            t_pv = pending_pv(trtoks)
            pending_tp = None
            pending_pv = None
            t_vtok_used = t_pv
            t_krot_used = t_pv
            t_qrot_used = t_pv

            if dbg and h == 0:
                S.wt("sync", t_oT_evac[NSLOT - 1])
                S.dma("sync", "dbg", lambda E: E.dma_start(
                    out=dbg_outs["dbg_oT"], in_=oT_all[:, 0:1024]))

        S.barrier()
        esA.close()

        esB1 = ExitStack()
        esB2 = ExitStack()
        sbB = lambda name, shape, dt: esB1.enter_context(nc.sbuf_tensor(name, shape, dt, side="right"))
        wo_sb = sbB("wo_sb", [128, NT * 2048], BF16)
        xo_sb = sbB("xo_sb", [128, 2 * 2048], F32)
        x1pre = sbB("x1pre", [128, 2048], F32)
        x1fl = sbB("x1fl", [128, 2048], F32)
        sqscr = sbB("sqscr", [128, 2048], BF16)
        x1bf = esB2.enter_context(nc.sbuf_tensor("x1bf", [128, NSLOT * 2048], BF16))
        x1lo = esB2.enter_context(nc.sbuf_tensor("x1lo", [128, NSLOT * 2048], BF16))

        for c in range(NT):
            S.dma("gpsimd", "gwo", lambda E, c=c: E.dma_start(
                out=wo_sb[:, c * 2048:(c + 1) * 2048], in_=wo_in[c * 128:(c + 1) * 128, :]))
        xo_tok = [None, None]
        xo_cons = [None, None]
        for kk in range(2):
            xo_tok[kk] = S.dma("sync", f"xo{kk}", lambda E, kk=kk: E.dma_start(
                out=xo_sb[:, kk * 2048:(kk + 1) * 2048], in_=x_own[kk * 128:(kk + 1) * 128, :]))

        S.wait("tensor", "gwo")
        t_op_evac = None
        t_x1_stores = []
        for kk in range(NSLOT):
            par = kk % 2
            S.wt("tensor", t_op_evac)
            for hh in range(NH):
                for n in range(4):
                    S.pe(lambda E, kk=kk, hh=hh, n=n: E.matmul(
                        psA[:, n * 512:(n + 1) * 512],
                        oT_all[:, hh * 1024 + kk * 128: hh * 1024 + (kk + 1) * 128],
                        wo_sb[:, hh * 2048 + n * 512: hh * 2048 + (n + 1) * 512],
                        start=(hh == 0), stop=(hh == NH - 1)))
            t_mm = S.pe_mark()
            S.wt("vector", t_mm)
            S.wait("vector", f"xo{par}", xo_tok[par][1])
            t_res = S.op("vector", lambda E, par=par: E.tensor_add(
                x1pre[:], psA[:, 0:2048], xo_sb[:, par * 2048:(par + 1) * 2048]))
            t_op_evac = t_res
            xo_cons[par] = t_res
            if kk + 2 < NSLOT:
                S.wt("sync", t_res)
                xo_tok[par] = S.dma("sync", f"xo{par}", lambda E, kk=kk: E.dma_start(
                    out=xo_sb[:, (kk % 2) * 2048:(kk % 2 + 1) * 2048],
                    in_=x_own[(kk + 2) * 128:(kk + 3) * 128, :]))
            S.wt("scalar", t_res)
            t_sq = S.op("scalar", lambda E: E.activation(
                sqscr[:], x1pre[:], AF.Square, accum_out=sm[:, 16:17]))
            t_mean = S.op("vector", lambda E: E.reduce_sum(sm[:, 17:18], x1pre[:], axis=AX.X))
            S.op("vector", lambda E: E.tensor_scalar_mul(sm[:, 18:19], sm[:, 17:18], 1.0 / D))
            S.wt("vector", t_sq)
            S.op("vector", lambda E: E.tensor_scalar_mul(sm[:, 19:20], sm[:, 16:17], 1.0 / D))
            S.op("vector", lambda E: E.tensor_mul(sm[:, 20:21], sm[:, 18:19], sm[:, 18:19]))
            S.op("vector", lambda E: E.tensor_sub(sm[:, 21:22], sm[:, 19:20], sm[:, 20:21]))
            t_var = S.op("vector", lambda E: E.tensor_scalar_add(sm[:, 22:23], sm[:, 21:22], EPS))
            S.wt("scalar", t_var)
            t_std = S.op("scalar", lambda E: E.activation(sm[:, 23:24], sm[:, 22:23], AF.Sqrt))
            S.wt("vector", t_std)
            t_rstd = S.op("vector", lambda E: E.reciprocal(sm[:, 24:25], sm[:, 23:24]))
            t_x1 = S.op("vector", lambda E: E.tensor_scalar(
                x1fl[:], x1pre[:], sm[:, 18:19], sm[:, 24:25], ALU.subtract, ALU.mult))
            t_hi = S.op("vector", lambda E, kk=kk: _tcopy(E, 
                x1bf[:, kk * 2048:(kk + 1) * 2048], x1fl[:]))
            t_lo = S.op("vector", lambda E, kk=kk: E.tensor_sub(
                x1lo[:, kk * 2048:(kk + 1) * 2048], x1fl[:], x1bf[:, kk * 2048:(kk + 1) * 2048]))
            S.wt("gpsimd", t_hi)
            t_x1_stores.append(S.dma("gpsimd", "x1s", lambda E, kk=kk: E.dma_start(
                out=x1_dram[kk * 128:(kk + 1) * 128, :], in_=x1bf[:, kk * 2048:(kk + 1) * 2048])))
            if dbg and kk == 0:
                S.wt("sync", t_hi)
                S.dma("sync", "dbg", lambda E: E.dma_start(
                    out=dbg_outs["dbg_x1"], in_=x1bf[:, 0:2048]))

        S.barrier()
        esB1.close()
        esO.close()

        esF = ExitStack()
        esF2 = ExitStack()
        sbF = lambda name, shape, dt: esF2.enter_context(nc.sbuf_tensor(name, shape, dt, side="right"))
        x1fm = esF.enter_context(nc.sbuf_tensor("x1fm", [128, NT * 1024], BF16))
        x1lofm = sbF("x1lofm", [128, NT * 1024], BF16)
        ghi = sbF("ghi", [128, NT * NE], BF16)
        glo = sbF("glo", [128, NT * NE], BF16)
        gfl = sbF("gfl", [128, NT * NE], F32)
        scores = sbF("scores", [128, NSLOT * NE], F32)
        m8g = sbF("m8g", [128, NSLOT * 8], F32)
        mig = sbF("mig", [128, NSLOT * 8], U32)
        sel2 = sbF("sel2", [128, NSLOT * NE], BF16)
        rank_all = sbF("rank_all", [128, NSLOT * NE], F32)

        for src_sb, dst_sb in ((x1bf, x1fm), (x1lo, x1lofm)):
            for kk in range(NSLOT):
                transpose_run(
                    NT,
                    lambda j, src_sb=src_sb, kk=kk: src_sb[:, kk * 2048 + j * 128: kk * 2048 + (j + 1) * 128],
                    lambda j0, cnt, dst_sb=dst_sb, kk=kk: dst_sb[:].rearrange(
                        "p (j n) -> p j n", n=1024)[:, j0:j0 + cnt, kk * 128:(kk + 1) * 128])
        t_fm_done2 = {"vector": trs["last"][0], "scalar": trs["last"][1]}

        for c in range(NT):
            S.dma("sync", "cst", lambda E, c=c: E.dma_start(
                out=gfl[:, c * NE:(c + 1) * NE], in_=gateT[c * 128:(c + 1) * 128, :]))
        S.wait("vector", "cst")
        t_ghi = S.op("vector", lambda E: _tcopy(E, ghi[:], gfl[:]))
        t_glo = S.op("vector", lambda E: E.tensor_sub(glo[:], gfl[:], ghi[:]))

        S.wt("tensor", t_fm_done2["vector"])
        S.wt("tensor", t_fm_done2["scalar"])
        S.wt("tensor", t_glo)
        t_sc_ev = None
        gate_steps = [(0, c) for c in range(NT)] + [(1, c) for c in range(NT)] + [(2, c) for c in range(NT)]
        for kk in range(NSLOT):
            S.wt("tensor", t_sc_ev)
            for idx, (p_, c) in enumerate(gate_steps):
                xa = x1fm if p_ in (0, 2) else x1lofm
                ga = ghi if p_ in (0, 1) else glo
                S.pe(lambda E, xa=xa, ga=ga, kk=kk, c=c, idx=idx: E.matmul(
                    psS[:, 0:NE],
                    xa[:, c * 1024 + kk * 128: c * 1024 + (kk + 1) * 128],
                    ga[:, c * NE:(c + 1) * NE],
                    start=(idx == 0), stop=(idx == len(gate_steps) - 1)))
            tpe = S.pe_mark()
            S.wt("scalar", tpe)
            t_sc_ev = S.op("scalar", lambda E, kk=kk: E.activation(
                scores[:, kk * NE:(kk + 1) * NE], psS[:, 0:NE], AF.Sigmoid))

        t_sel2_done = None
        S.wt("vector", t_sc_ev)
        for kk in range(NSLOT):
            S.op("vector", lambda E, kk=kk: E.max(
                m8g[:, kk * 8:(kk + 1) * 8], scores[:, kk * NE:(kk + 1) * NE]))
            S.op("vector", lambda E, kk=kk: E.max_index(
                mig[:, kk * 8:(kk + 1) * 8], m8g[:, kk * 8:(kk + 1) * 8],
                scores[:, kk * NE:(kk + 1) * NE]))
            t_sel2_done = S.op("vector", lambda E, kk=kk: E.tensor_single_scalar(
                sel2[:, kk * NE:(kk + 1) * NE], scores[:, kk * NE:(kk + 1) * NE],
                m8g[:, kk * 8 + 1:kk * 8 + 2], ALU.is_ge))
            S.op("vector", lambda E, kk=kk: _tcopy(E, 
                gA[:, kk * 8 + 0:kk * 8 + 1], mig[:, kk * 8 + 0:kk * 8 + 1]))
            S.op("vector", lambda E, kk=kk: _tcopy(E, 
                gA[:, kk * 8 + 1:kk * 8 + 2], mig[:, kk * 8 + 1:kk * 8 + 2]))
            S.op("vector", lambda E, kk=kk: E.tensor_add(
                sm[:, 30:31], m8g[:, kk * 8 + 0:kk * 8 + 1], m8g[:, kk * 8 + 1:kk * 8 + 2]))
            S.op("vector", lambda E: E.reciprocal(sm[:, 31:32], sm[:, 30:31]))
            S.op("vector", lambda E, kk=kk: E.tensor_mul(
                gA[:, kk * 8 + 4:kk * 8 + 5], m8g[:, kk * 8 + 0:kk * 8 + 1], sm[:, 31:32]))
            S.op("vector", lambda E, kk=kk: E.tensor_mul(
                gA[:, kk * 8 + 5:kk * 8 + 6], m8g[:, kk * 8 + 1:kk * 8 + 2], sm[:, 31:32]))
        if dbg:
            S.wt("sync", t_sel2_done)
            S.dma("sync", "dbg", lambda E: E.dma_start(out=dbg_outs["dbg_scores"], in_=scores[:]))

        S.wt("tensor", t_sel2_done)
        t_rank_ev = None
        for kk in range(NSLOT):
            S.wt("tensor", t_rank_ev)
            for j in range(kk + 1):
                S.pe(lambda E, kk=kk, j=j: E.matmul(
                    psS[:, 0:NE],
                    (triu if j == kk else onesb)[:],
                    sel2[:, j * NE:(j + 1) * NE],
                    start=(j == 0), stop=(j == kk)))
            tpe = S.pe_mark()
            S.wt("vector", tpe)
            t_rank_ev = S.op("vector", lambda E, kk=kk: _tcopy(E, 
                rank_all[:, kk * NE:(kk + 1) * NE], psS[:, 0:NE]))
        if dbg:
            S.wt("sync", t_rank_ev)
            S.dma("sync", "dbg", lambda E: E.dma_start(out=dbg_outs["dbg_rank"], in_=rank_all[:]))

        t_toki = S.op("vector", lambda E: _tcopy(E, toki[:], iotatok[:]))
        zbuf = sbF("zbuf", [128, 32], I32)
        t_z = S.op("vector", lambda E: E.memset(zbuf[:], 0))
        S.wt("sync", t_z)
        t_idz = S.dma("sync", "idz", lambda E: E.dma_start(
            out=idx_dram.rearrange("(p k) o -> p (k o)", p=128), in_=zbuf[:]))
        ohb = sbF("ohb", [128, NE], F32)
        t_scat_last = None
        for kk in range(NSLOT):
            for cpy in range(2):
                col = kk * 8 + cpy
                rcol = kk * 8 + 2 + cpy
                S.op("vector", lambda E, col=col: E.tensor_single_scalar(
                    ohb[:], iota16[:], gA[:, col:col + 1], ALU.is_equal))
                S.op("vector", lambda E, kk=kk, rcol=rcol: E.scalar_tensor_tensor(
                    ohb[:], rank_all[:, kk * NE:(kk + 1) * NE], 1.0, ohb[:],
                    ALU.mult, ALU.mult, accum_out=gA[:, rcol:rcol + 1]))
                S.op("vector", lambda E, rcol=rcol: E.tensor_scalar_min(
                    gA[:, rcol:rcol + 1], gA[:, rcol:rcol + 1], float(CAP - 1)))
                S.op("vector", lambda E, col=col, rcol=rcol: E.tensor_scalar(
                    sm[:, 33:34], gA[:, col:col + 1], float(CAP), 1.0, ALU.mult, ALU.mult))
                t_s = S.op("vector", lambda E, rcol=rcol: E.tensor_add(
                    sm[:, 34:35], sm[:, 33:34], gA[:, rcol:rcol + 1]))
                dst_i = s1i if cpy == 0 else s2i
                t_si = S.op("vector", lambda E, dst_i=dst_i, kk=kk: _tcopy(E, 
                    dst_i[:, kk:kk + 1], sm[:, 34:35]))
                S.op("vector", lambda E, dst_i=dst_i, kk=kk: E.tensor_single_scalar(
                    iwrk[:, 0:1], dst_i[:, kk:kk + 1], 127, ALU.bitwise_and))
                S.op("vector", lambda E: E.tensor_single_scalar(
                    iwrk[:, 1:2], iwrk[:, 0:1], 5, ALU.logical_shift_left))
                S.op("vector", lambda E, dst_i=dst_i, kk=kk: E.tensor_single_scalar(
                    iwrk[:, 2:3], dst_i[:, kk:kk + 1], 7, ALU.arith_shift_right))
                pcol = kk * 2 + cpy
                t_pos = S.op("vector", lambda E, pcol=pcol: E.tensor_tensor(
                    posb[:, pcol:pcol + 1], iwrk[:, 1:2], iwrk[:, 2:3], ALU.add))
                S.wt("gpsimd", t_pos)
                S.wt("gpsimd", t_toki)
                S.wait("gpsimd", "idz", t_idz[1])
                t_scat_last = S.dma("gpsimd", "isc", lambda E, kk=kk, pcol=pcol: E.indirect_dma_start(
                    idx_dram[:], IndirectOffsetOnAxis(posb[:, pcol:pcol + 1], 0),
                    toki[:, kk:kk + 1], None))
        S.wait("gpsimd", "isc")
        t_ilb = S.dma("gpsimd", "ilb", lambda E: E.dma_start(
            out=idxsb[:], in_=idx_dram.rearrange("(p k) o -> p (k o)", p=128)))
        if dbg:
            S.wait("sync", "ilb")
            S.dma("sync", "dbg", lambda E: E.dma_start(out=dbg_outs["dbg_idx"], in_=idxsb[:]))

        S.barrier()
        esF2.close()
        esS1 = ExitStack()
        rbuf = esF.enter_context(nc.sbuf_tensor("rbuf", [128, 8 * 2048], BF16))
        ring_cons = [None] * 8
        ring_n = [0]

        def ring_load(src_ap):
            g = ring_n[0]
            ring_n[0] += 1
            slot = g % 8
            if ring_cons[slot] is not None:
                S.wt("sync", ring_cons[slot])
            tok = S.dma("sync", f"ring{slot}", lambda E, src_ap=src_ap, slot=slot: E.dma_start(
                out=rbuf[:, slot * 2048:(slot + 1) * 2048], in_=src_ap))
            return slot, tok

        h_sh = esS1.enter_context(nc.sbuf_tensor("h_sh", [128, NSC * 1024], BF16, side="right"))
        sa_sh = esS1.enter_context(nc.sbuf_tensor("sa_sh", [128, 1024], BF16, side="right"))

        sh_chunks = []
        for j in range(NSC):
            sh_chunks.append(sw1c[j])
            sh_chunks.append(sw3c[j])
        sh_slots = [None] * len(sh_chunks)
        loaded_n = [0]

        def ensure_loaded(upto):
            while loaded_n[0] <= upto:
                g = loaded_n[0]
                sh_slots[g] = ring_load(sh_chunks[g])
                loaded_n[0] += 1

        ensure_loaded(5)
        S.wt("tensor", t_fm_done2["vector"])
        S.wt("tensor", t_fm_done2["scalar"])
        t_psS_u = None
        t_psA_u = None
        t_sa_u = None
        for j in range(NSC):
            ensure_loaded(min(2 * j + 7, len(sh_chunks) - 1))
            sl1, tk1 = sh_slots[2 * j]
            sl3, tk3 = sh_slots[2 * j + 1]
            S.wait("tensor", tk1[0], tk1[1])
            S.wt("tensor", t_psS_u)
            for c in range(NT):
                for i in range(2):
                    S.pe(lambda E, sl1=sl1, c=c, i=i: E.matmul(
                        psS[:, i * 512:(i + 1) * 512],
                        rbuf[:, sl1 * 2048 + c * 128: sl1 * 2048 + (c + 1) * 128],
                        x1fm[:, c * 1024 + i * 512: c * 1024 + (i + 1) * 512],
                        start=(c == 0), stop=(c == NT - 1)))
            t_a = S.pe_mark()
            ring_cons[sl1] = t_a
            S.wait("tensor", tk3[0], tk3[1])
            S.wt("tensor", t_psA_u)
            for c in range(NT):
                for i in range(2):
                    S.pe(lambda E, sl3=sl3, c=c, i=i: E.matmul(
                        psA[:, i * 512:(i + 1) * 512],
                        rbuf[:, sl3 * 2048 + c * 128: sl3 * 2048 + (c + 1) * 128],
                        x1fm[:, c * 1024 + i * 512: c * 1024 + (i + 1) * 512],
                        start=(c == 0), stop=(c == NT - 1)))
            t_b = S.pe_mark()
            ring_cons[sl3] = t_b
            S.wt("scalar", t_a)
            S.wt("scalar", t_sa_u)
            t_sa = S.op("scalar", lambda E: E.activation(sa_sh[:], psS[:, 0:1024], AF.Silu))
            t_psS_u = t_sa
            S.wt("vector", t_b)
            S.wt("vector", t_sa)
            t_h = S.op("vector", lambda E, j=j: E.tensor_mul(
                h_sh[:, j * 1024:(j + 1) * 1024], psA[:, 0:1024], sa_sh[:]))
            t_psA_u = t_h
            t_sa_u = t_h

        S.barrier()
        esF.close()
        esB2.close()

        esS2 = ExitStack()
        s2w = esS2.enter_context(nc.sbuf_tensor("s2w", [128, NSC * 2048], BF16))
        y0b = esS2.enter_context(nc.sbuf_tensor("y0b", [128, 2 * 2048], F32))
        for j in range(NSC):
            S.dma("sync", "s2w", lambda E, j=j: E.dma_start(
                out=s2w[:, j * 2048:(j + 1) * 2048], in_=sw2r[j]))
        S.wait("tensor", "s2w")
        t_y0_ev = [None, None]
        t_y0_st = [None, None]
        for tt in range(NSLOT):
            pb = tt % 2
            S.wt("tensor", t_psA_u)
            S.wt("tensor", t_y0_ev[pb])
            for j in range(NSC):
                for n in range(4):
                    S.pe(lambda E, j=j, tt=tt, n=n: E.matmul(
                        psA[:, n * 512:(n + 1) * 512],
                        h_sh[:, j * 1024 + tt * 128: j * 1024 + (tt + 1) * 128],
                        s2w[:, j * 2048 + n * 512: j * 2048 + (n + 1) * 512],
                        start=(j == 0), stop=(j == NSC - 1)))
            t_mm = S.pe_mark()
            t_psA_u = t_mm
            S.wt("vector", t_mm)
            if t_y0_st[pb] is not None:
                S.wait("vector", f"y0s{pb}", t_y0_st[pb][1])
            t_ev = S.op("vector", lambda E, pb=pb: _tcopy(E, 
                y0b[:, pb * 2048:(pb + 1) * 2048], psA[:, 0:2048]))
            t_y0_ev[pb] = t_ev
            t_psA_u = t_ev
            S.wt("gpsimd", t_ev)
            t_y0_st[pb] = S.dma("gpsimd", f"y0s{pb}", lambda E, tt=tt, pb=pb: E.dma_start(
                out=y0_dram[tt * 128:(tt + 1) * 128, :], in_=y0b[:, pb * 2048:(pb + 1) * 2048]))
        if dbg:
            S.wait("sync", "y0s0")
            S.wait("sync", "y0s1")
            S.dma("sync", "dbg", lambda E: E.dma_start(
                out=dbg_outs["dbg_y0"], in_=y0b[:, (NSLOT - 1) % 2 * 2048:((NSLOT - 1) % 2 + 1) * 2048]))

        S.barrier()
        esS1.close()
        esS2.close()

        esE = ExitStack()
        sbE = lambda name, shape, dt: esE.enter_context(nc.sbuf_tensor(name, shape, dt))
        rbuf2 = sbE("rbuf2", [128, 32 * 2048], BF16)
        xg = sbE("xg", [128, 2 * 2048], BF16)
        xgfm = sbE("xgfm", [128, NT * 256], BF16)
        sa_e = sbE("sa_e", [128, 2 * IFF], BF16)
        h_tok = sbE("h_tok", [128, 2 * IFF], BF16)
        h_ex = sbE("h_ex", [128, NIC * 256], BF16)
        ysb = sbE("ysb", [128, 2 * 2048], F32)

        ring2_cons = [None] * 32
        ring2_n = [0]

        def ring2_load(src_ap):
            g = ring2_n[0]
            ring2_n[0] += 1
            slot = g % 32
            w = src_ap.shape[-1]
            if ring2_cons[slot] is not None:
                S.wt("sync", ring2_cons[slot])
            tok = S.dma("sync", f"rg{slot}", lambda E, src_ap=src_ap, slot=slot, w=w: E.dma_start(
                out=rbuf2[:, slot * 2048:slot * 2048 + w], in_=src_ap))
            return slot, tok

        EXC = 2 * NT + NIC
        ex_chunks = []
        for e in range(NE):
            for c in range(NT):
                ex_chunks.append(w1c[e, c])
            for c in range(NT):
                ex_chunks.append(w3c[e, c])
            for j in range(NIC):
                ex_chunks.append(w2r[e, j])
        ex_slots = [None] * len(ex_chunks)
        ex_loaded = [0]

        def ex_ensure(upto):
            while ex_loaded[0] <= upto:
                g = ex_loaded[0]
                ex_slots[g] = ring2_load(ex_chunks[g])
                ex_loaded[0] += 1

        S.wait("gpsimd", "x1s")
        S.wait("gpsimd", "ilb", t_ilb[1])
        t_xg_cons = [None, None]
        t_xgfm_cons = None
        t_s1_free = None
        t_sa_u = None
        t_htok_u = None
        t_hex_w = None
        t_s2_banks = None
        t_ys_st = [None, None]
        t_ys_ev = [None, None]
        NCH = [512, 512, IFF - 1024]

        def s1_region(n):
            if n < 2:
                return psS[:, n * 512: n * 512 + NCH[n]]
            return psA[:, 0:NCH[2]]

        pending_s2 = []
        xg_pref = [None, None]

        for e in range(NE):
            ex_ensure(min(EXC * e + 15, len(ex_chunks) - 1))
            for te in range(2):
                kt = 2 * e + te
                if xg_pref[te] is not None:
                    gtok = xg_pref[te]
                    xg_pref[te] = None
                else:
                    if t_xg_cons[te] is not None:
                        S.wt("gpsimd", t_xg_cons[te][0])
                        S.wt("gpsimd", t_xg_cons[te][1])
                    gtok = S.dma("gpsimd", f"xg{te}", lambda E, kt=kt, te=te: E.indirect_dma_start(
                        xg[:, te * 2048:(te + 1) * 2048], None,
                        x1_dram[:], IndirectOffsetOnAxis(idxsb[:, kt:kt + 1], 0)))
                S.wait("tensor", f"xg{te}", gtok[1])
                if t_xgfm_cons is not None:
                    S.wt("tensor", t_xgfm_cons)
                transpose_run(
                    NT,
                    lambda j, te=te: xg[:, te * 2048 + j * 128: te * 2048 + (j + 1) * 128],
                    lambda j0, cnt, te=te: xgfm[:].rearrange(
                        "p (j n) -> p j n", n=256)[:, j0:j0 + cnt, te * 128:(te + 1) * 128])
                t_xg_cons[te] = tr_both()
            t_xg_tr = tr_both()

            def s1_pass(mat, t, first):
                S.wt("tensor", t_s1_free)
                if first:
                    S.wt("tensor", t_xg_tr[0])
                    S.wt("tensor", t_xg_tr[1])
                tkm = None
                for c in range(NT):
                    ex_ensure(EXC * e + mat * NT + c)
                    slw, tkw = ex_slots[EXC * e + mat * NT + c]
                    S.wait("tensor", tkw[0], tkw[1])
                    for n in range(3):
                        S.pe(lambda E, slw=slw, c=c, t=t, n=n: E.matmul(
                            s1_region(n),
                            xgfm[:, c * 256 + t * 128: c * 256 + (t + 1) * 128],
                            rbuf2[:, slw * 2048 + n * 512: slw * 2048 + n * 512 + NCH[n]],
                            start=(c == 0), stop=(c == NT - 1)))
                    tkm = S.pe_mark()
                    if mat == 1 and t == 1:
                        ring2_cons[slw] = tkm
                    if mat == 0 and t == 1:
                        ring2_cons[slw] = tkm
                return tkm

            def silu_evac(t):
                nonlocal_toks = []
                last = None
                for n in range(3):
                    S.wt("scalar", t_s1_pass_tok)
                    if t == 0 and t_sa_u is not None:
                        S.wt("scalar", t_sa_u)
                    last = S.op("scalar", lambda E, t=t, n=n: E.activation(
                        sa_e[:, t * IFF + 512 * n: t * IFF + 512 * n + NCH[n]],
                        s1_region(n), AF.Silu))
                return last

            def h_evac(t):
                last = None
                for n in range(3):
                    S.wt("vector", t_s1_pass_tok)
                    S.wt("vector", t_sa_tok[t])
                    if t_htok_u is not None:
                        S.wt("vector", t_htok_u[0])
                        S.wt("vector", t_htok_u[1])
                    last = S.op("vector", lambda E, t=t, n=n: E.tensor_mul(
                        h_tok[:, t * IFF + 512 * n: t * IFF + 512 * n + NCH[n]],
                        s1_region(n),
                        sa_e[:, t * IFF + 512 * n: t * IFF + 512 * n + NCH[n]]))
                return last

            t_sa_tok = [None, None]
            t_s1_pass_tok = s1_pass(0, 0, True)
            if e + 1 < NE:
                for te in range(2):
                    kt = 2 * (e + 1) + te
                    if t_xg_cons[te] is not None:
                        S.wt("gpsimd", t_xg_cons[te][0])
                        S.wt("gpsimd", t_xg_cons[te][1])
                    xg_pref[te] = S.dma("gpsimd", f"xg{te}", lambda E, kt=kt, te=te: E.indirect_dma_start(
                        xg[:, te * 2048:(te + 1) * 2048], None,
                        x1_dram[:], IndirectOffsetOnAxis(idxsb[:, kt:kt + 1], 0)))
            if len(pending_s2) > 0:
                pending_s2[0]()
            t_sa_tok[0] = silu_evac(0)
            t_s1_free = t_sa_tok[0]
            if len(pending_s2) > 1:
                pending_s2[1]()
            t_s1_pass_tok = s1_pass(0, 1, False)
            if len(pending_s2) > 2:
                pending_s2[2]()
            t_sa_tok[1] = silu_evac(1)
            t_s1_free = t_sa_tok[1]
            if len(pending_s2) > 3:
                pending_s2[3]()
            t_s1_pass_tok = s1_pass(1, 0, False)
            t_h0 = h_evac(0)
            t_s1_free = t_h0
            t_s1_pass_tok = s1_pass(1, 1, False)
            t_xgfm_cons = t_s1_pass_tok
            t_h1 = h_evac(1)
            t_s1_free = t_h1

            S.wt("tensor", t_h1)
            for t in range(2):
                transpose_run(
                    NIC,
                    lambda j, t=t: h_tok[:, t * IFF + j * 128: t * IFF + (j + 1) * 128],
                    lambda j0, cnt, t=t: h_ex[:].rearrange(
                        "p (j n) -> p j n", n=256)[:, j0:j0 + cnt, t * 128:(t + 1) * 128])
            t_hex_w = tr_both()
            t_htok_u = t_hex_w

            def make_piece(e, tt, dh, hexw):
                def emit():
                    nonlocal t_s2_banks
                    S.wt("tensor", t_s2_banks)
                    S.wt("tensor", hexw[0])
                    S.wt("tensor", hexw[1])
                    tkm = None
                    for j in range(NIC):
                        ex_ensure(EXC * e + 2 * NT + j)
                        sl2, tk2 = ex_slots[EXC * e + 2 * NT + j]
                        S.wait("tensor", tk2[0], tk2[1])
                        for n in range(2):
                            S.pe(lambda E, sl2=sl2, j=j, tt=tt, dh=dh, n=n: E.matmul(
                                psA[:, 1024 + n * 512: 1024 + (n + 1) * 512],
                                h_ex[:, j * 256 + tt * 128: j * 256 + (tt + 1) * 128],
                                rbuf2[:, sl2 * 2048 + dh * 1024 + n * 512: sl2 * 2048 + dh * 1024 + (n + 1) * 512],
                                start=(j == 0), stop=(j == NIC - 1)))
                    tkm = S.pe_mark()
                    if tt == 1 and dh == 1:
                        for j in range(NIC):
                            ring2_cons[ex_slots[EXC * e + 2 * NT + j][0]] = tkm
                    pb = tt
                    S.wt("vector", tkm)
                    if dh == 0 and t_ys_st[pb] is not None:
                        S.wait("vector", f"yss{pb}", t_ys_st[pb][1])
                    t_ev = S.op("vector", lambda E, pb=pb, dh=dh: _tcopy(E, 
                        ysb[:, pb * 2048 + dh * 1024: pb * 2048 + (dh + 1) * 1024],
                        psA[:, 1024:2048]))
                    t_s2_banks = t_ev
                    if dh == 1:
                        S.wt("gpsimd", t_ev)
                        t_ys_st[pb] = S.dma("gpsimd", f"yss{pb}", lambda E, e=e, tt=tt, pb=pb: E.dma_start(
                            out=ys_dram[e * CAP + tt * 128: e * CAP + (tt + 1) * 128, :],
                            in_=ysb[:, pb * 2048:(pb + 1) * 2048]))
                    return tkm
                return emit
            pending_s2 = [make_piece(e, 0, 0, t_hex_w), make_piece(e, 0, 1, t_hex_w),
                          make_piece(e, 1, 0, t_hex_w), make_piece(e, 1, 1, t_hex_w)]
            ex_ensure(min(EXC * (e + 1) + 15, len(ex_chunks) - 1))

        for p in pending_s2:
            p()

        if dbg:
            S.wait("sync", "yss0")
            S.wait("sync", "yss1")
            S.dma("sync", "dbg", lambda E: E.dma_start(
                out=dbg_outs["dbg_ysort"], in_=ysb[:, 0:2048]))

        S.barrier()
        esE.close()

        esC = ExitStack()
        sbC = lambda name, shape, dt: esC.enter_context(nc.sbuf_tensor(name, shape, dt))
        y0l = sbC("y0l", [128, 2 * 2048], F32)
        g1b = sbC("g1b", [128, 2 * 2048], F32)
        g2b = sbC("g2b", [128, 2 * 2048], F32)
        x1rl = sbC("x1rl", [128, 2 * 2048], BF16)
        ycmb = sbC("ycmb", [128, 2048], F32)
        sqscr2 = sbC("sqscr2", [128, 2048], BF16)
        outsb = sbC("outsb", [128, 2 * 2048], F32)

        S.wait("gpsimd", "yss0")
        S.wait("gpsimd", "yss1")
        S.wait("sync", "y0s0")
        S.wait("sync", "y0s1")
        t_y0l = [None, None]
        t_g1 = [None, None]
        t_g2 = [None, None]
        t_x1l = [None, None]
        t_cons = [None, None]
        t_out_ev = [None, None]

        def phc_load(kk):
            pb = kk % 2
            if t_cons[pb] is not None:
                S.wt("sync", t_cons[pb])
                S.wt("gpsimd", t_cons[pb])
            t_y0l[pb] = S.dma("sync", f"y0l{pb}", lambda E, kk=kk, pb=pb: E.dma_start(
                out=y0l[:, pb * 2048:(pb + 1) * 2048], in_=y0_dram[kk * 128:(kk + 1) * 128, :]))
            t_x1l[pb] = S.dma("sync", f"x1l{pb}", lambda E, kk=kk, pb=pb: E.dma_start(
                out=x1rl[:, pb * 2048:(pb + 1) * 2048], in_=x1_dram[kk * 128:(kk + 1) * 128, :]))
            t_g1[pb] = S.dma("gpsimd", f"gg1{pb}", lambda E, kk=kk, pb=pb: E.indirect_dma_start(
                g1b[:, pb * 2048:(pb + 1) * 2048], None,
                ys_dram[:], IndirectOffsetOnAxis(s1i[:, kk:kk + 1], 0)))
            t_g2[pb] = S.dma("gpsimd", f"gg2{pb}", lambda E, kk=kk, pb=pb: E.indirect_dma_start(
                g2b[:, pb * 2048:(pb + 1) * 2048], None,
                ys_dram[:], IndirectOffsetOnAxis(s2i[:, kk:kk + 1], 0)))

        phc_load(0)
        phc_load(1)
        for kk in range(NSLOT):
            pb = kk % 2
            for tname, tl in (("y0l", t_y0l), ("x1l", t_x1l)):
                S.wait("vector", f"{tname}{pb}", tl[pb][1])
            S.wait("vector", f"gg1{pb}", t_g1[pb][1])
            S.wait("vector", f"gg2{pb}", t_g2[pb][1])
            t1 = S.op("vector", lambda E, kk=kk, pb=pb: E.scalar_tensor_tensor(
                ycmb[:], g1b[:, pb * 2048:(pb + 1) * 2048], gA[:, kk * 8 + 4:kk * 8 + 5],
                y0l[:, pb * 2048:(pb + 1) * 2048], ALU.mult, ALU.add))
            t2 = S.op("vector", lambda E, kk=kk, pb=pb: E.scalar_tensor_tensor(
                ycmb[:], g2b[:, pb * 2048:(pb + 1) * 2048], gA[:, kk * 8 + 5:kk * 8 + 6],
                ycmb[:], ALU.mult, ALU.add))
            t3 = S.op("vector", lambda E, pb=pb: E.tensor_add(
                ycmb[:], ycmb[:], x1rl[:, pb * 2048:(pb + 1) * 2048]))
            t_cons[pb] = t3
            S.wt("scalar", t3)
            t_sq = S.op("scalar", lambda E: E.activation(
                sqscr2[:], ycmb[:], AF.Square, accum_out=sm[:, 40:41]))
            S.op("vector", lambda E: E.reduce_sum(sm[:, 41:42], ycmb[:], axis=AX.X))
            S.op("vector", lambda E: E.tensor_scalar_mul(sm[:, 42:43], sm[:, 41:42], 1.0 / D))
            S.wt("vector", t_sq)
            S.op("vector", lambda E: E.tensor_scalar_mul(sm[:, 43:44], sm[:, 40:41], 1.0 / D))
            S.op("vector", lambda E: E.tensor_mul(sm[:, 44:45], sm[:, 42:43], sm[:, 42:43]))
            S.op("vector", lambda E: E.tensor_sub(sm[:, 45:46], sm[:, 43:44], sm[:, 44:45]))
            t_var = S.op("vector", lambda E: E.tensor_scalar_add(sm[:, 46:47], sm[:, 45:46], EPS))
            S.wt("scalar", t_var)
            t_std = S.op("scalar", lambda E: E.activation(sm[:, 47:48], sm[:, 46:47], AF.Sqrt))
            S.wt("vector", t_std)
            S.op("vector", lambda E: E.reciprocal(sm[:, 48:49], sm[:, 47:48]))
            if t_out_ev[pb] is not None:
                S.wait("vector", f"sto{pb}", t_out_ev[pb][1])
            t_o = S.op("vector", lambda E, pb=pb: E.tensor_scalar(
                outsb[:, pb * 2048:(pb + 1) * 2048], ycmb[:], sm[:, 42:43], sm[:, 48:49],
                ALU.subtract, ALU.mult))
            S.wt("scalar", t_o)
            t_out_ev[pb] = S.dma("scalar", f"sto{pb}", lambda E, kk=kk, pb=pb: E.dma_start(
                out=out_d[kk * 128:(kk + 1) * 128, :], in_=outsb[:, pb * 2048:(pb + 1) * 2048]))
            if kk + 2 < NSLOT:
                phc_load(kk + 2)

        if dbg:
            S.wt("sync", t_sel2_done)
            s12f = sbC("s12f", [128, 2 * NSLOT], F32)
            S.op("vector", lambda E: _tcopy(E, s12f[:, 0:NSLOT], s1i[:]))
            tlast = S.op("vector", lambda E: _tcopy(E, s12f[:, NSLOT:2 * NSLOT], s2i[:]))
            S.wt("sync", tlast)
            S.dma("sync", "dbg", lambda E: E.dma_start(out=dbg_outs["dbg_s12"], in_=s12f[:]))

        S.wait("sync", "sto0")
        S.wait("sync", "sto1")
        esC.close()

        with nc.Block() as block:
            S.build(block)

    return nc



def _host_prep(inputs):
    x = np.asarray(inputs["L0_attn_x"], dtype=np.float32)
    wq = np.asarray(inputs["L0_attn_Wq"], dtype=np.float32)
    wk = np.asarray(inputs["L0_attn_Wk"], dtype=np.float32)
    wv = np.asarray(inputs["L0_attn_Wv"], dtype=np.float32)
    wg = np.asarray(inputs["L0_attn_Wg"], dtype=np.float32)
    wo = np.asarray(inputs["L0_attn_Wo"], dtype=np.float32)
    gate_w = np.asarray(inputs["L0_ffn_gate_w"], dtype=np.float32)
    w1 = np.asarray(inputs["L0_ffn_w1"])
    w2 = np.asarray(inputs["L0_ffn_w2"])
    w3 = np.asarray(inputs["L0_ffn_w3"])
    sw1 = np.asarray(inputs["L0_ffn_sw1"])
    sw2 = np.asarray(inputs["L0_ffn_sw2"])
    sw3 = np.asarray(inputs["L0_ffn_sw3"])

    def headchunk(w):
        return np.ascontiguousarray(w.reshape(NT, 128, NH, HD).transpose(2, 1, 0, 3))

    wqh, wkh, wvh = headchunk(wq), headchunk(wk), headchunk(wv)

    def ichunk(wT, n):
        out = np.empty((n, 128, D), dtype=wT.dtype)
        for j in range(n):
            out[j] = wT[:, j * 128:(j + 1) * 128].reshape(NT, 128, 128).transpose(1, 0, 2).reshape(128, D)
        return out

    sw1c = ichunk(np.ascontiguousarray(sw1.T), NSC)
    sw3c = ichunk(np.ascontiguousarray(sw3.T), NSC)
    sw2r = np.ascontiguousarray(sw2.T.reshape(NSC, 128, D))
    w1c = np.ascontiguousarray(w1.transpose(0, 2, 1).reshape(NE, NT, 128, IFF))
    w3c = np.ascontiguousarray(w3.transpose(0, 2, 1).reshape(NE, NT, 128, IFF))
    w2r = np.stack([np.ascontiguousarray(w2[e].T.reshape(NIC, 128, D)) for e in range(NE)])
    gateT = np.ascontiguousarray(gate_w.T)

    ident = np.eye(128, dtype=BF)
    half = HD // 2
    inv = 1.0 / (10000.0 ** (np.arange(half, dtype=np.float64) / half))
    pos = np.arange(2048, dtype=np.float64)
    th = pos[:, None] * inv[None, :]
    cosv = np.cos(th).astype(np.float32)
    sinv = np.sin(th).astype(np.float32)
    cc = np.concatenate([cosv.T, cosv.T], axis=0).astype(BF)
    ssf = np.concatenate([-sinv.T, sinv.T], axis=0).astype(BF)
    pswap = np.zeros((128, 128), dtype=BF)
    pswap[(np.arange(128) + 64) % 128, np.arange(128)] = 1.0
    ones = np.ones((128, 128), dtype=BF)
    triu = np.triu(np.ones((128, 128), dtype=np.float32), 1).astype(BF)
    iota16 = np.broadcast_to(np.arange(NE, dtype=np.float32), (128, NE)).copy()
    iotatok = (np.arange(NSLOT)[None, :] * 128 + np.arange(128)[:, None]).astype(np.float32)

    consts = dict(c_ident=ident, c_cc=cc, c_ss=ssf, c_pswap=pswap, c_ones=ones,
                  c_triu=triu, c_iota16=iota16, c_iotatok=iotatok)
    shared = dict(wqh=wqh, wkh=wkh, wvh=wvh, wo=wo, wg=wg, gateT=gateT,
                  sw1c=sw1c, sw3c=sw3c, sw2r=sw2r, w1c=w1c, w3c=w3c, w2r=w2r, **consts)

    pvec = np.arange(128)
    in_maps = []
    for core in range(8):
        b, par = core // 2, core % 2
        own = np.concatenate([np.arange(128 * (2 * kk + par), 128 * (2 * kk + par) + 128)
                              for kk in range(NSLOT)])
        xb = x[b]
        m = dict(shared)
        m["xT"] = np.ascontiguousarray(xb.T)
        m["xT_own"] = np.ascontiguousarray(xb.T[:, own])
        m["x_own"] = np.ascontiguousarray(xb[own])
        m["c_ccq"] = np.ascontiguousarray(cc[:, own])
        m["c_ssq"] = np.ascontiguousarray(ssf[:, own])
        causal = np.zeros((128, 256), dtype=np.float32)
        for p in range(128):
            lim = p if par == 0 else 128 + p
            causal[p, :lim + 1] = 1.0
        m["c_causal"] = causal.astype(BF)
        bandm = np.zeros((2, 128, 320), dtype=np.float32)
        for v in range(2):
            for p in range(128):
                for bj in range(320):
                    if v == 0:
                        L = 256
                        j = bj
                    else:
                        L = 512
                        j = L - 320 + bj
                    i = (L - 256 + p) if par == 0 else (L - 128 + p)
                    if 0 <= j < L and 0 <= i - j <= 64:
                        bandm[v, p, bj] = 1.0
        m["c_band"] = bandm.astype(BF)
        cmpm = np.zeros((NSLOT, 128, 32), dtype=np.float32)
        curadd = np.zeros((NSLOT, 128, 32), dtype=np.float32)
        for kk in range(NSLOT):
            t = 2 * kk + par
            i = 128 * t + pvec
            for mm_ in range(32):
                vis = (i >= 64 * (mm_ + 1) - 1)
                cmpm[kk, vis, mm_] = 1.0
            cur = i // 64
            curadd[kk, pvec, cur] = 1e30
        m["c_cmpm"] = cmpm
        m["c_curadd"] = curadd
        in_maps.append(m)
    return in_maps


def kernel(**inputs):
    in_maps = _host_prep(inputs)
    nc = build_program()
    res = run_bass_kernel_spmd(nc, in_maps, core_ids=list(range(8)))
    out = np.empty((4, 2048, D), dtype=np.float32)
    for core in range(8):
        b, par = core // 2, core % 2
        o = res.results[core]["out"]
        for kk in range(NSLOT):
            t = 2 * kk + par
            out[b, 128 * t:128 * (t + 1)] = o[128 * kk:128 * (kk + 1)]
    return out


# version 23 (final)
# speedup vs baseline: 1.0426x; 1.0074x over previous
import numpy as np
import ml_dtypes
from contextlib import ExitStack

import concourse.bass as bass
import concourse.mybir as mybir
from concourse.bass_core_rust import IndirectOffsetOnAxis
from concourse.bass_utils import run_bass_kernel_spmd

BF = ml_dtypes.bfloat16
F32 = mybir.dt.float32
BF16 = mybir.dt.bfloat16
I32 = mybir.dt.int32
U32 = mybir.dt.uint32
AF = mybir.ActivationFunctionType
ALU = mybir.AluOpType
AX = mybir.AxisListType

D = 2048
NH = 16
HD = 128
NSLOT = 8
NT = 16
NE = 16
IFF = 1408
SI = 2816
CAP = 256
NIC = IFF // 128
NSC = SI // 128
SCALE = float(HD) ** -0.5
EPS = 1e-5

ENGS = ("tensor", "vector", "scalar", "gpsimd", "sync")
SELF = {"tensor": "PE", "vector": "DVE", "scalar": "ACT", "gpsimd": "GP", "sync": "SQ"}


class Sched:

    def __init__(self, nc, es):
        self.nc = nc
        self.es = es
        self.q = {e: [] for e in ENGS}
        self.sems = {}
        self.C = {}
        self._pe_pending = None

    def _pe_flush(self):
        if self._pe_pending is not None:
            fnp = self._pe_pending
            self._pe_pending = None
            self.q["tensor"].append(lambda E, fnp=fnp: fnp(E))

    def sem(self, name):
        if name not in self.sems:
            self.sems[name] = self.es.enter_context(self.nc.semaphore(name))
            self.C[name] = 0
        return self.sems[name]

    def wait(self, eng, name, val=None):
        s = self.sem(name)
        v = self.C[name] if val is None else val
        if v > 0:
            self.q[eng].append(lambda E, s=s, v=v: E.wait_ge(s, v))

    def wt(self, eng, tok):
        if tok is not None:
            self.wait(eng, tok[0], tok[1])

    def op(self, eng, fn):
        if eng == "tensor":
            self._pe_flush()
        sn = SELF[eng]
        s = self.sem(sn)
        if eng != "tensor" and self.C[sn] > 0:
            self.q[eng].append(lambda E, s=s, v=self.C[sn]: E.wait_ge(s, v))
        self.C[sn] += 1
        v = self.C[sn]
        self.q[eng].append(lambda E, fn=fn, s=s: fn(E).then_inc(s, 1))
        return (sn, v)

    def pe(self, fn):
        self._pe_flush()
        self._pe_pending = fn

    def pe_mark(self):
        if self._pe_pending is None:
            return self.op("tensor", lambda E: E.nop())
        fnp = self._pe_pending
        self._pe_pending = None
        return self.op("tensor", fnp)

    def dma(self, eng, name, fn):
        s = self.sem(name)
        self.C[name] += 16
        v = self.C[name]
        self.q[eng].append(lambda E, fn=fn, s=s: fn(E).then_inc(s, 16))
        return (name, v)

    def barrier(self):
        snap = {n: v for n, v in self.C.items() if v > 0}
        for eng in ENGS:
            sn = SELF[eng]
            for name, v in snap.items():
                if name != sn:
                    self.wait(eng, name, v)

    def build(self, block):
        self._pe_flush()
        for eng in ENGS:
            lst = self.q[eng]
            if lst:
                getattr(block, eng)(lambda E, lst=lst: [f(E) for f in lst])


def _tcopy(E, out, in_):
    fn = getattr(E, "tensor_copy", None)
    if fn is None:
        fn = E.copy
    return fn(out, in_)


def ceil_div(a, b):
    return (a + b - 1) // b


def build_program(dbg=False):
    nc = bass.Bass("TRN2", target_bir_lowering=False)

    def din(name, shape, dt):
        return nc.dram_tensor(name, shape, dt, kind="ExternalInput").ap()

    xT = din("xT", [D, 2048], F32)
    xT_own = din("xT_own", [D, 1024], F32)
    x_own = din("x_own", [1024, D], F32)
    wqh = din("wqh", [NH, 128, NT, HD], F32)
    wkh = din("wkh", [NH, 128, NT, HD], F32)
    wvh = din("wvh", [NH, 128, NT, HD], F32)
    wo_in = din("wo", [D, D], F32)
    wg_in = din("wg", [D, 48], F32)
    gateT = din("gateT", [D, NE], F32)
    sw1c = din("sw1c", [NSC, 128, D], BF16)
    sw3c = din("sw3c", [NSC, 128, D], BF16)
    sw2r = din("sw2r", [NSC, 128, D], BF16)
    w1c = din("w1c", [NE, NT, 128, IFF], BF16)
    w3c = din("w3c", [NE, NT, 128, IFF], BF16)
    w2r = din("w2r", [NE, NIC, 128, D], BF16)
    c_ident = din("c_ident", [128, 128], BF16)
    c_cc = din("c_cc", [128, 2048], BF16)
    c_ss = din("c_ss", [128, 2048], BF16)
    c_ccq = din("c_ccq", [128, 1024], BF16)
    c_ssq = din("c_ssq", [128, 1024], BF16)
    c_pswap = din("c_pswap", [128, 128], BF16)
    c_causal = din("c_causal", [128, 256], BF16)
    c_band = din("c_band", [2, 128, 320], BF16)
    c_cmpm = din("c_cmpm", [NSLOT, 128, 32], F32)
    c_curadd = din("c_curadd", [NSLOT, 128, 32], F32)
    c_ones = din("c_ones", [128, 128], BF16)
    c_triu = din("c_triu", [128, 128], BF16)
    c_iota16 = din("c_iota16", [128, NE], F32)
    c_iotatok = din("c_iotatok", [128, NSLOT], F32)

    out_d = nc.dram_tensor("out", [1024, D], F32, kind="ExternalOutput").ap()

    x1_dram = nc.dram_tensor("x1_dram", [1024, D], BF16, kind="Internal").ap()
    y0_dram = nc.dram_tensor("y0_dram", [1024, D], F32, kind="Internal").ap()
    ys_dram = nc.dram_tensor("ys_dram", [NE * CAP, D], F32, kind="Internal").ap()
    idx_dram = nc.dram_tensor("idx_dram", [128 * 32, 1], I32, kind="Internal").ap()

    dbg_outs = {}
    if dbg:
        for name, shape, dt in [
            ("dbg_krot", [128, 2048], BF16), ("dbg_e", [128, 2048], BF16),
            ("dbg_pcomb", [128, 2048], BF16), ("dbg_oT", [128, 1024], BF16),
            ("dbg_x1", [128, D], BF16), ("dbg_scores", [128, NSLOT * NE], F32),
            ("dbg_rank", [128, NSLOT * NE], F32), ("dbg_s12", [128, 2 * NSLOT], F32),
            ("dbg_g", [128, 48], F32), ("dbg_idx", [128, 32], I32),
            ("dbg_y0", [128, D], F32), ("dbg_ysort", [128, D], F32),
        ]:
            dbg_outs[name] = nc.dram_tensor(name, shape, dt, kind="ExternalOutput").ap()

    es_root = ExitStack()
    with es_root:
        S = Sched(nc, es_root)
        sb = lambda name, shape, dt: es_root.enter_context(nc.sbuf_tensor(name, shape, dt))

        ident = sb("ident", [128, 128], BF16)
        pswap = sb("pswap", [128, 128], BF16)
        onesb = sb("onesb", [128, 128], BF16)
        triu = sb("triu", [128, 128], BF16)
        iota16 = sb("iota16", [128, NE], F32)
        iotatok = sb("iotatok", [128, NSLOT], F32)
        sm = sb("sm", [128, 64], F32)
        gA = sb("gA", [128, NSLOT * 8], F32)
        s1i = sb("s1i", [128, NSLOT], I32)
        s2i = sb("s2i", [128, NSLOT], I32)
        toki = sb("toki", [128, NSLOT], I32)
        iwrk = sb("iwrk", [128, 8], I32)
        posb = sb("posb", [128, 16], I32)
        idxsb = sb("idxsb", [128, 32], I32)

        psA = es_root.enter_context(nc.psum_tensor("psA", [128, 2048], F32))
        psS = es_root.enter_context(nc.psum_tensor("psS", [128, 1024], F32))
        psT = es_root.enter_context(nc.psum_tensor("psT", [128, 1024], BF16))
        psU = es_root.enter_context(nc.psum_tensor("psU", [128, 1024], BF16))

        trs = {"bank": 0, "free": [None, None], "last": [None, None]}

        def transpose_run(n, src_chunk_fn, dst_view_fn):
            j0 = 0
            while j0 < n:
                cnt = min(8, n - j0)
                bank = trs["bank"]
                trs["bank"] ^= 1
                ps_t = psT if bank == 0 else psU
                eng = "vector" if bank == 0 else "scalar"
                S.wt("tensor", trs["free"][bank])
                for g in range(cnt):
                    S.pe(lambda E, ps_t=ps_t, g=g, j=j0 + g: E.transpose(
                        ps_t[:, g * 128:(g + 1) * 128], src_chunk_fn(j), ident[:]))
                tkm = S.pe_mark()
                S.wt(eng, tkm)
                tev = S.op(eng, lambda E, ps_t=ps_t, cnt=cnt, j0=j0: _tcopy(
                    E, dst_view_fn(j0, cnt),
                    ps_t[:, 0:cnt * 128].rearrange("p (j n) -> p j n", n=128)))
                trs["free"][bank] = tev
                trs["last"][0 if eng == "vector" else 1] = tev
                j0 += cnt
            return trs["last"]

        def transpose_to(src_ap_fn, dst_ap_fn, rows=128):
            bank = trs["bank"]
            trs["bank"] ^= 1
            ps_t = psT if bank == 0 else psU
            eng = "vector" if bank == 0 else "scalar"
            S.wt("tensor", trs["free"][bank])
            S.pe(lambda E, ps_t=ps_t, rows=rows: E.transpose(
                ps_t[0:rows, 0:128], src_ap_fn(), ident[:]))
            tkm = S.pe_mark()
            S.wt(eng, tkm)
            tev = S.op(eng, lambda E, ps_t=ps_t, rows=rows, dst_ap_fn=dst_ap_fn: _tcopy(
                E, dst_ap_fn(), ps_t[0:rows, 0:128]))
            trs["free"][bank] = tev
            trs["last"][0 if eng == "vector" else 1] = tev
            return tev

        def tr_both():
            return (trs["last"][0], trs["last"][1])

        for nm, dst, src in [("ident", ident, c_ident), ("pswap", pswap, c_pswap),
                             ("ones", onesb, c_ones), ("triu", triu, c_triu),
                             ("iota16", iota16, c_iota16), ("iotatok", iotatok, c_iotatok)]:
            S.dma("sync", "cst", lambda E, dst=dst, src=src: E.dma_start(out=dst[:], in_=src))

        esA = ExitStack()
        sbA = lambda name, shape, dt: esA.enter_context(nc.sbuf_tensor(name, shape, dt))
        cc = sbA("cc", [128, 2048], BF16)
        ss = sbA("ss", [128, 2048], BF16)
        ccq = sbA("ccq", [128, 1024], BF16)
        ssq = sbA("ssq", [128, 1024], BF16)
        causal = sbA("causal", [128, 256], BF16)
        bandm = sbA("bandm", [128, 2 * 320], BF16)
        cmpm = sbA("cmpm", [128, NSLOT * 32], F32)
        curadd = sbA("curadd", [128, NSLOT * 32], F32)
        g_all = sbA("g_all", [128, NSLOT * 48], F32)
        xfm = sbA("xfm", [128, NT * 2048], BF16)
        xfo = sbA("xfo", [128, NT * 1024], BF16)
        wkb = sbA("wkb", [128, NT * 128], BF16)
        wqb = sbA("wqb", [128, NT * 128], BF16)
        wvb = sbA("wvb", [128, NT * 128], BF16)
        wgb = sbA("wgb", [128, NT * 48], BF16)
        kpre = sbA("kpre", [128, 2048], BF16)
        krot = sbA("krot", [128, 2048], BF16)
        vfm_sb = sbA("vfm_sb", [128, 2048], BF16)
        vtok = sbA("vtok", [128, 2048], BF16)
        qpre = sbA("qpre", [128, 1024], BF16)
        qrot = sbA("qrot", [128, 1024], BF16)
        kcmp = sbA("kcmp", [128, 32], BF16)
        cmpf = sbA("cmpf", [128, 32], F32)
        e_sb = sbA("e_sb", [128, 3840], BF16)
        ecmp = sbA("ecmp", [128, 32], F32)
        tpcmp = sbA("tpcmp", [128, 2 * 32], BF16)
        pcmpT = sbA("pcmpT", [128, 2 * 128], BF16)
        vcmpb = sbA("vcmpb", [128, 32], BF16)
        vcmpt = sbA("vcmpt", [128, 128], BF16)
        m8a = sbA("m8a", [128, 8], F32)
        selb = sbA("selb", [128, 32], F32)
        eswa = sbA("eswa", [128, 320], F32)
        pcomb = sbA("pcomb", [128, 4096], BF16)
        rtmp = sbA("rtmp", [128, 512], BF16)
        pT = sbA("pT", [128, NT * 256], BF16)
        esO = ExitStack()
        oT_all = esO.enter_context(nc.sbuf_tensor("oT_all", [128, NH * 1024], BF16, side="right"))

        for nm, dst, src in [("cc", cc, c_cc), ("ss", ss, c_ss), ("ccq", ccq, c_ccq),
                             ("ssq", ssq, c_ssq), ("causal", causal, c_causal)]:
            S.dma("sync", "cst", lambda E, dst=dst, src=src: E.dma_start(out=dst[:], in_=src))
        for v in range(2):
            S.dma("sync", "cst", lambda E, v=v: E.dma_start(
                out=bandm[:, v * 320:(v + 1) * 320], in_=c_band[v]))
        for kk in range(NSLOT):
            S.dma("sync", "cst", lambda E, kk=kk: E.dma_start(
                out=cmpm[:, kk * 32:(kk + 1) * 32], in_=c_cmpm[kk]))
            S.dma("sync", "cst", lambda E, kk=kk: E.dma_start(
                out=curadd[:, kk * 32:(kk + 1) * 32], in_=c_curadd[kk]))

        gx_tok = {}
        for c in range(NT):
            gx_tok[c] = S.dma("gpsimd", f"gx{c // 4}", lambda E, c=c: E.dma_start(
                out=xfm[:, c * 2048:(c + 1) * 2048], in_=xT[c * 128:(c + 1) * 128, :]))
        for c in range(NT):
            S.dma("gpsimd", "xfo", lambda E, c=c: E.dma_start(
                out=xfo[:, c * 1024:(c + 1) * 1024], in_=xT_own[c * 128:(c + 1) * 128, :]))
        for c in range(NT):
            S.dma("gpsimd", "gw", lambda E, c=c: E.dma_start(
                out=wgb[:, c * 48:(c + 1) * 48], in_=wg_in[c * 128:(c + 1) * 128, :]))

        for eng in ("tensor", "vector", "scalar", "gpsimd"):
            S.wait(eng, "cst")
        t_e_init = S.op("vector", lambda E: E.memset(e_sb[:], 0.0))
        t_p_init = S.op("vector", lambda E: E.memset(pcomb[:], 0.0))

        def load_w(nm, h, gate_tok):
            src, buf = {"wk": (wkh, wkb), "wq": (wqh, wqb), "wv": (wvh, wvb)}[nm]
            if gate_tok is not None:
                S.wt("gpsimd", gate_tok)
            return S.dma("gpsimd", nm, lambda E, src=src, buf=buf, h=h: E.dma_start(
                out=buf[:], in_=src[h].rearrange("p c m -> p (c m)")))

        hw_tok = {0: {nm: load_w(nm, 0, None) for nm in ("wk", "wq", "wv")}}

        for i in range(4):
            S.wait("tensor", f"gx{i}")
        S.wait("tensor", "xfo")
        S.wait("tensor", "gw")
        t_g_evac = None
        for kk in range(NSLOT):
            if t_g_evac is not None:
                S.wt("tensor", t_g_evac)
            for c in range(NT):
                S.pe(lambda E, kk=kk, c=c: E.matmul(
                    psS[:, 0:48],
                    xfo[:, c * 1024 + kk * 128: c * 1024 + (kk + 1) * 128],
                    wgb[:, c * 48:(c + 1) * 48],
                    start=(c == 0), stop=(c == NT - 1)))
            tpe = S.pe_mark()
            S.wt("scalar", tpe)
            t_g_evac = S.op("scalar", lambda E, kk=kk: E.activation(
                g_all[:, kk * 48:(kk + 1) * 48], psS[:, 0:48], AF.Sigmoid))
        if dbg:
            S.wt("sync", t_g_evac)
            S.dma("sync", "dbg", lambda E: E.dma_start(out=dbg_outs["dbg_g"], in_=g_all[:, 0:48]))

        t_oT_evac = [None] * NSLOT
        t_trA = [None] * 8
        t_trB = [None] * 8
        t_projA = t_g_evac
        t_sev = [None, None, None, None]
        t_scmp_ev = None
        t_kpre_used = None
        t_krot_used = None
        t_vfm_used2 = None
        t_qrot_used = None
        t_vtok_used = None
        t_e_used = [None, None]
        t_pcomb_used = [None, None]
        t_ecmp_used = [None, None]
        t_eswa_used = None
        t_bs_used = None
        t_sel_used = None
        t_imp_used = None
        t_w_cons = {}

        PS = {}

        def emit_kproj(h):
            st = PS.setdefault(h, {})
            S.wait("tensor", "wk", hw_tok[h]["wk"][1])
            S.wt("tensor", t_qrot_done_g[0])
            S.wt("tensor", t_kpre_used_g[0])
            S.wt("tensor", t_sev[2])
            S.wt("tensor", t_sev[3])
            kev = None
            for pair in range(2):
                for c in range(NT):
                    if h == 0 and c % 4 == 0:
                        S.wait("tensor", f"gx{c // 4}", gx_tok[c + 3][1])
                    for i in range(2):
                        tc = pair * 2 + i
                        S.pe(lambda E, c=c, tc=tc, i=i: E.matmul(
                            psA[:, i * 512:(i + 1) * 512],
                            wkb[:, c * 128:(c + 1) * 128],
                            xfm[:, c * 2048 + tc * 512: c * 2048 + (tc + 1) * 512],
                            start=(c == 0), stop=(c == NT - 1)))
                tpe = S.pe_mark()
                S.wt("scalar", tpe)
                for i in range(2):
                    tc = pair * 2 + i
                    kev = S.op("scalar", lambda E, tc=tc, i=i: E.copy(
                        kpre[:, tc * 512:(tc + 1) * 512], psA[:, i * 512:(i + 1) * 512]))
                S.wt("tensor", kev)
            if h + 1 < NH:
                hw_tok.setdefault(h + 1, {})["wk"] = load_w("wk", h + 1, tpe)
            st["kpre_ev"] = kev

        def emit_vproj(h):
            st = PS.setdefault(h, {})
            S.wait("tensor", "wv", hw_tok[h]["wv"][1])
            vev = None
            for pair in range(2):
                for c in range(NT):
                    for i in range(2):
                        tc = pair * 2 + i
                        S.pe(lambda E, c=c, tc=tc, i=i: E.matmul(
                            psA[:, i * 512:(i + 1) * 512],
                            wvb[:, c * 128:(c + 1) * 128],
                            xfm[:, c * 2048 + tc * 512: c * 2048 + (tc + 1) * 512],
                            start=(c == 0), stop=(c == NT - 1)))
                tpe = S.pe_mark()
                S.wt("scalar", tpe)
                for i in range(2):
                    tc = pair * 2 + i
                    vev = S.op("scalar", lambda E, tc=tc, i=i: E.copy(
                        vfm_sb[:, tc * 512:(tc + 1) * 512], psA[:, i * 512:(i + 1) * 512]))
                S.wt("tensor", vev)
            if h + 1 < NH:
                hw_tok[h + 1]["wv"] = load_w("wv", h + 1, tpe)
            st["vfm_ev"] = vev

        def emit_qproj(h):
            st = PS.setdefault(h, {})
            S.wait("tensor", "wq", hw_tok[h]["wq"][1])
            qev = None
            for c in range(NT):
                for i in range(2):
                    S.pe(lambda E, c=c, i=i: E.matmul(
                        psA[:, i * 512:(i + 1) * 512],
                        wqb[:, c * 128:(c + 1) * 128],
                        xfo[:, c * 1024 + i * 512: c * 1024 + (i + 1) * 512],
                        start=(c == 0), stop=(c == NT - 1)))
            tpe = S.pe_mark()
            if h + 1 < NH:
                hw_tok[h + 1]["wq"] = load_w("wq", h + 1, tpe)
            S.wt("scalar", tpe)
            for i in range(2):
                qev = S.op("scalar", lambda E, i=i: E.copy(
                    qpre[:, i * 512:(i + 1) * 512], psA[:, i * 512:(i + 1) * 512]))
            S.wt("tensor", qev)
            st["qev"] = qev

        t_qrot_done_g = [None]
        t_kpre_used_g = [None]

        for h in range(NH):
            if h == 0:
                emit_kproj(0)
                emit_vproj(0)
                emit_qproj(0)
            t_kpre_ev = PS[h]["kpre_ev"]
            t_vfm_ev = PS[h]["vfm_ev"]
            qev = PS[h]["qev"]

            S.wt("tensor", t_vfm_ev)
            S.wt("tensor", t_vtok_used)
            transpose_run(
                NT,
                lambda j: vfm_sb[:, j * 128:(j + 1) * 128],
                lambda j0, cnt: vtok[:].rearrange("p (j n) -> p j n", n=128)[:, j0:j0 + cnt, :])
            t_v_last2 = {"vector": trs["last"][0], "scalar": trs["last"][1]}
            S.wt("vector", t_vfm_ev)
            S.op("vector", lambda E: E.reduce_sum(
                cmpf[:], vfm_sb[:].rearrange("p (nb k) -> p nb k", nb=32), axis=AX.X))
            t_vcb = S.op("vector", lambda E: _tcopy(E, vcmpb[:], cmpf[:]))
            S.wt("tensor", t_vcb)
            t_vct = transpose_to(lambda: vcmpb[:], lambda: vcmpt[0:32, :], rows=32)

            for kq in range(NSLOT):
                S.wt("tensor", t_oT_evac[kq])
            S.wt("vector", t_kpre_ev)
            S.wt("vector", t_krot_used)
            S.op("vector", lambda E: E.tensor_mul(krot[:], kpre[:], cc[:]))
            t_rope_last = None
            t_r1k = [None] * 4
            swp_toks = []
            for tc in range(4):
                S.pe(lambda E, tc=tc: E.matmul(
                    psA[:, tc * 512:(tc + 1) * 512], pswap[:], kpre[:, tc * 512:(tc + 1) * 512],
                    start=True, stop=True))
                swp_toks.append(S.pe_mark())
            for tc in range(4):
                S.wt("vector", swp_toks[tc])
                t_r1k[tc] = S.op("vector", lambda E, tc=tc: E.tensor_mul(
                    rtmp[:], psA[:, tc * 512:(tc + 1) * 512], ss[:, tc * 512:(tc + 1) * 512]))
                t_rope_last = S.op("vector", lambda E, tc=tc: E.tensor_add(
                    krot[:, tc * 512:(tc + 1) * 512], krot[:, tc * 512:(tc + 1) * 512], rtmp[:]))
            t_krot_done = t_rope_last
            S.wt("vector", qev)
            S.op("vector", lambda E: E.tensor_mul(qrot[:], qpre[:], ccq[:]))
            swq_toks = []
            for i in range(2):
                S.wt("tensor", t_r1k[i])
                S.pe(lambda E, i=i: E.matmul(
                    psA[:, i * 512:(i + 1) * 512], pswap[:], qpre[:, i * 512:(i + 1) * 512],
                    start=True, stop=True))
                swq_toks.append(S.pe_mark())
            for i in range(2):
                S.wt("vector", swq_toks[i])
                t_r1 = S.op("vector", lambda E, i=i: E.tensor_mul(
                    rtmp[:], psA[:, i * 512:(i + 1) * 512], ssq[:, i * 512:(i + 1) * 512]))
                t_rope_last = S.op("vector", lambda E, i=i: E.tensor_add(
                    qrot[:, i * 512:(i + 1) * 512], qrot[:, i * 512:(i + 1) * 512], rtmp[:]))
                S.wt("tensor", t_r1)
            t_qrot_done = t_rope_last
            t_qrot_done_g[0] = t_rope_last
            t_projA = t_rope_last
            t_kpre_used = t_krot_done
            t_kpre_used_g[0] = t_krot_done

            S.wt("tensor", t_krot_done)
            S.pe(lambda E: E.matmul(
                psA[:, 2016:2032], ident[:], ident[:, 0:16], start=True, stop=True))
            S.pe_mark()
            S.wt("vector", t_krot_done)
            S.op("vector", lambda E: E.reduce_sum(
                cmpf[:], krot[:].rearrange("p (nb k) -> p nb k", nb=32), axis=AX.X))
            t_kcmp = S.op("vector", lambda E: _tcopy(E, kcmp[:], cmpf[:]))
            S.wt("tensor", t_kcmp)
            S.pe(lambda E: E.matmul(
                psA[:, 2016:2032], ident[:], kcmp[:, 0:16], start=True, stop=True))
            S.pe_mark()

            if dbg and h == 0:
                S.wt("sync", t_krot_done)
                S.dma("sync", "dbg", lambda E: E.dma_start(out=dbg_outs["dbg_krot"], in_=krot[:]))

            pending_tp = None
            pending_pv = None
            t_pv = None
            t_pv_pair_last = None
            for kk in range(NSLOT):
                par = kk % 2
                L = 256 * (kk + 1)
                nb = L // 64
                nchunks = ceil_div(L, 512)
                eS = e_sb[:, par * 1792: par * 1792 + L]
                pC = pcomb[:, par * 2048: par * 2048 + L + (256 if par == 0 else 0)]

                S.wt("tensor", t_kcmp)
                S.wt("tensor", t_qrot_done)
                S.wt("tensor", t_scmp_ev)
                for kq in range(4, NSLOT):
                    S.wt("tensor", t_oT_evac[kq])
                S.pe(lambda E, kk=kk: E.matmul(
                    psA[:, 2016:2048], qrot[:, kk * 128:(kk + 1) * 128], kcmp[:],
                    start=True, stop=True))
                t_scmp = S.pe_mark()

                t_ex_last = None
                for sc in range(nchunks):
                    N = min(512, L - sc * 512)
                    bank = (sc % 4) if kk < 4 else (sc % 2)
                    ps_b = psS if bank < 2 else psA
                    boff = (bank % 2) * 512
                    S.wt("tensor", t_sev[bank])
                    S.pe(lambda E, kk=kk, sc=sc, N=N, ps_b=ps_b, boff=boff: E.matmul(
                        ps_b[:, boff: boff + N],
                        qrot[:, kk * 128:(kk + 1) * 128],
                        krot[:, sc * 512: sc * 512 + N],
                        start=True, stop=True))
                    tpe = S.pe_mark()
                    S.wt("scalar", tpe)
                    S.wt("scalar", t_e_used[par])
                    tex = S.op("scalar", lambda E, eS=eS, sc=sc, N=N, ps_b=ps_b, boff=boff: E.activation(
                        eS[:, sc * 512: sc * 512 + N], ps_b[:, boff: boff + N],
                        AF.Exp, scale=SCALE))
                    t_sev[bank] = tex
                    t_ex_last = tex

                if pending_tp is not None:
                    trtoks = pending_tp()
                    pending_tp = None
                    if pending_pv is not None:
                        t_pv = pending_pv(trtoks)
                        t_pv_pair_last = t_pv
                        pending_pv = None
                if h + 1 < NH:
                    if kk == 4:
                        emit_kproj(h + 1)
                    elif kk == 5:
                        emit_vproj(h + 1)
                    elif kk == 6:
                        emit_qproj(h + 1)

                S.wt("scalar", t_scmp)
                S.wt("scalar", t_ecmp_used[par])
                t_ec0 = S.op("scalar", lambda E: E.activation(
                    sm[:, 32:64], psA[:, 2016:2048], AF.Exp, scale=SCALE / 64.0))
                t_scmp_ev = t_ec0
                for bank in range(min(nchunks, 4)):
                    S.wt("vector", t_sev[bank])
                S.wt("vector", t_e_used[par])
                t_diag = S.op("vector", lambda E, eS=eS, L=L: E.tensor_mul(
                    eS[:, L - 256:L], eS[:, L - 256:L], causal[:]))
                S.wt("vector", t_ec0)
                t_ecm = S.op("vector", lambda E, kk=kk: E.scalar_tensor_tensor(
                    ecmp[:], sm[:, 32:64], 1.0, cmpm[:, kk * 32:(kk + 1) * 32],
                    ALU.mult, ALU.mult, accum_out=sm[:, 0:1]))
                t_imp = S.op("vector", lambda E, kk=kk: E.tensor_add(
                    eswa[:, 0:32], ecmp[:], curadd[:, kk * 32:(kk + 1) * 32]))
                S.op("vector", lambda E: E.max(m8a[:], eswa[:, 0:32]))
                t_sel = S.op("vector", lambda E: E.tensor_single_scalar(
                    selb[:], eswa[:, 0:32], m8a[:, 7:8], ALU.is_ge))
                t_imp_used = t_sel
                S.wt("tensor", t_imp)
                S.pe(lambda E: E.matmul(
                    psA[:, 2016:2032], ident[:], kcmp[:, 0:16], start=True, stop=True))
                t_warm = S.pe_mark()

                if t_pcomb_used[par] is not None:
                    S.wt("vector", t_pcomb_used[par][0])
                    S.wt("vector", t_pcomb_used[par][1])
                S.wt("vector", t_diag)
                tA = S.op("vector", lambda E, eS=eS, pC=pC, L=L, nb=nb: E.scalar_tensor_tensor(
                    pC[:, 0:L], eS[:, 0:L], 1.0,
                    selb[:, 0:nb].rearrange("p n -> p n ()").broadcast_to([128, nb, 64]),
                    ALU.mult, ALU.mult, accum_out=sm[:, 1:2]))
                t_e_used[par] = tA
                bs0 = 0 if kk == 0 else L - 320
                bv = 0 if kk == 0 else 1
                S.wt("vector", t_eswa_used)
                t_esw = S.op("vector", lambda E, par=par, bs0=bs0, bv=bv: E.scalar_tensor_tensor(
                    eswa[:], e_sb[:, par * 1792 + bs0: par * 1792 + bs0 + 320],
                    1.0, bandm[:, bv * 320:(bv + 1) * 320],
                    ALU.mult, ALU.mult, accum_out=sm[:, 2:3]))
                S.op("vector", lambda E: E.tensor_scalar_add(sm[:, 4:7], sm[:, 0:3], 1e-30))
                S.op("vector", lambda E: E.reciprocal(sm[:, 12:15], sm[:, 4:7]))
                t_rr = S.op("vector", lambda E, kk=kk, h=h: E.tensor_tensor(
                    sm[:, 8:11],
                    g_all[:, kk * 48: kk * 48 + 48].rearrange("p (a b) -> p a b", b=16)[:, :, h:h + 1],
                    sm[:, 12:15], ALU.mult))
                tB = S.op("vector", lambda E, pC=pC, L=L: E.tensor_scalar_mul(
                    pC[:, 0:L], pC[:, 0:L], sm[:, 9:10]))
                S.wt("tensor", tA)
                S.pe(lambda E: E.matmul(
                    psA[:, 2016:2032], ident[:], kcmp[:, 0:16], start=True, stop=True))
                S.pe_mark()
                t4 = S.op("vector", lambda E, par=par, bs0=bs0: E.scalar_tensor_tensor(
                    pcomb[:, par * 2048 + bs0: par * 2048 + bs0 + 320], eswa[:], sm[:, 10:11],
                    pcomb[:, par * 2048 + bs0: par * 2048 + bs0 + 320],
                    ALU.mult, ALU.add))
                if par == 0:
                    S.wt("gpsimd", t4)
                    if t_pcomb_used[par] is not None:
                        S.wt("gpsimd", t_pcomb_used[par][0])
                    t4 = S.op("gpsimd", lambda E, par=par, L=L: E.memset(
                        pcomb[:, par * 2048 + L: par * 2048 + L + 256], 0.0))
                t_eswa_used = t4
                t_pre = S.op("vector", lambda E, kk=kk, par=par: E.tensor_scalar(
                    tpcmp[:, par * 32:(par + 1) * 32], ecmp[:], sm[:, 8:9], 1.0 / 64.0,
                    ALU.mult, ALU.mult))
                t_ecmp_used[par] = t_pre
                t_pcomb_ready = t4
                t_tpcmp_ready = t_pre

                if dbg and h == 0 and kk == 2:
                    S.wt("sync", t4)
                    S.dma("sync", "dbg", lambda E, eS=eS: E.dma_start(out=dbg_outs["dbg_e"], in_=eS))
                    S.dma("sync", "dbg", lambda E, pC=pC: E.dma_start(out=dbg_outs["dbg_pcomb"], in_=pC))

                def make_tp(kk=kk, par=par, L=L, pC=pC, t_pcomb_ready=t_pcomb_ready,
                            t_tpcmp_ready=t_tpcmp_ready, t_vct=t_vct, h=h):
                    def emit():
                        S.wt("tensor", t_pcomb_ready)
                        S.wt("tensor", t_pv_pair_last)
                        nch = L // 128 + (2 if par == 0 else 0)
                        transpose_run(
                            nch,
                            lambda j: pC[:, j * 128:(j + 1) * 128],
                            lambda j0, cnt, par=par: pT[:].rearrange(
                                "p (j two n) -> p j two n", two=2, n=128)[:, j0:j0 + cnt, par, :])
                        S.wt("tensor", t_tpcmp_ready)
                        t_pct = transpose_to(
                            lambda: tpcmp[:, par * 32:(par + 1) * 32],
                            lambda: pcmpT[0:32, par * 128:(par + 1) * 128], rows=32)
                        t_tr_last2 = {"vector": trs["last"][0], "scalar": trs["last"][1]}
                        t_pcomb_used[par] = (t_tr_last2["vector"], t_tr_last2["scalar"])
                        return (t_tr_last2, t_pct)
                    return emit

                def make_pv(kk=kk, L=L, h=h):
                    def emit(trtoks):
                        t_tr_last2, t_pct = trtoks
                        pair = kk // 2
                        S.wt("tensor", t_tr_last2["vector"])
                        S.wt("tensor", t_tr_last2["scalar"])
                        S.wt("tensor", t_pct)
                        S.wt("tensor", t_vct)
                        for kq in range(4 * (kk // 4), 4 * (kk // 4) + 4):
                            S.wt("tensor", t_oT_evac[kq])
                        for jc in range(L // 128):
                            S.pe(lambda E, jc=jc, pair=pair: E.matmul(
                                psA[:, 1024 + pair * 256: 1024 + (pair + 1) * 256],
                                vtok[:, jc * 128:(jc + 1) * 128],
                                pT[:, jc * 256:(jc + 1) * 256],
                                start=(jc == 0), stop=False))
                        S.pe(lambda E, pair=pair: E.matmul(
                            psA[:, 1024 + pair * 256: 1024 + pair * 256 + 128],
                            vcmpt[0:32, :], pcmpT[0:32, 0:128], start=False, stop=False))
                        S.pe(lambda E, pair=pair: E.matmul(
                            psA[:, 1024 + pair * 256 + 128: 1024 + (pair + 1) * 256],
                            vcmpt[0:32, :], pcmpT[0:32, 128:256], start=False, stop=True))
                        t_pv_ = S.pe_mark()
                        eng = "vector" if kk < 4 else "scalar"
                        S.wt(eng, t_pv_)
                        tev = S.op(eng, lambda E, kk=kk, h=h, pair=pair: _tcopy(E, 
                            oT_all[:, h * 1024 + pair * 256: h * 1024 + (pair + 1) * 256],
                            psA[:, 1024 + pair * 256: 1024 + (pair + 1) * 256]))
                        t_oT_evac[kk - 1] = tev
                        t_oT_evac[kk] = tev
                        return t_pv_
                    return emit

                pending_tp = make_tp()
                if par == 1:
                    pending_pv = make_pv()
            trtoks = pending_tp()
            t_pv = pending_pv(trtoks)
            pending_tp = None
            pending_pv = None
            t_vtok_used = t_pv
            t_krot_used = t_pv
            t_qrot_used = t_pv

            if dbg and h == 0:
                S.wt("sync", t_oT_evac[NSLOT - 1])
                S.dma("sync", "dbg", lambda E: E.dma_start(
                    out=dbg_outs["dbg_oT"], in_=oT_all[:, 0:1024]))

        S.barrier()
        esA.close()

        esB1 = ExitStack()
        esB2 = ExitStack()
        sbB = lambda name, shape, dt: esB1.enter_context(nc.sbuf_tensor(name, shape, dt, side="right"))
        wo_sb = sbB("wo_sb", [128, NT * 2048], BF16)
        xo_sb = sbB("xo_sb", [128, 2 * 2048], F32)
        x1pre = sbB("x1pre", [128, 2048], F32)
        x1fl = sbB("x1fl", [128, 2048], F32)
        sqscr = sbB("sqscr", [128, 2048], BF16)
        x1bf = esB2.enter_context(nc.sbuf_tensor("x1bf", [128, NSLOT * 2048], BF16))
        x1lo = esB2.enter_context(nc.sbuf_tensor("x1lo", [128, NSLOT * 2048], BF16))

        for c in range(NT):
            S.dma("gpsimd", "gwo", lambda E, c=c: E.dma_start(
                out=wo_sb[:, c * 2048:(c + 1) * 2048], in_=wo_in[c * 128:(c + 1) * 128, :]))
        xo_tok = [None, None]
        xo_cons = [None, None]
        for kk in range(2):
            xo_tok[kk] = S.dma("sync", f"xo{kk}", lambda E, kk=kk: E.dma_start(
                out=xo_sb[:, kk * 2048:(kk + 1) * 2048], in_=x_own[kk * 128:(kk + 1) * 128, :]))

        S.wait("tensor", "gwo")
        t_op_evac = None
        t_x1_stores = []
        for kk in range(NSLOT):
            par = kk % 2
            S.wt("tensor", t_op_evac)
            for hh in range(NH):
                for n in range(4):
                    S.pe(lambda E, kk=kk, hh=hh, n=n: E.matmul(
                        psA[:, n * 512:(n + 1) * 512],
                        oT_all[:, hh * 1024 + kk * 128: hh * 1024 + (kk + 1) * 128],
                        wo_sb[:, hh * 2048 + n * 512: hh * 2048 + (n + 1) * 512],
                        start=(hh == 0), stop=(hh == NH - 1)))
            t_mm = S.pe_mark()
            S.wt("vector", t_mm)
            S.wait("vector", f"xo{par}", xo_tok[par][1])
            t_res = S.op("vector", lambda E, par=par: E.tensor_add(
                x1pre[:], psA[:, 0:2048], xo_sb[:, par * 2048:(par + 1) * 2048]))
            t_op_evac = t_res
            xo_cons[par] = t_res
            if kk + 2 < NSLOT:
                S.wt("sync", t_res)
                xo_tok[par] = S.dma("sync", f"xo{par}", lambda E, kk=kk: E.dma_start(
                    out=xo_sb[:, (kk % 2) * 2048:(kk % 2 + 1) * 2048],
                    in_=x_own[(kk + 2) * 128:(kk + 3) * 128, :]))
            S.wt("scalar", t_res)
            t_sq = S.op("scalar", lambda E: E.activation(
                sqscr[:], x1pre[:], AF.Square, accum_out=sm[:, 16:17]))
            t_mean = S.op("vector", lambda E: E.reduce_sum(sm[:, 17:18], x1pre[:], axis=AX.X))
            S.op("vector", lambda E: E.tensor_scalar_mul(sm[:, 18:19], sm[:, 17:18], 1.0 / D))
            S.wt("vector", t_sq)
            S.op("vector", lambda E: E.tensor_scalar_mul(sm[:, 19:20], sm[:, 16:17], 1.0 / D))
            S.op("vector", lambda E: E.tensor_mul(sm[:, 20:21], sm[:, 18:19], sm[:, 18:19]))
            S.op("vector", lambda E: E.tensor_sub(sm[:, 21:22], sm[:, 19:20], sm[:, 20:21]))
            t_var = S.op("vector", lambda E: E.tensor_scalar_add(sm[:, 22:23], sm[:, 21:22], EPS))
            S.wt("scalar", t_var)
            t_std = S.op("scalar", lambda E: E.activation(sm[:, 23:24], sm[:, 22:23], AF.Sqrt))
            S.wt("vector", t_std)
            t_rstd = S.op("vector", lambda E: E.reciprocal(sm[:, 24:25], sm[:, 23:24]))
            t_x1 = S.op("vector", lambda E: E.tensor_scalar(
                x1fl[:], x1pre[:], sm[:, 18:19], sm[:, 24:25], ALU.subtract, ALU.mult))
            t_hi = S.op("vector", lambda E, kk=kk: _tcopy(E, 
                x1bf[:, kk * 2048:(kk + 1) * 2048], x1fl[:]))
            t_lo = S.op("vector", lambda E, kk=kk: E.tensor_sub(
                x1lo[:, kk * 2048:(kk + 1) * 2048], x1fl[:], x1bf[:, kk * 2048:(kk + 1) * 2048]))
            S.wt("gpsimd", t_hi)
            t_x1_stores.append(S.dma("gpsimd", "x1s", lambda E, kk=kk: E.dma_start(
                out=x1_dram[kk * 128:(kk + 1) * 128, :], in_=x1bf[:, kk * 2048:(kk + 1) * 2048])))
            if dbg and kk == 0:
                S.wt("sync", t_hi)
                S.dma("sync", "dbg", lambda E: E.dma_start(
                    out=dbg_outs["dbg_x1"], in_=x1bf[:, 0:2048]))

        S.barrier()
        esB1.close()
        esO.close()

        esF = ExitStack()
        esF2 = ExitStack()
        sbF = lambda name, shape, dt: esF2.enter_context(nc.sbuf_tensor(name, shape, dt, side="right"))
        x1fm = esF.enter_context(nc.sbuf_tensor("x1fm", [128, NT * 1024], BF16))
        x1lofm = sbF("x1lofm", [128, NT * 1024], BF16)
        ghi = sbF("ghi", [128, NT * NE], BF16)
        glo = sbF("glo", [128, NT * NE], BF16)
        gfl = sbF("gfl", [128, NT * NE], F32)
        scores = sbF("scores", [128, NSLOT * NE], F32)
        m8g = sbF("m8g", [128, NSLOT * 8], F32)
        mig = sbF("mig", [128, NSLOT * 8], U32)
        sel2 = sbF("sel2", [128, NSLOT * NE], BF16)
        rank_all = sbF("rank_all", [128, NSLOT * NE], F32)

        for src_sb, dst_sb in ((x1bf, x1fm), (x1lo, x1lofm)):
            for kk in range(NSLOT):
                transpose_run(
                    NT,
                    lambda j, src_sb=src_sb, kk=kk: src_sb[:, kk * 2048 + j * 128: kk * 2048 + (j + 1) * 128],
                    lambda j0, cnt, dst_sb=dst_sb, kk=kk: dst_sb[:].rearrange(
                        "p (j n) -> p j n", n=1024)[:, j0:j0 + cnt, kk * 128:(kk + 1) * 128])
        t_fm_done2 = {"vector": trs["last"][0], "scalar": trs["last"][1]}

        for c in range(NT):
            S.dma("sync", "cst", lambda E, c=c: E.dma_start(
                out=gfl[:, c * NE:(c + 1) * NE], in_=gateT[c * 128:(c + 1) * 128, :]))
        S.wait("vector", "cst")
        t_ghi = S.op("vector", lambda E: _tcopy(E, ghi[:], gfl[:]))
        t_glo = S.op("vector", lambda E: E.tensor_sub(glo[:], gfl[:], ghi[:]))

        S.wt("tensor", t_fm_done2["vector"])
        S.wt("tensor", t_fm_done2["scalar"])
        S.wt("tensor", t_glo)
        t_sc_ev = None
        gate_steps = [(0, c) for c in range(NT)] + [(1, c) for c in range(NT)] + [(2, c) for c in range(NT)]
        for kk in range(NSLOT):
            S.wt("tensor", t_sc_ev)
            for idx, (p_, c) in enumerate(gate_steps):
                xa = x1fm if p_ in (0, 2) else x1lofm
                ga = ghi if p_ in (0, 1) else glo
                S.pe(lambda E, xa=xa, ga=ga, kk=kk, c=c, idx=idx: E.matmul(
                    psS[:, 0:NE],
                    xa[:, c * 1024 + kk * 128: c * 1024 + (kk + 1) * 128],
                    ga[:, c * NE:(c + 1) * NE],
                    start=(idx == 0), stop=(idx == len(gate_steps) - 1)))
            tpe = S.pe_mark()
            S.wt("scalar", tpe)
            t_sc_ev = S.op("scalar", lambda E, kk=kk: E.activation(
                scores[:, kk * NE:(kk + 1) * NE], psS[:, 0:NE], AF.Sigmoid))

        t_sel2_done = None
        S.wt("vector", t_sc_ev)
        for kk in range(NSLOT):
            S.op("vector", lambda E, kk=kk: E.max(
                m8g[:, kk * 8:(kk + 1) * 8], scores[:, kk * NE:(kk + 1) * NE]))
            S.op("vector", lambda E, kk=kk: E.max_index(
                mig[:, kk * 8:(kk + 1) * 8], m8g[:, kk * 8:(kk + 1) * 8],
                scores[:, kk * NE:(kk + 1) * NE]))
            t_sel2_done = S.op("vector", lambda E, kk=kk: E.tensor_single_scalar(
                sel2[:, kk * NE:(kk + 1) * NE], scores[:, kk * NE:(kk + 1) * NE],
                m8g[:, kk * 8 + 1:kk * 8 + 2], ALU.is_ge))
            S.op("vector", lambda E, kk=kk: _tcopy(E, 
                gA[:, kk * 8 + 0:kk * 8 + 1], mig[:, kk * 8 + 0:kk * 8 + 1]))
            S.op("vector", lambda E, kk=kk: _tcopy(E, 
                gA[:, kk * 8 + 1:kk * 8 + 2], mig[:, kk * 8 + 1:kk * 8 + 2]))
            S.op("vector", lambda E, kk=kk: E.tensor_add(
                sm[:, 30:31], m8g[:, kk * 8 + 0:kk * 8 + 1], m8g[:, kk * 8 + 1:kk * 8 + 2]))
            S.op("vector", lambda E: E.reciprocal(sm[:, 31:32], sm[:, 30:31]))
            S.op("vector", lambda E, kk=kk: E.tensor_mul(
                gA[:, kk * 8 + 4:kk * 8 + 5], m8g[:, kk * 8 + 0:kk * 8 + 1], sm[:, 31:32]))
            S.op("vector", lambda E, kk=kk: E.tensor_mul(
                gA[:, kk * 8 + 5:kk * 8 + 6], m8g[:, kk * 8 + 1:kk * 8 + 2], sm[:, 31:32]))
        if dbg:
            S.wt("sync", t_sel2_done)
            S.dma("sync", "dbg", lambda E: E.dma_start(out=dbg_outs["dbg_scores"], in_=scores[:]))

        S.wt("tensor", t_sel2_done)
        t_rank_ev = None
        for kk in range(NSLOT):
            S.wt("tensor", t_rank_ev)
            for j in range(kk + 1):
                S.pe(lambda E, kk=kk, j=j: E.matmul(
                    psS[:, 0:NE],
                    (triu if j == kk else onesb)[:],
                    sel2[:, j * NE:(j + 1) * NE],
                    start=(j == 0), stop=(j == kk)))
            tpe = S.pe_mark()
            S.wt("vector", tpe)
            t_rank_ev = S.op("vector", lambda E, kk=kk: _tcopy(E, 
                rank_all[:, kk * NE:(kk + 1) * NE], psS[:, 0:NE]))
        if dbg:
            S.wt("sync", t_rank_ev)
            S.dma("sync", "dbg", lambda E: E.dma_start(out=dbg_outs["dbg_rank"], in_=rank_all[:]))

        t_toki = S.op("vector", lambda E: _tcopy(E, toki[:], iotatok[:]))
        zbuf = sbF("zbuf", [128, 32], I32)
        t_z = S.op("vector", lambda E: E.memset(zbuf[:], 0))
        S.wt("sync", t_z)
        t_idz = S.dma("sync", "idz", lambda E: E.dma_start(
            out=idx_dram.rearrange("(p k) o -> p (k o)", p=128), in_=zbuf[:]))
        ohb = sbF("ohb", [128, NE], F32)
        t_scat_last = None
        for kk in range(NSLOT):
            for cpy in range(2):
                col = kk * 8 + cpy
                rcol = kk * 8 + 2 + cpy
                S.op("vector", lambda E, col=col: E.tensor_single_scalar(
                    ohb[:], iota16[:], gA[:, col:col + 1], ALU.is_equal))
                S.op("vector", lambda E, kk=kk, rcol=rcol: E.scalar_tensor_tensor(
                    ohb[:], rank_all[:, kk * NE:(kk + 1) * NE], 1.0, ohb[:],
                    ALU.mult, ALU.mult, accum_out=gA[:, rcol:rcol + 1]))
                S.op("vector", lambda E, rcol=rcol: E.tensor_scalar_min(
                    gA[:, rcol:rcol + 1], gA[:, rcol:rcol + 1], float(CAP - 1)))
                S.op("vector", lambda E, col=col, rcol=rcol: E.tensor_scalar(
                    sm[:, 33:34], gA[:, col:col + 1], float(CAP), 1.0, ALU.mult, ALU.mult))
                t_s = S.op("vector", lambda E, rcol=rcol: E.tensor_add(
                    sm[:, 34:35], sm[:, 33:34], gA[:, rcol:rcol + 1]))
                dst_i = s1i if cpy == 0 else s2i
                t_si = S.op("vector", lambda E, dst_i=dst_i, kk=kk: _tcopy(E, 
                    dst_i[:, kk:kk + 1], sm[:, 34:35]))
                S.op("vector", lambda E, dst_i=dst_i, kk=kk: E.tensor_single_scalar(
                    iwrk[:, 0:1], dst_i[:, kk:kk + 1], 127, ALU.bitwise_and))
                S.op("vector", lambda E: E.tensor_single_scalar(
                    iwrk[:, 1:2], iwrk[:, 0:1], 5, ALU.logical_shift_left))
                S.op("vector", lambda E, dst_i=dst_i, kk=kk: E.tensor_single_scalar(
                    iwrk[:, 2:3], dst_i[:, kk:kk + 1], 7, ALU.arith_shift_right))
                pcol = kk * 2 + cpy
                t_pos = S.op("vector", lambda E, pcol=pcol: E.tensor_tensor(
                    posb[:, pcol:pcol + 1], iwrk[:, 1:2], iwrk[:, 2:3], ALU.add))
                S.wt("gpsimd", t_pos)
                S.wt("gpsimd", t_toki)
                S.wait("gpsimd", "idz", t_idz[1])
                t_scat_last = S.dma("gpsimd", "isc", lambda E, kk=kk, pcol=pcol: E.indirect_dma_start(
                    idx_dram[:], IndirectOffsetOnAxis(posb[:, pcol:pcol + 1], 0),
                    toki[:, kk:kk + 1], None))
        S.wait("gpsimd", "isc")
        t_ilb = S.dma("gpsimd", "ilb", lambda E: E.dma_start(
            out=idxsb[:], in_=idx_dram.rearrange("(p k) o -> p (k o)", p=128)))
        if dbg:
            S.wait("sync", "ilb")
            S.dma("sync", "dbg", lambda E: E.dma_start(out=dbg_outs["dbg_idx"], in_=idxsb[:]))

        S.barrier()
        esF2.close()
        esS1 = ExitStack()
        rbuf = esF.enter_context(nc.sbuf_tensor("rbuf", [128, 8 * 2048], BF16))
        ring_cons = [None] * 8
        ring_n = [0]

        def ring_load(src_ap):
            g = ring_n[0]
            ring_n[0] += 1
            slot = g % 8
            if ring_cons[slot] is not None:
                S.wt("sync", ring_cons[slot])
            tok = S.dma("sync", f"ring{slot}", lambda E, src_ap=src_ap, slot=slot: E.dma_start(
                out=rbuf[:, slot * 2048:(slot + 1) * 2048], in_=src_ap))
            return slot, tok

        h_sh = esS1.enter_context(nc.sbuf_tensor("h_sh", [128, NSC * 1024], BF16, side="right"))
        sa_sh = esS1.enter_context(nc.sbuf_tensor("sa_sh", [128, 1024], BF16, side="right"))

        sh_chunks = []
        for j in range(NSC):
            sh_chunks.append(sw1c[j])
            sh_chunks.append(sw3c[j])
        sh_slots = [None] * len(sh_chunks)
        loaded_n = [0]

        def ensure_loaded(upto):
            while loaded_n[0] <= upto:
                g = loaded_n[0]
                sh_slots[g] = ring_load(sh_chunks[g])
                loaded_n[0] += 1

        ensure_loaded(5)
        S.wt("tensor", t_fm_done2["vector"])
        S.wt("tensor", t_fm_done2["scalar"])
        t_psS_u = None
        t_psA_u = None
        t_sa_u = None
        for j in range(NSC):
            ensure_loaded(min(2 * j + 7, len(sh_chunks) - 1))
            sl1, tk1 = sh_slots[2 * j]
            sl3, tk3 = sh_slots[2 * j + 1]
            S.wait("tensor", tk1[0], tk1[1])
            S.wt("tensor", t_psS_u)
            for c in range(NT):
                for i in range(2):
                    S.pe(lambda E, sl1=sl1, c=c, i=i: E.matmul(
                        psS[:, i * 512:(i + 1) * 512],
                        rbuf[:, sl1 * 2048 + c * 128: sl1 * 2048 + (c + 1) * 128],
                        x1fm[:, c * 1024 + i * 512: c * 1024 + (i + 1) * 512],
                        start=(c == 0), stop=(c == NT - 1)))
            t_a = S.pe_mark()
            ring_cons[sl1] = t_a
            S.wait("tensor", tk3[0], tk3[1])
            S.wt("tensor", t_psA_u)
            for c in range(NT):
                for i in range(2):
                    S.pe(lambda E, sl3=sl3, c=c, i=i: E.matmul(
                        psA[:, i * 512:(i + 1) * 512],
                        rbuf[:, sl3 * 2048 + c * 128: sl3 * 2048 + (c + 1) * 128],
                        x1fm[:, c * 1024 + i * 512: c * 1024 + (i + 1) * 512],
                        start=(c == 0), stop=(c == NT - 1)))
            t_b = S.pe_mark()
            ring_cons[sl3] = t_b
            S.wt("scalar", t_a)
            S.wt("scalar", t_sa_u)
            t_sa = S.op("scalar", lambda E: E.activation(sa_sh[:], psS[:, 0:1024], AF.Silu))
            t_psS_u = t_sa
            S.wt("vector", t_b)
            S.wt("vector", t_sa)
            t_h = S.op("vector", lambda E, j=j: E.tensor_mul(
                h_sh[:, j * 1024:(j + 1) * 1024], psA[:, 0:1024], sa_sh[:]))
            t_psA_u = t_h
            t_sa_u = t_h

        S.barrier()
        esF.close()
        esB2.close()

        esS2 = ExitStack()
        s2w = esS2.enter_context(nc.sbuf_tensor("s2w", [128, NSC * 2048], BF16))
        y0b = esS2.enter_context(nc.sbuf_tensor("y0b", [128, 2 * 2048], F32))
        for j in range(NSC):
            S.dma("sync", "s2w", lambda E, j=j: E.dma_start(
                out=s2w[:, j * 2048:(j + 1) * 2048], in_=sw2r[j]))
        S.wait("tensor", "s2w")
        t_y0_ev = [None, None]
        t_y0_st = [None, None]
        for tt in range(NSLOT):
            pb = tt % 2
            S.wt("tensor", t_psA_u)
            S.wt("tensor", t_y0_ev[pb])
            for j in range(NSC):
                for n in range(4):
                    S.pe(lambda E, j=j, tt=tt, n=n: E.matmul(
                        psA[:, n * 512:(n + 1) * 512],
                        h_sh[:, j * 1024 + tt * 128: j * 1024 + (tt + 1) * 128],
                        s2w[:, j * 2048 + n * 512: j * 2048 + (n + 1) * 512],
                        start=(j == 0), stop=(j == NSC - 1)))
            t_mm = S.pe_mark()
            t_psA_u = t_mm
            S.wt("vector", t_mm)
            if t_y0_st[pb] is not None:
                S.wait("vector", f"y0s{pb}", t_y0_st[pb][1])
            t_ev = S.op("vector", lambda E, pb=pb: _tcopy(E, 
                y0b[:, pb * 2048:(pb + 1) * 2048], psA[:, 0:2048]))
            t_y0_ev[pb] = t_ev
            t_psA_u = t_ev
            S.wt("gpsimd", t_ev)
            t_y0_st[pb] = S.dma("gpsimd", f"y0s{pb}", lambda E, tt=tt, pb=pb: E.dma_start(
                out=y0_dram[tt * 128:(tt + 1) * 128, :], in_=y0b[:, pb * 2048:(pb + 1) * 2048]))
        if dbg:
            S.wait("sync", "y0s0")
            S.wait("sync", "y0s1")
            S.dma("sync", "dbg", lambda E: E.dma_start(
                out=dbg_outs["dbg_y0"], in_=y0b[:, (NSLOT - 1) % 2 * 2048:((NSLOT - 1) % 2 + 1) * 2048]))

        S.barrier()
        esS1.close()
        esS2.close()

        esE = ExitStack()
        sbE = lambda name, shape, dt: esE.enter_context(nc.sbuf_tensor(name, shape, dt))
        rbuf2 = sbE("rbuf2", [128, 32 * 2048], BF16)
        xg = sbE("xg", [128, 2 * 2048], BF16)
        xgfm = sbE("xgfm", [128, NT * 256], BF16)
        sa_e = sbE("sa_e", [128, 2 * IFF], BF16)
        h_tok = sbE("h_tok", [128, 2 * IFF], BF16)
        h_ex = sbE("h_ex", [128, NIC * 256], BF16)
        ysb = sbE("ysb", [128, 2 * 2048], F32)

        ring2_cons = [None] * 32
        ring2_n = [0]

        def ring2_load(src_ap):
            g = ring2_n[0]
            ring2_n[0] += 1
            slot = g % 32
            w = src_ap.shape[-1]
            if ring2_cons[slot] is not None:
                S.wt("sync", ring2_cons[slot])
            tok = S.dma("sync", f"rg{slot}", lambda E, src_ap=src_ap, slot=slot, w=w: E.dma_start(
                out=rbuf2[:, slot * 2048:slot * 2048 + w], in_=src_ap))
            return slot, tok

        EXC = 2 * NT + NIC
        ex_chunks = []
        for e in range(NE):
            for c in range(NT):
                ex_chunks.append(w1c[e, c])
            for c in range(NT):
                ex_chunks.append(w3c[e, c])
            for j in range(NIC):
                ex_chunks.append(w2r[e, j])
        ex_slots = [None] * len(ex_chunks)
        ex_loaded = [0]

        def ex_ensure(upto):
            while ex_loaded[0] <= upto:
                g = ex_loaded[0]
                ex_slots[g] = ring2_load(ex_chunks[g])
                ex_loaded[0] += 1

        S.wait("gpsimd", "x1s")
        S.wait("gpsimd", "ilb", t_ilb[1])
        t_xg_cons = [None, None]
        t_xgfm_cons = None
        t_s1_free = None
        t_sa_u = None
        t_htok_u = None
        t_hex_w = None
        t_s2_banks = None
        t_ys_st = [None, None]
        t_ys_ev = [None, None]
        NCH = [512, 512, IFF - 1024]

        def s1_region(n):
            if n < 2:
                return psS[:, n * 512: n * 512 + NCH[n]]
            return psA[:, 0:NCH[2]]

        pending_s2 = []
        xg_pref = [None, None]

        for e in range(NE):
            ex_ensure(min(EXC * e + 15, len(ex_chunks) - 1))
            for te in range(2):
                kt = 2 * e + te
                if xg_pref[te] is not None:
                    gtok = xg_pref[te]
                    xg_pref[te] = None
                else:
                    if t_xg_cons[te] is not None:
                        S.wt("gpsimd", t_xg_cons[te][0])
                        S.wt("gpsimd", t_xg_cons[te][1])
                    gtok = S.dma("gpsimd", f"xg{te}", lambda E, kt=kt, te=te: E.indirect_dma_start(
                        xg[:, te * 2048:(te + 1) * 2048], None,
                        x1_dram[:], IndirectOffsetOnAxis(idxsb[:, kt:kt + 1], 0)))
                S.wait("tensor", f"xg{te}", gtok[1])
                if t_xgfm_cons is not None:
                    S.wt("tensor", t_xgfm_cons)
                transpose_run(
                    NT,
                    lambda j, te=te: xg[:, te * 2048 + j * 128: te * 2048 + (j + 1) * 128],
                    lambda j0, cnt, te=te: xgfm[:].rearrange(
                        "p (j n) -> p j n", n=256)[:, j0:j0 + cnt, te * 128:(te + 1) * 128])
                t_xg_cons[te] = tr_both()
            t_xg_tr = tr_both()

            def s1_pass(mat, t, first):
                S.wt("tensor", t_s1_free)
                if first:
                    S.wt("tensor", t_xg_tr[0])
                    S.wt("tensor", t_xg_tr[1])
                tkm = None
                for c in range(NT):
                    ex_ensure(EXC * e + mat * NT + c)
                    slw, tkw = ex_slots[EXC * e + mat * NT + c]
                    S.wait("tensor", tkw[0], tkw[1])
                    for n in range(3):
                        S.pe(lambda E, slw=slw, c=c, t=t, n=n: E.matmul(
                            s1_region(n),
                            xgfm[:, c * 256 + t * 128: c * 256 + (t + 1) * 128],
                            rbuf2[:, slw * 2048 + n * 512: slw * 2048 + n * 512 + NCH[n]],
                            start=(c == 0), stop=(c == NT - 1)))
                    tkm = S.pe_mark()
                    if mat == 1 and t == 1:
                        ring2_cons[slw] = tkm
                    if mat == 0 and t == 1:
                        ring2_cons[slw] = tkm
                return tkm

            def silu_evac(t):
                nonlocal_toks = []
                last = None
                for n in range(3):
                    S.wt("scalar", t_s1_pass_tok)
                    if t == 0 and t_sa_u is not None:
                        S.wt("scalar", t_sa_u)
                    last = S.op("scalar", lambda E, t=t, n=n: E.activation(
                        sa_e[:, t * IFF + 512 * n: t * IFF + 512 * n + NCH[n]],
                        s1_region(n), AF.Silu))
                return last

            def h_evac(t):
                last = None
                for n in range(3):
                    S.wt("vector", t_s1_pass_tok)
                    S.wt("vector", t_sa_tok[t])
                    if t_htok_u is not None:
                        S.wt("vector", t_htok_u[0])
                        S.wt("vector", t_htok_u[1])
                    last = S.op("vector", lambda E, t=t, n=n: E.tensor_mul(
                        h_tok[:, t * IFF + 512 * n: t * IFF + 512 * n + NCH[n]],
                        s1_region(n),
                        sa_e[:, t * IFF + 512 * n: t * IFF + 512 * n + NCH[n]]))
                return last

            t_sa_tok = [None, None]
            t_s1_pass_tok = s1_pass(0, 0, True)
            if e + 1 < NE:
                for te in range(2):
                    kt = 2 * (e + 1) + te
                    if t_xg_cons[te] is not None:
                        S.wt("gpsimd", t_xg_cons[te][0])
                        S.wt("gpsimd", t_xg_cons[te][1])
                    xg_pref[te] = S.dma("gpsimd", f"xg{te}", lambda E, kt=kt, te=te: E.indirect_dma_start(
                        xg[:, te * 2048:(te + 1) * 2048], None,
                        x1_dram[:], IndirectOffsetOnAxis(idxsb[:, kt:kt + 1], 0)))
            if len(pending_s2) > 0:
                pending_s2[0]()
            t_sa_tok[0] = silu_evac(0)
            t_s1_free = t_sa_tok[0]
            if len(pending_s2) > 1:
                pending_s2[1]()
            t_s1_pass_tok = s1_pass(0, 1, False)
            if len(pending_s2) > 2:
                pending_s2[2]()
            t_sa_tok[1] = silu_evac(1)
            t_s1_free = t_sa_tok[1]
            if len(pending_s2) > 3:
                pending_s2[3]()
            t_s1_pass_tok = s1_pass(1, 0, False)
            t_h0 = h_evac(0)
            t_s1_free = t_h0
            t_s1_pass_tok = s1_pass(1, 1, False)
            t_xgfm_cons = t_s1_pass_tok
            t_h1 = h_evac(1)
            t_s1_free = t_h1

            S.wt("tensor", t_h1)
            for t in range(2):
                transpose_run(
                    NIC,
                    lambda j, t=t: h_tok[:, t * IFF + j * 128: t * IFF + (j + 1) * 128],
                    lambda j0, cnt, t=t: h_ex[:].rearrange(
                        "p (j n) -> p j n", n=256)[:, j0:j0 + cnt, t * 128:(t + 1) * 128])
            t_hex_w = tr_both()
            t_htok_u = t_hex_w

            def make_piece(e, tt, dh, hexw):
                def emit():
                    nonlocal t_s2_banks
                    S.wt("tensor", t_s2_banks)
                    S.wt("tensor", hexw[0])
                    S.wt("tensor", hexw[1])
                    tkm = None
                    for j in range(NIC):
                        ex_ensure(EXC * e + 2 * NT + j)
                        sl2, tk2 = ex_slots[EXC * e + 2 * NT + j]
                        S.wait("tensor", tk2[0], tk2[1])
                        for n in range(2):
                            S.pe(lambda E, sl2=sl2, j=j, tt=tt, dh=dh, n=n: E.matmul(
                                psA[:, 1024 + n * 512: 1024 + (n + 1) * 512],
                                h_ex[:, j * 256 + tt * 128: j * 256 + (tt + 1) * 128],
                                rbuf2[:, sl2 * 2048 + dh * 1024 + n * 512: sl2 * 2048 + dh * 1024 + (n + 1) * 512],
                                start=(j == 0), stop=(j == NIC - 1)))
                    tkm = S.pe_mark()
                    if tt == 1 and dh == 1:
                        for j in range(NIC):
                            ring2_cons[ex_slots[EXC * e + 2 * NT + j][0]] = tkm
                    pb = tt
                    S.wt("vector", tkm)
                    if dh == 0 and t_ys_st[pb] is not None:
                        S.wait("vector", f"yss{pb}", t_ys_st[pb][1])
                    t_ev = S.op("vector", lambda E, pb=pb, dh=dh: _tcopy(E, 
                        ysb[:, pb * 2048 + dh * 1024: pb * 2048 + (dh + 1) * 1024],
                        psA[:, 1024:2048]))
                    t_s2_banks = t_ev
                    if dh == 1:
                        S.wt("gpsimd", t_ev)
                        t_ys_st[pb] = S.dma("gpsimd", f"yss{pb}", lambda E, e=e, tt=tt, pb=pb: E.dma_start(
                            out=ys_dram[e * CAP + tt * 128: e * CAP + (tt + 1) * 128, :],
                            in_=ysb[:, pb * 2048:(pb + 1) * 2048]))
                    return tkm
                return emit
            pending_s2 = [make_piece(e, 0, 0, t_hex_w), make_piece(e, 0, 1, t_hex_w),
                          make_piece(e, 1, 0, t_hex_w), make_piece(e, 1, 1, t_hex_w)]
            ex_ensure(min(EXC * (e + 1) + 15, len(ex_chunks) - 1))

        for p in pending_s2:
            p()

        if dbg:
            S.wait("sync", "yss0")
            S.wait("sync", "yss1")
            S.dma("sync", "dbg", lambda E: E.dma_start(
                out=dbg_outs["dbg_ysort"], in_=ysb[:, 0:2048]))

        S.barrier()
        esE.close()

        esC = ExitStack()
        sbC = lambda name, shape, dt: esC.enter_context(nc.sbuf_tensor(name, shape, dt))
        y0l = sbC("y0l", [128, 2 * 2048], F32)
        g1b = sbC("g1b", [128, 2 * 2048], F32)
        g2b = sbC("g2b", [128, 2 * 2048], F32)
        x1rl = sbC("x1rl", [128, 2 * 2048], BF16)
        ycmb = sbC("ycmb", [128, 2048], F32)
        sqscr2 = sbC("sqscr2", [128, 2048], BF16)
        outsb = sbC("outsb", [128, 2 * 2048], F32)

        S.wait("gpsimd", "yss0")
        S.wait("gpsimd", "yss1")
        S.wait("sync", "y0s0")
        S.wait("sync", "y0s1")
        t_y0l = [None, None]
        t_g1 = [None, None]
        t_g2 = [None, None]
        t_x1l = [None, None]
        t_cons = [None, None]
        t_out_ev = [None, None]

        def phc_load(kk):
            pb = kk % 2
            if t_cons[pb] is not None:
                S.wt("sync", t_cons[pb])
                S.wt("gpsimd", t_cons[pb])
            t_y0l[pb] = S.dma("sync", f"y0l{pb}", lambda E, kk=kk, pb=pb: E.dma_start(
                out=y0l[:, pb * 2048:(pb + 1) * 2048], in_=y0_dram[kk * 128:(kk + 1) * 128, :]))
            t_x1l[pb] = S.dma("sync", f"x1l{pb}", lambda E, kk=kk, pb=pb: E.dma_start(
                out=x1rl[:, pb * 2048:(pb + 1) * 2048], in_=x1_dram[kk * 128:(kk + 1) * 128, :]))
            t_g1[pb] = S.dma("gpsimd", f"gg1{pb}", lambda E, kk=kk, pb=pb: E.indirect_dma_start(
                g1b[:, pb * 2048:(pb + 1) * 2048], None,
                ys_dram[:], IndirectOffsetOnAxis(s1i[:, kk:kk + 1], 0)))
            t_g2[pb] = S.dma("gpsimd", f"gg2{pb}", lambda E, kk=kk, pb=pb: E.indirect_dma_start(
                g2b[:, pb * 2048:(pb + 1) * 2048], None,
                ys_dram[:], IndirectOffsetOnAxis(s2i[:, kk:kk + 1], 0)))

        phc_load(0)
        phc_load(1)
        for kk in range(NSLOT):
            pb = kk % 2
            for tname, tl in (("y0l", t_y0l), ("x1l", t_x1l)):
                S.wait("vector", f"{tname}{pb}", tl[pb][1])
            S.wait("vector", f"gg1{pb}", t_g1[pb][1])
            S.wait("vector", f"gg2{pb}", t_g2[pb][1])
            t1 = S.op("vector", lambda E, kk=kk, pb=pb: E.scalar_tensor_tensor(
                ycmb[:], g1b[:, pb * 2048:(pb + 1) * 2048], gA[:, kk * 8 + 4:kk * 8 + 5],
                y0l[:, pb * 2048:(pb + 1) * 2048], ALU.mult, ALU.add))
            t2 = S.op("vector", lambda E, kk=kk, pb=pb: E.scalar_tensor_tensor(
                ycmb[:], g2b[:, pb * 2048:(pb + 1) * 2048], gA[:, kk * 8 + 5:kk * 8 + 6],
                ycmb[:], ALU.mult, ALU.add))
            t3 = S.op("vector", lambda E, pb=pb: E.tensor_add(
                ycmb[:], ycmb[:], x1rl[:, pb * 2048:(pb + 1) * 2048]))
            t_cons[pb] = t3
            S.wt("scalar", t3)
            t_sq = S.op("scalar", lambda E: E.activation(
                sqscr2[:], ycmb[:], AF.Square, accum_out=sm[:, 40:41]))
            S.op("vector", lambda E: E.reduce_sum(sm[:, 41:42], ycmb[:], axis=AX.X))
            S.op("vector", lambda E: E.tensor_scalar_mul(sm[:, 42:43], sm[:, 41:42], 1.0 / D))
            S.wt("vector", t_sq)
            S.op("vector", lambda E: E.tensor_scalar_mul(sm[:, 43:44], sm[:, 40:41], 1.0 / D))
            S.op("vector", lambda E: E.tensor_mul(sm[:, 44:45], sm[:, 42:43], sm[:, 42:43]))
            S.op("vector", lambda E: E.tensor_sub(sm[:, 45:46], sm[:, 43:44], sm[:, 44:45]))
            t_var = S.op("vector", lambda E: E.tensor_scalar_add(sm[:, 46:47], sm[:, 45:46], EPS))
            S.wt("scalar", t_var)
            t_std = S.op("scalar", lambda E: E.activation(sm[:, 47:48], sm[:, 46:47], AF.Sqrt))
            S.wt("vector", t_std)
            S.op("vector", lambda E: E.reciprocal(sm[:, 48:49], sm[:, 47:48]))
            if t_out_ev[pb] is not None:
                S.wait("vector", f"sto{pb}", t_out_ev[pb][1])
            t_o = S.op("vector", lambda E, pb=pb: E.tensor_scalar(
                outsb[:, pb * 2048:(pb + 1) * 2048], ycmb[:], sm[:, 42:43], sm[:, 48:49],
                ALU.subtract, ALU.mult))
            S.wt("scalar", t_o)
            t_out_ev[pb] = S.dma("scalar", f"sto{pb}", lambda E, kk=kk, pb=pb: E.dma_start(
                out=out_d[kk * 128:(kk + 1) * 128, :], in_=outsb[:, pb * 2048:(pb + 1) * 2048]))
            if kk + 2 < NSLOT:
                phc_load(kk + 2)

        if dbg:
            S.wt("sync", t_sel2_done)
            s12f = sbC("s12f", [128, 2 * NSLOT], F32)
            S.op("vector", lambda E: _tcopy(E, s12f[:, 0:NSLOT], s1i[:]))
            tlast = S.op("vector", lambda E: _tcopy(E, s12f[:, NSLOT:2 * NSLOT], s2i[:]))
            S.wt("sync", tlast)
            S.dma("sync", "dbg", lambda E: E.dma_start(out=dbg_outs["dbg_s12"], in_=s12f[:]))

        S.wait("sync", "sto0")
        S.wait("sync", "sto1")
        esC.close()

        with nc.Block() as block:
            S.build(block)

    return nc



def _host_prep(inputs):
    x = np.asarray(inputs["L0_attn_x"], dtype=np.float32)
    wq = np.asarray(inputs["L0_attn_Wq"], dtype=np.float32)
    wk = np.asarray(inputs["L0_attn_Wk"], dtype=np.float32)
    wv = np.asarray(inputs["L0_attn_Wv"], dtype=np.float32)
    wg = np.asarray(inputs["L0_attn_Wg"], dtype=np.float32)
    wo = np.asarray(inputs["L0_attn_Wo"], dtype=np.float32)
    gate_w = np.asarray(inputs["L0_ffn_gate_w"], dtype=np.float32)
    w1 = np.asarray(inputs["L0_ffn_w1"])
    w2 = np.asarray(inputs["L0_ffn_w2"])
    w3 = np.asarray(inputs["L0_ffn_w3"])
    sw1 = np.asarray(inputs["L0_ffn_sw1"])
    sw2 = np.asarray(inputs["L0_ffn_sw2"])
    sw3 = np.asarray(inputs["L0_ffn_sw3"])

    def headchunk(w):
        return np.ascontiguousarray(w.reshape(NT, 128, NH, HD).transpose(2, 1, 0, 3))

    wqh, wkh, wvh = headchunk(wq), headchunk(wk), headchunk(wv)

    def ichunk(wT, n):
        out = np.empty((n, 128, D), dtype=wT.dtype)
        for j in range(n):
            out[j] = wT[:, j * 128:(j + 1) * 128].reshape(NT, 128, 128).transpose(1, 0, 2).reshape(128, D)
        return out

    sw1c = ichunk(np.ascontiguousarray(sw1.T), NSC)
    sw3c = ichunk(np.ascontiguousarray(sw3.T), NSC)
    sw2r = np.ascontiguousarray(sw2.T.reshape(NSC, 128, D))
    w1c = np.ascontiguousarray(w1.transpose(0, 2, 1).reshape(NE, NT, 128, IFF))
    w3c = np.ascontiguousarray(w3.transpose(0, 2, 1).reshape(NE, NT, 128, IFF))
    w2r = np.stack([np.ascontiguousarray(w2[e].T.reshape(NIC, 128, D)) for e in range(NE)])
    gateT = np.ascontiguousarray(gate_w.T)

    ident = np.eye(128, dtype=BF)
    half = HD // 2
    inv = 1.0 / (10000.0 ** (np.arange(half, dtype=np.float64) / half))
    pos = np.arange(2048, dtype=np.float64)
    th = pos[:, None] * inv[None, :]
    cosv = np.cos(th).astype(np.float32)
    sinv = np.sin(th).astype(np.float32)
    cc = np.concatenate([cosv.T, cosv.T], axis=0).astype(BF)
    ssf = np.concatenate([-sinv.T, sinv.T], axis=0).astype(BF)
    pswap = np.zeros((128, 128), dtype=BF)
    pswap[(np.arange(128) + 64) % 128, np.arange(128)] = 1.0
    ones = np.ones((128, 128), dtype=BF)
    triu = np.triu(np.ones((128, 128), dtype=np.float32), 1).astype(BF)
    iota16 = np.broadcast_to(np.arange(NE, dtype=np.float32), (128, NE)).copy()
    iotatok = (np.arange(NSLOT)[None, :] * 128 + np.arange(128)[:, None]).astype(np.float32)

    consts = dict(c_ident=ident, c_cc=cc, c_ss=ssf, c_pswap=pswap, c_ones=ones,
                  c_triu=triu, c_iota16=iota16, c_iotatok=iotatok)
    shared = dict(wqh=wqh, wkh=wkh, wvh=wvh, wo=wo, wg=wg, gateT=gateT,
                  sw1c=sw1c, sw3c=sw3c, sw2r=sw2r, w1c=w1c, w3c=w3c, w2r=w2r, **consts)

    pvec = np.arange(128)
    in_maps = []
    for core in range(8):
        b, par = core // 2, core % 2
        own = np.concatenate([np.arange(128 * (2 * kk + par), 128 * (2 * kk + par) + 128)
                              for kk in range(NSLOT)])
        xb = x[b]
        m = dict(shared)
        m["xT"] = np.ascontiguousarray(xb.T)
        m["xT_own"] = np.ascontiguousarray(xb.T[:, own])
        m["x_own"] = np.ascontiguousarray(xb[own])
        m["c_ccq"] = np.ascontiguousarray(cc[:, own])
        m["c_ssq"] = np.ascontiguousarray(ssf[:, own])
        causal = np.zeros((128, 256), dtype=np.float32)
        for p in range(128):
            lim = p if par == 0 else 128 + p
            causal[p, :lim + 1] = 1.0
        m["c_causal"] = causal.astype(BF)
        bandm = np.zeros((2, 128, 320), dtype=np.float32)
        for v in range(2):
            for p in range(128):
                for bj in range(320):
                    if v == 0:
                        L = 256
                        j = bj
                    else:
                        L = 512
                        j = L - 320 + bj
                    i = (L - 256 + p) if par == 0 else (L - 128 + p)
                    if 0 <= j < L and 0 <= i - j <= 64:
                        bandm[v, p, bj] = 1.0
        m["c_band"] = bandm.astype(BF)
        cmpm = np.zeros((NSLOT, 128, 32), dtype=np.float32)
        curadd = np.zeros((NSLOT, 128, 32), dtype=np.float32)
        for kk in range(NSLOT):
            t = 2 * kk + par
            i = 128 * t + pvec
            for mm_ in range(32):
                vis = (i >= 64 * (mm_ + 1) - 1)
                cmpm[kk, vis, mm_] = 1.0
            cur = i // 64
            curadd[kk, pvec, cur] = 1e30
        m["c_cmpm"] = cmpm
        m["c_curadd"] = curadd
        in_maps.append(m)
    return in_maps


def kernel(**inputs):
    in_maps = _host_prep(inputs)
    nc = build_program()
    res = run_bass_kernel_spmd(nc, in_maps, core_ids=list(range(8)))
    out = np.empty((4, 2048, D), dtype=np.float32)
    for core in range(8):
        b, par = core // 2, core % 2
        o = res.results[core]["out"]
        for kk in range(NSLOT):
            t = 2 * kk + par
            out[b, 128 * t:128 * (t + 1)] = o[128 * kk:128 * (kk + 1)]
    return out
